# version 03
# baseline (speedup 1.0000x reference)
import contextlib
import numpy as np
import concourse.bass as bass
import concourse.mybir as mybir
from concourse.bass_utils import run_bass_kernel_spmd

F32 = mybir.dt.float32
BF16 = mybir.dt.bfloat16
I32 = mybir.dt.int32
AF = mybir.ActivationFunctionType
ALU = mybir.AluOpType
AX = mybir.AxisListType

NCORE = 8
B, S, D = 16, 2048, 1024
BPC = B // NCORE
TOK = BPC * S
SS = 512
NSS = TOK // SS
NST = SS // 128
SS_PER_SEQ = S // SS


ENG_SEM = {"sync": "dl0", "gpsimd": "g", "tensor": "t", "vector": "v", "scalar": "s"}


class Prog:

    def __init__(self, nc):
        self.nc = nc
        self.ops = {e: [] for e in ("sync", "gpsimd", "tensor", "vector", "scalar")}
        self.cnt = {"dw": 0, "dl0": 0, "dl1": 0, "ds0": 0, "ds1": 0,
                    "dx0": 0, "dx1": 0, "dc0": 0, "dc1": 0, "dbg": 0, "t": 0, "v": 0, "s": 0, "g": 0}
        self.sems = {}
        self.last = {}
        self.prev = {}

    def do(self, eng, build, after=(), hard=(), sig=None, amt=1, chain=True):
        if sig is None:
            sig = ENG_SEM[eng]
        hard = list(hard)
        if chain and eng in ("vector", "scalar", "gpsimd"):
            prev = self.prev.get(eng)
            if prev is not None:
                hard.append(prev)
        self.cnt[sig] += amt
        val = self.cnt[sig]
        if amt == 0:
            val = None
        own = ENG_SEM[eng] if eng not in ("sync", "gpsimd") else "-"
        seen = {}
        for t in list(after):
            if t is None or t[0] == own:
                continue
            if seen.get(t[0], -1) < t[1]:
                seen[t[0]] = t[1]
        for t in list(hard):
            if t is None:
                continue
            if seen.get(t[0], -1) < t[1]:
                seen[t[0]] = t[1]
        waits = sorted(seen.items())

        def run(E, waits=waits, build=build, sig=sig, amt=amt):
            for sname, v in waits:
                E.wait_ge(self.sems[sname], v)
            ins = build(E)
            if amt and ins is not None:
                ins.then_inc(self.sems[sig], amt)
            elif amt:
                E.sem_inc(self.sems[sig], amt)

        self.ops[eng].append(run)
        tok = (sig, val) if val is not None else None
        if eng in ("vector", "scalar", "gpsimd") and val is not None and sig in ("v", "s", "g"):
            self.prev[eng] = tok
        return tok

    def dma(self, eng, out, in_, after=(), stream=0, sig=None, **kw):
        if sig is None:
            sig = ("dl%d" if eng == "sync" else "ds%d") % stream
        return self.do(eng, lambda E: E.dma_start(out=out, in_=in_, **kw),
                       after=after, sig=sig, amt=16)

    def wait_all(self, eng, toks):
        def run(E, toks=[t for t in toks if t is not None]):
            seen = {}
            for sname, v in toks:
                if seen.get(sname, -1) < v:
                    seen[sname] = v
            for sname, v in sorted(seen.items()):
                E.wait_ge(self.sems[sname], v)
        self.ops[eng].append(run)

    def barrier(self):
        snap = [(k, v) for k, v in self.cnt.items() if v > 0]
        for eng in self.ops:
            own = ENG_SEM[eng]
            self.wait_all(eng, [(k, v) for k, v in snap if k != own])

    def build(self):
        nc = self.nc
        with contextlib.ExitStack() as st:
            for name in self.cnt:
                self.sems[name] = st.enter_context(nc.semaphore(name))
            with nc.Block() as block:
                for eng in self.ops:
                    def make(eng):
                        def body(E):
                            for run in self.ops[eng]:
                                run(E)
                        return body
                    getattr(block, eng)(make(eng))


def sb(stk, nc, name, shape, dt):
    return stk.enter_context(nc.sbuf_tensor(name, shape, dt))


def ps(stk, nc, name, shape, dt):
    return stk.enter_context(nc.psum_tensor(name, shape, dt))


class Common:

    def __init__(self, p, stk):
        nc = p.nc
        self.ident = sb(stk, nc, "ident", [128, 128], BF16)
        self.onesrow_f = sb(stk, nc, "onesrow_f", [1, 512], F32)
        self.onesrow_b = sb(stk, nc, "onesrow_b", [1, 128], BF16)
        self.onescol_f = sb(stk, nc, "onescol_f", [128, 1], F32)
        self.rowscratch = sb(stk, nc, "rowscratch", [1, 16], F32)
        self.i32scr = sb(stk, nc, "i32scr", [128, 1024], I32)
        self.rowacc = sb(stk, nc, "rowacc", [128, 2], F32)
        self.rndA = sb(stk, nc, "rndA", [128, 1024], F32)
        self.rndB = sb(stk, nc, "rndB", [128, 1024], F32)
        t0 = p.do("vector", lambda E: E.memset(self.ident[:], 1.0))
        p.do("vector", lambda E: E.memset(self.onescol_f[:], 1.0))
        p.do("vector", lambda E: E.memset(self.onesrow_b[:], 1.0))
        self.tok_ones = p.do("vector", lambda E: E.memset(self.onesrow_f[:], 1.0))
        self.tok_ident = p.do(
            "gpsimd",
            lambda E: E.affine_select(self.ident[:], self.ident[:], [[-1, 128]],
                                      ALU.is_equal, 0.0, base=0, channel_multiplier=1),
            after=[t0])


def ln_stats(p, x_ap, sq_scr, stat, c0, eps, tok_in=(), n=1024.0):
    su, sq, rs, nb = (stat[:, c0 + i:c0 + i + 1] for i in range(4))
    p.do("vector", lambda E: E.reduce_sum(su, x_ap, axis=AX.X), after=tok_in)
    p.do("vector", lambda E: E.scalar_tensor_tensor(
        sq_scr, x_ap, 1.0, x_ap, ALU.mult, ALU.mult, accum_out=sq), after=tok_in)
    p.do("vector", lambda E: E.tensor_scalar_mul(su, su, 1.0 / n))
    p.do("vector", lambda E: E.tensor_scalar_mul(sq, sq, 1.0 / n))
    p.do("vector", lambda E: E.scalar_tensor_tensor(rs, su, 1.0, su, ALU.mult, ALU.mult))
    p.do("vector", lambda E: E.tensor_sub(sq, sq, rs))
    t7 = p.do("vector", lambda E: E.tensor_scalar_add(sq, sq, eps))
    t8 = p.do("scalar", lambda E: E.activation(rs, sq, AF.Sqrt), after=[t7])
    p.do("vector", lambda E: E.reciprocal(rs, rs), after=[t8])
    t10 = p.do("vector", lambda E: E.scalar_tensor_tensor(nb, su, -1.0, rs, ALU.mult, ALU.mult))
    return rs, nb, t10


def rms_stats(p, x_ap, sq_scr, stat, c0, eps, tok_in=(), n=1024.0):
    sq, rs = stat[:, c0:c0 + 1], stat[:, c0 + 1:c0 + 2]
    p.do("vector", lambda E: E.scalar_tensor_tensor(
        sq_scr, x_ap, 1.0, x_ap, ALU.mult, ALU.mult, accum_out=sq), after=tok_in)
    t2 = p.do("vector", lambda E: E.tensor_scalar(sq, sq, 1.0 / n, eps, ALU.mult, ALU.add))
    t3 = p.do("scalar", lambda E: E.activation(rs, sq, AF.Sqrt), after=[t2])
    t4 = p.do("vector", lambda E: E.reciprocal(rs, rs), after=[t3])
    return rs, t4



def build_l0_attn(p, stk, com, dram, x_in, x_out, tok_x_in=None, dbg=None):
    nc = p.nc
    L = p.last

    wstage = sb(stk, nc, "a_wstage", [128, 2, 1024], F32)
    winT = sb(stk, nc, "a_winT", [128, 8, 256], BF16)
    woutT = sb(stk, nc, "a_woutT", [128, 2, 1024], BF16)
    wskipT = sb(stk, nc, "a_wskipT", [128, 8, 1024], BF16)
    normw = sb(stk, nc, "a_normw", [128, 8], F32)
    b_in_c = sb(stk, nc, "a_binc", [128, 2], F32)
    brow = sb(stk, nc, "a_brow", [1, 2048], F32)
    bias_rep = sb(stk, nc, "a_biasrep", [128, 1024], F32)
    dec_c = sb(stk, nc, "a_decc", [128, 2], F32)
    dec_s = sb(stk, nc, "a_decs", [128, 2], F32)
    decseg = [sb(stk, nc, "a_decseg%d" % i, [128, 256], F32) for i in range(2)]

    psU = [ps(stk, nc, "a_psU%d" % i, [128, 512], F32) for i in range(2)]
    psT = [ps(stk, nc, "a_psT%d" % i, [128, 128], BF16) for i in range(2)]
    psY = [ps(stk, nc, "a_psY%d" % i, [128, 512], F32) for i in range(4)]

    t_nw = p.dma("sync", normw[:], dram["norm_w"][:], sig="dw")
    t_bi = p.dma("sync", b_in_c[:], dram["b_in"][:], sig="dw")
    t_bo = p.dma("sync", brow[0:1, 0:1024], dram["b_out"][:], sig="dw")
    t_bs = p.dma("sync", brow[0:1, 1024:2048], dram["b_skip"][:], sig="dw")
    t_ld = p.dma("sync", dec_c[:], dram["log_decay"][:], sig="dw")
    dwtot = ("dw", p.cnt["dw"])
    t_nw = t_bi = t_bo = t_bs = t_ld = dwtot

    win_r = dram["w_in_t"].rearrange("(c p) n -> c p n", p=128)
    wsk_r = dram["w_skip_t"].rearrange("(c p) e -> c p e", p=128)
    wout_r = dram["w_out_t"].rearrange("(c p) e -> c p e", p=128)
    tk_win, tk_wsk, tk_wout = [], [], []
    for c in range(8):
        sl = c % 2
        t_w = p.dma("sync", wstage[:, sl, 0:256], win_r[c], stream=sl,
                    after=[tk_win[-2] if c >= 2 else None])
        tk_win.append(p.do("vector", lambda E, c=c, sl=sl: E.tensor_scalar_mul(
            winT[:, c, :], wstage[:, sl, 0:256], normw[:, c:c + 1]), after=[dwtot, t_w]))
    for c in range(8):
        sl = c % 2
        t_w = p.dma("sync", wstage[:, sl, :], wsk_r[c], stream=sl,
                    after=[tk_wsk[-2] if c >= 2 else tk_win[6 + c]])
        tk_wsk.append(p.do("vector", lambda E, c=c, sl=sl: E.tensor_scalar_mul(
            wskipT[:, c, :], wstage[:, sl, :], normw[:, c:c + 1]), after=[t_w], hard=[dwtot]))
    for c in range(2):
        sl = c % 2
        t_w = p.dma("sync", wstage[:, sl, :], wout_r[c], stream=sl,
                    after=[tk_wsk[6 + c]])
        tk_wout.append(p.do("vector", lambda E, c=c, sl=sl: E.tensor_scalar_add(
            woutT[:, c, :], wstage[:, sl, :], 0.0), after=[t_w]))

    t_ba = p.do("vector", lambda E: E.tensor_add(
        brow[0:1, 0:1024], brow[0:1, 0:1024], brow[0:1, 1024:2048]), after=[dwtot])
    tb0 = p.do("tensor", lambda E: E.matmul(
        psY[0][:], com.onesrow_f[0:1, 0:128], brow[0:1, 0:512], start=True, stop=True),
        after=[t_ba, com.tok_ones])
    tb1 = p.do("tensor", lambda E: E.matmul(
        psY[1][:], com.onesrow_f[0:1, 0:128], brow[0:1, 512:1024], start=True, stop=True),
        after=[t_ba, com.tok_ones])
    tb_e0 = p.do("vector", lambda E: E.tensor_copy(bias_rep[:, 0:512], psY[0][:]), after=[tb0])
    tb_e1 = p.do("vector", lambda E: E.tensor_copy(bias_rep[:, 512:1024], psY[1][:]), after=[tb1])
    L["a_psY0"] = tb_e0
    L["a_psY1"] = tb_e1
    t_sg = p.do("scalar", lambda E: E.activation(dec_s[:], dec_c[:], AF.Sigmoid), after=[dwtot])
    t_ds = []
    for i in range(2):
        t_ms = p.do("vector", lambda E, i=i: E.memset(decseg[i][:], 1.0))
        t_ds.append(p.do("vector", lambda E, i=i: E.tensor_scalar_mul(
            decseg[i][:], decseg[i][:], dec_s[:, i:i + 1]), after=[t_sg], hard=[t_sg, t_ms]))


    x_sb = [sb(stk, nc, "a_x%d" % i, [128, NST, 1024], F32) for i in range(2)]
    tok_sb = sb(stk, nc, "a_tok", [128, NST, 1024], F32)
    xr_bf = sb(stk, nc, "a_xr", [128, NST, 1024], BF16)
    xrT = [sb(stk, nc, "a_xrT%d" % i, [128, 8, 512], BF16) for i in range(2)]
    useg = [[sb(stk, nc, "a_useg%d_%d" % (i, j), [128, 256], F32) for j in range(4)]
            for i in range(2)]
    stseg = [[sb(stk, nc, "a_stseg%d_%d" % (i, j), [128, 256], F32) for j in range(4)]
             for i in range(2)]
    stbf = [[sb(stk, nc, "a_stbf%d_%d" % (i, j), [128, 256], BF16) for j in range(4)]
            for i in range(2)]
    out_sb = [sb(stk, nc, "a_out%d" % i, [128, NST, 1024], F32) for i in range(2)]
    sq_scr = sb(stk, nc, "a_sq", [128, 1024], F32)
    stat = [sb(stk, nc, "a_stat%d" % i, [128, 48], F32) for i in range(2)]
    carry = [sb(stk, nc, "a_carry%d" % i, [128, 1], F32) for i in range(2)]

    x_t = x_in.rearrange("(k s p) d -> k p s d", p=128, s=NST)
    xo_t = x_out.rearrange("(k s p) d -> k p s d", p=128, s=NST)

    tok_store = []
    t_load = [None] * NSS
    t_load[0] = p.dma("sync", x_sb[0][:], x_t[0], stream=0,
                      after=[tok_x_in, tk_wout[0]])

    def emit_ss(k):
        bi = k % 2
        xsb, xrt, osb, sta = x_sb[bi], xrT[bi], out_sb[bi], stat[bi]
        use, sts, stb = useg[bi], stseg[bi], stbf[bi]
        if k + 1 < NSS:
            gate = L.get("a_x%d" % ((k + 1) % 2)) if k >= 1 else tk_wout[1]
            t_load[k + 1] = p.dma("sync", x_sb[(k + 1) % 2][:], x_t[k + 1],
                                  stream=(k + 1) % 2, after=[gate, tok_x_in])
        tk_xrT = []
        t_tr = None
        for s_i in range(NST):
            xs = xsb[:, s_i, :]
            rs, nb, tst = ln_stats(p, xs, sq_scr[:], sta, 12 * s_i, 1e-5,
                                   tok_in=[t_load[k], L.get("a_stat%d_%d" % (bi, s_i))])
            t_tok = p.do("scalar", lambda E, xs=xs, s_i=s_i, rs=rs, nb=nb: E.activation(
                tok_sb[:, s_i, :], xs, AF.Identity, bias=nb, scale=rs),
                after=[t_load[k], L.get("a_tok")], hard=[tst])
            rs2, trs2 = rms_stats(p, tok_sb[:, s_i, :], sq_scr[:], sta, 12 * s_i + 4,
                                  1e-4, tok_in=[t_tok])
            t_cast = p.do("scalar", lambda E, s_i=s_i, rs2=rs2: E.activation(
                xr_bf[:, s_i, :], tok_sb[:, s_i, :], AF.Identity, scale=rs2),
                after=[L.get("a_xr")], hard=[trs2])
            L["a_stat%d_%d" % (bi, s_i)] = t_cast
            for c in range(8):
                pt = psT[c % 2]
                t_tr = p.do("tensor", lambda E, c=c, s_i=s_i, pt=pt: E.transpose(
                    pt[:], xr_bf[:, s_i, 128 * c:128 * (c + 1)], com.ident[:]),
                    after=[t_cast, com.tok_ident, L.get("a_psT%d" % (c % 2))])
                t_ev = p.do("vector", lambda E, c=c, s_i=s_i, pt=pt, xrt=xrt: E.tensor_copy(
                    xrt[:, c, 128 * s_i:128 * (s_i + 1)], pt[:]),
                    after=[t_tr, L.get("a_xrT%d" % bi)])
                L["a_psT%d" % (c % 2)] = t_ev
                tk_xrT.append(t_ev)
        L["a_xr"] = t_tr

        t_uev = {}
        for nch in range(2):
            mm = None
            for c in range(8):
                mm = p.do("tensor", lambda E, nch=nch, c=c, xrt=xrt: E.matmul(
                    psU[nch][:], winT[:, c, 128 * nch:128 * (nch + 1)], xrt[:, c, :],
                    start=(c == 0), stop=(c == 7)),
                    after=tk_xrT + [L.get("a_psU%d" % nch)] + tk_win)
            for sg in range(2):
                t_uev[(nch, sg)] = p.do("scalar", lambda E, nch=nch, sg=sg, use=use: E.activation(
                    use[2 * nch + sg][:], psU[nch][:, 256 * sg:256 * (sg + 1)],
                    AF.Identity, bias=b_in_c[:, nch:nch + 1], scale=1.0),
                    after=[mm, dwtot, L.get("a_useg%d_%d" % (bi, 2 * nch + sg))])
            L["a_psU%d" % nch] = t_uev[(nch, 1)]

        seq_start = (k % SS_PER_SEQ) == 0
        t_sc = {}
        for nch in range(2):
            for sg in range(2):
                j = 2 * nch + sg
                if sg == 0 and seq_start:
                    init, hard_init = 0.0, []
                else:
                    init, hard_init = carry[nch][:, 0:1], [L.get("a_carrywr%d" % nch)]
                t_sc[j] = p.do("vector", lambda E, j=j, nch=nch, init=init, sts=sts, use=use: (
                    E.tensor_tensor_scan(sts[j][:], decseg[nch][:],
                                         use[j][:], init, ALU.mult, ALU.add)),
                    after=[t_uev[(nch, sg)], t_ds[nch]], hard=hard_init)
                L["a_useg%d_%d" % (bi, j)] = t_sc[j]
                L["a_carrywr%d" % nch] = p.do(
                    "vector", lambda E, j=j, nch=nch, sts=sts: E.tensor_copy(
                        carry[nch][:], sts[j][:, 255:256]))
        t_st = {}
        for j in range(4):
            t_st[j] = p.do("vector", lambda E, j=j, stb=stb, sts=sts: E.tensor_scalar_add(
                stb[j][:], sts[j][:], 0.0), after=[t_sc[j], L.get("a_ymm%d" % bi)])

        t_out = []
        last_mm = None
        t_inner = None
        for s_i in range(NST):
            t_ep = []
            for half in range(2):
                pyk = 2 * (s_i % 2) + half
                pY = psY[pyk]
                mm = None
                for nch in range(2):
                    j = 2 * nch + (s_i // 2)
                    off = 128 * (s_i % 2)
                    mm = p.do("tensor", lambda E, j=j, nch=nch, off=off, half=half, pY=pY, stb=stb: E.matmul(
                        pY[:], stb[j][:, off:off + 128],
                        woutT[:, nch, 512 * half:512 * (half + 1)],
                        start=(nch == 0), stop=False),
                        after=[t_st[j], L.get("a_psY%d" % pyk)] + tk_wout)
                for c in range(8):
                    mm = p.do("tensor", lambda E, c=c, s_i=s_i, half=half, pY=pY, xrt=xrt: E.matmul(
                        pY[:], xrt[:, c, 128 * s_i:128 * (s_i + 1)],
                        wskipT[:, c, 512 * half:512 * (half + 1)],
                        start=False, stop=(c == 7)),
                        after=tk_xrT + tk_wsk)
                last_mm = mm
                t_e = p.do("vector", lambda E, s_i=s_i, half=half, pY=pY, osb=osb: E.tensor_add(
                    osb[:, s_i, 512 * half:512 * (half + 1)], pY[:],
                    bias_rep[:, 512 * half:512 * (half + 1)]),
                    after=[mm, tb_e0, tb_e1, L.get("a_out%d" % bi)])
                L["a_psY%d" % pyk] = t_e
                t_ep.append(t_e)
            t_inner = p.do("vector", lambda E, s_i=s_i, osb=osb: E.tensor_add(
                osb[:, s_i, :], osb[:, s_i, :], tok_sb[:, s_i, :]),
                after=t_ep)
            rs3, nb3, tst3 = ln_stats(p, osb[:, s_i, :], sq_scr[:], sta,
                                      12 * s_i + 6, 1e-5, tok_in=[t_inner])
            t_ln = p.do("vector", lambda E, s_i=s_i, rs3=rs3, nb3=nb3, osb=osb: E.tensor_scalar(
                osb[:, s_i, :], osb[:, s_i, :], rs3, nb3, ALU.mult, ALU.add),
                hard=[tst3])
            t_fin = p.do("vector", lambda E, s_i=s_i, osb=osb, xsb=xsb: E.tensor_add(
                osb[:, s_i, :], osb[:, s_i, :], xsb[:, s_i, :]))
            t_out.append(t_fin)
        L["a_ymm%d" % bi] = last_mm
        L["a_xrT%d" % bi] = last_mm
        L["a_tok"] = t_inner
        L["a_x%d" % bi] = t_out[-1]
        if k == 0 and dbg is not None:
            p.dma("sync", dbg["tok"][:], tok_sb[:], sig="dbg", after=[t_inner])
            p.dma("sync", dbg["xr"][:], xr_bf[:], sig="dbg", after=[L["a_xr"]])
            p.dma("sync", dbg["xrT"][:], xrt[:], sig="dbg", after=[L.get("a_psU1")])
            p.dma("sync", dbg["useg"][:], use[0][:], sig="dbg", after=[t_sc[0]])
            p.dma("sync", dbg["stseg"][:], sts[0][:], sig="dbg", after=[t_st[0]])
            p.dma("sync", dbg["dec"][:], decseg[0][:], sig="dbg", after=[t_ds[0]])
            p.dma("sync", dbg["decs"][:], dec_s[:], sig="dbg", after=[t_ds[0]])
        t_str = p.dma("gpsimd", xo_t[k], osb[:], stream=bi, after=t_out)
        L["a_out%d" % bi] = t_str
        tok_store.append(t_str)

    for k in range(NSS):
        emit_ss(k)
    return tok_store



def build_l1_ffn(p, stk, com, dram, x_in, x_out, tok_x_in=None):
    nc = p.nc
    L = p.last

    stage = sb(stk, nc, "f_stage", [128, 8192], F32)
    u1b = sb(stk, nc, "f_u1", [128, 8, 256], BF16)
    v1b = sb(stk, nc, "f_v1", [128, 2, 4096], BF16)
    u2b = sb(stk, nc, "f_u2", [128, 32, 256], BF16)
    v2b = sb(stk, nc, "f_v2", [128, 2, 1024], BF16)
    b1c = sb(stk, nc, "f_b1c", [128, 32], F32)
    brow = sb(stk, nc, "f_brow", [1, 1024], F32)
    b2rep = sb(stk, nc, "f_b2rep", [128, 1024], F32)

    psP = ps(stk, nc, "f_psP", [128, 512], F32)
    psH = [ps(stk, nc, "f_psH%d" % i, [128, 512], F32) for i in range(2)]
    psQ = ps(stk, nc, "f_psQ", [128, 512], F32)
    psO = [ps(stk, nc, "f_psO%d" % i, [128, 512], F32) for i in range(2)]
    psT = [ps(stk, nc, "f_psT%d" % i, [128, 128], BF16) for i in range(2)]

    t_b1 = p.dma("sync", b1c[:], dram["b1"][:], sig="dw")
    t_b2 = p.dma("sync", brow[:], dram["b2"][:], sig="dw")
    dwtot = ("dw", p.cnt["dw"])

    def stg(nelem):
        return stage[:, 0:nelem]

    t_w = p.dma("sync", stg(2048).rearrange("p (c n) -> p c n", n=256),
                dram["u1"].rearrange("(c p) n -> p c n", p=128))
    c_u1 = p.do("vector", lambda E: E.tensor_scalar_add(
        u1b[:], stg(2048).rearrange("p (c n) -> p c n", n=256), 0.0), after=[t_w])
    t_w = p.dma("sync", stg(8192).rearrange("p (c f) -> p c f", f=4096),
                dram["v1"].rearrange("(c p) f -> p c f", p=128), after=[c_u1])
    c_v1 = p.do("vector", lambda E: E.tensor_scalar_add(
        v1b[:], stg(8192).rearrange("p (c f) -> p c f", f=4096), 0.0), after=[t_w])
    t_w = p.dma("sync", stg(8192).rearrange("p (c n) -> p c n", n=256),
                dram["u2"].rearrange("(c p) n -> p c n", p=128), after=[c_v1])
    c_u2 = p.do("vector", lambda E: E.tensor_scalar_add(
        u2b[:], stg(8192).rearrange("p (c n) -> p c n", n=256), 0.0), after=[t_w])
    t_w = p.dma("sync", stg(2048).rearrange("p (c f) -> p c f", f=1024),
                dram["v2"].rearrange("(c p) f -> p c f", p=128), after=[c_u2])
    c_v2 = p.do("vector", lambda E: E.tensor_scalar_add(
        v2b[:], stg(2048).rearrange("p (c f) -> p c f", f=1024), 0.0), after=[t_w])
    tb0 = p.do("tensor", lambda E: E.matmul(
        psO[0][:], com.onesrow_f[0:1, 0:128], brow[0:1, 0:512], start=True, stop=True),
        after=[dwtot, com.tok_ones])
    tb1 = p.do("tensor", lambda E: E.matmul(
        psO[1][:], com.onesrow_f[0:1, 0:128], brow[0:1, 512:1024], start=True, stop=True),
        after=[dwtot, com.tok_ones])
    tb_e0 = p.do("vector", lambda E: E.tensor_copy(b2rep[:, 0:512], psO[0][:]), after=[tb0])
    tb_e1 = p.do("vector", lambda E: E.tensor_copy(b2rep[:, 512:1024], psO[1][:]), after=[tb1])
    L["f_psO0"] = tb_e0
    L["f_psO1"] = tb_e1

    x_sb = [sb(stk, nc, "f_x%d" % i, [128, NST, 1024], F32) for i in range(2)]
    xn_bf = sb(stk, nc, "f_xn", [128, NST, 1024], BF16)
    xnT1 = sb(stk, nc, "f_xnT", [128, 8, 512], BF16)
    xnT = [xnT1, xnT1]
    p_sb = sb(stk, nc, "f_p", [128, NST, 256], BF16)
    p_fm = [sb(stk, nc, "f_pfm%d" % i, [128, 2, 512], BF16) for i in range(2)]
    h_sb = sb(stk, nc, "f_h", [128, 32, 512], BF16)
    q_fm = sb(stk, nc, "f_qfm", [128, 2, 512], BF16)
    out_sb1 = sb(stk, nc, "f_out", [128, NST, 1024], F32)
    out_sb = [out_sb1, out_sb1]
    sq_scr = sb(stk, nc, "f_sq", [128, 1024], F32)
    stat = [sb(stk, nc, "f_stat%d" % i, [128, 48], F32) for i in range(2)]

    x_t = x_in.rearrange("(k s p) d -> k p s d", p=128, s=NST)
    xo_t = x_out.rearrange("(k s p) d -> k p s d", p=128, s=NST)

    tok_store = []
    t_load = [None] * NSS
    t_load[0] = p.dma("sync", x_sb[0][:], x_t[0], stream=0, after=[tok_x_in, c_v2])

    def emit_ss(k):
        bi = k % 2
        xsb, osb, sta, xnt, pfm = x_sb[bi], out_sb[bi], stat[bi], xnT[bi], p_fm[bi]
        if k + 1 < NSS:
            gate = L.get("f_x%d" % ((k + 1) % 2)) if k >= 1 else c_v2
            t_load[k + 1] = p.dma("sync", x_sb[(k + 1) % 2][:], x_t[k + 1],
                                  stream=(k + 1) % 2, after=[gate, tok_x_in])
        tk_xnT = []
        t_tr = None
        for s_i in range(NST):
            xs = xsb[:, s_i, :]
            rs, nb, tst = ln_stats(p, xs, sq_scr[:], sta, 12 * s_i, 1e-5,
                                   tok_in=[t_load[k], L.get("f_stat%d_%d" % (bi, s_i))])
            t_xn = p.do("scalar", lambda E, xs=xs, s_i=s_i, rs=rs, nb=nb: E.activation(
                xn_bf[:, s_i, :], xs, AF.Identity, bias=nb, scale=rs),
                after=[t_load[k], L.get("f_xn")], hard=[tst])
            L["f_stat%d_%d" % (bi, s_i)] = t_xn
            for c in range(8):
                pt = psT[c % 2]
                t_tr = p.do("tensor", lambda E, c=c, s_i=s_i, pt=pt: E.transpose(
                    pt[:], xn_bf[:, s_i, 128 * c:128 * (c + 1)], com.ident[:]),
                    after=[t_xn, com.tok_ident, L.get("f_psT%d" % (c % 2))])
                t_ev = p.do("vector", lambda E, c=c, s_i=s_i, pt=pt, xnt=xnt: E.tensor_copy(
                    xnt[:, c, 128 * s_i:128 * (s_i + 1)], pt[:]),
                    after=[t_tr, L.get("f_xnT")])
                L["f_psT%d" % (c % 2)] = t_ev
                tk_xnT.append(t_ev)
        L["f_xn"] = t_tr

        t_pev = []
        for s_i in range(NST):
            mm = None
            for c in range(8):
                mm = p.do("tensor", lambda E, c=c, s_i=s_i, xnt=xnt: E.matmul(
                    psP[:, 0:256], xnt[:, c, 128 * s_i:128 * (s_i + 1)], u1b[:, c, :],
                    start=(c == 0), stop=(c == 7)),
                    after=tk_xnT + [L.get("f_psP"), c_u1])
            t_pe = p.do("vector", lambda E, s_i=s_i: E.tensor_scalar_add(
                p_sb[:, s_i, :], psP[:, 0:256], 0.0), after=[mm, L.get("f_p")])
            L["f_psP"] = t_pe
            L["f_xnT"] = mm
            t_pev.append(t_pe)
        tk_pfm = []
        for s_i in range(NST):
            for rc in range(2):
                pt = psT[rc % 2]
                t_tr = p.do("tensor", lambda E, s_i=s_i, rc=rc, pt=pt: E.transpose(
                    pt[:], p_sb[:, s_i, 128 * rc:128 * (rc + 1)], com.ident[:]),
                    after=[t_pev[s_i], com.tok_ident, L.get("f_psT%d" % (rc % 2))])
                t_ev = p.do("vector", lambda E, s_i=s_i, rc=rc, pt=pt, pfm=pfm: E.tensor_copy(
                    pfm[:, rc, 128 * s_i:128 * (s_i + 1)], pt[:]),
                    after=[t_tr, L.get("f_pfm%d" % bi)])
                L["f_psT%d" % (rc % 2)] = t_ev
                tk_pfm.append(t_ev)
        L["f_p"] = t_tr

        t_hev = []
        for hc in range(32):
            ph = psH[hc % 2]
            mm = None
            for rc in range(2):
                mm = p.do("tensor", lambda E, hc=hc, rc=rc, ph=ph, pfm=pfm: E.matmul(
                    ph[:], v1b[:, rc, 128 * hc:128 * (hc + 1)], pfm[:, rc, :],
                    start=(rc == 0), stop=(rc == 1)),
                    after=tk_pfm + [L.get("f_psH%d" % (hc % 2)), c_v1])
            t_h = p.do("scalar", lambda E, hc=hc, ph=ph: E.activation(
                h_sb[:, hc, :], ph[:], AF.Gelu_apprx_tanh, bias=b1c[:, hc:hc + 1], scale=1.0),
                after=[mm, dwtot, L.get("f_h")])
            L["f_psH%d" % (hc % 2)] = t_h
            t_hev.append(t_h)

        t_qev = []
        for rc in range(2):
            mm = None
            for hc in range(32):
                mm = p.do("tensor", lambda E, rc=rc, hc=hc: E.matmul(
                    psQ[:], u2b[:, hc, 128 * rc:128 * (rc + 1)], h_sb[:, hc, :],
                    start=(hc == 0), stop=(hc == 31)),
                    after=t_hev + [L.get("f_psQ"), c_u2])
            t_q = p.do("vector", lambda E, rc=rc: E.tensor_scalar_add(
                q_fm[:, rc, :], psQ[:], 0.0), after=[mm, L.get("f_qfm")])
            L["f_psQ"] = t_q
            t_qev.append(t_q)
        L["f_h"] = mm

        t_out = []
        last_mm = None
        for s_i in range(NST):
            t_ep = []
            for half in range(2):
                pO = psO[half]
                mm = None
                for rc in range(2):
                    mm = p.do("tensor", lambda E, s_i=s_i, half=half, rc=rc, pO=pO: E.matmul(
                        pO[:], q_fm[:, rc, 128 * s_i:128 * (s_i + 1)],
                        v2b[:, rc, 512 * half:512 * (half + 1)],
                        start=(rc == 0), stop=(rc == 1)),
                        after=t_qev + [L.get("f_psO%d" % half), c_v2])
                last_mm = mm
                t_e = p.do("vector", lambda E, s_i=s_i, half=half, pO=pO, osb=osb: E.tensor_add(
                    osb[:, s_i, 512 * half:512 * (half + 1)], pO[:],
                    b2rep[:, 512 * half:512 * (half + 1)]),
                    after=[mm, tb_e0, tb_e1, L.get("f_out")])
                L["f_psO%d" % half] = t_e
                t_ep.append(t_e)
            rs3, nb3, tst3 = ln_stats(p, osb[:, s_i, :], sq_scr[:], sta,
                                      12 * s_i + 6, 1e-5, tok_in=t_ep)
            t_ln = p.do("vector", lambda E, s_i=s_i, rs3=rs3, nb3=nb3, osb=osb: E.tensor_scalar(
                osb[:, s_i, :], osb[:, s_i, :], rs3, nb3, ALU.mult, ALU.add),
                hard=[tst3])
            t_fin = p.do("vector", lambda E, s_i=s_i, osb=osb, xsb=xsb: E.tensor_add(
                osb[:, s_i, :], osb[:, s_i, :], xsb[:, s_i, :]))
            t_out.append(t_fin)
        L["f_qfm"] = last_mm
        L["f_x%d" % bi] = t_out[-1]
        t_str = p.dma("gpsimd", xo_t[k], osb[:], stream=bi, after=t_out)
        L["f_out"] = t_str
        tok_store.append(t_str)

    for k in range(NSS):
        emit_ss(k)
    return tok_store





def ternarize(p, com, w_r, n_chunks, ncols, stage, wtern_view, mwcol, psrep, pref,
              gate0=None, gate1=None, wtern_war=None):
    L = p.last
    acc = com.rowacc
    t0 = p.do("vector", lambda E: E.memset(acc[:, 0:1], 0.0))
    tc = [gate0, gate1]
    for c in range(n_chunks):
        sl = c % 2
        t_w = p.dma("sync", stage[:, sl, 0:ncols], w_r[c], stream=sl, after=[tc[sl]])
        p.do("vector", lambda E, sl=sl: E.reduce_sum(
            acc[:, 1:2], stage[:, sl, 0:ncols], axis=AX.X, apply_absolute_value=True),
            after=[t_w], hard=[t0])
        tc[sl] = p.do("vector", lambda E: E.tensor_add(acc[:, 0:1], acc[:, 0:1], acc[:, 1:2]))
    pskey = "war_" + psrep.name
    t_mm = p.do("tensor", lambda E: E.matmul(
        psrep[0:1, 0:1], com.onescol_f[:, 0:1], acc[:, 0:1], start=True, stop=True),
        after=[tc[0], tc[1], com.tok_ones, L.get(pskey)])
    row = com.rowscratch
    p.do("vector", lambda E, n=n_chunks * 128 * ncols: E.tensor_scalar(
        row[0:1, 0:1], psrep[0:1, 0:1], 1.0 / n, 1e-5, ALU.mult, ALU.max), after=[t_mm])
    p.do("vector", lambda E: E.tensor_scalar_mul(row[0:1, 1:2], row[0:1, 0:1], -1.0))
    p.do("vector", lambda E: E.tensor_scalar_mul(row[0:1, 2:3], row[0:1, 0:1], 0.5))
    t2 = p.do("vector", lambda E: E.tensor_scalar_mul(row[0:1, 3:4], row[0:1, 0:1], -0.5))
    t_rep = p.do("tensor", lambda E: E.matmul(
        psrep[:, 0:3], com.onesrow_f[0:1, 0:128], row[0:1, 1:4], start=True, stop=True),
        after=[t2])
    t_mw = p.do("vector", lambda E: E.tensor_copy(mwcol[:], psrep[:, 0:3]), after=[t_rep])
    L[pskey] = t_mw
    tern_done = []
    tc = [t_mw, t_mw]
    for c in range(n_chunks):
        sl = c % 2
        t_w = p.dma("sync", stage[:, sl, 0:ncols], w_r[c], stream=sl, after=[tc[sl]])
        t_a = p.do("vector", lambda E, sl=sl: E.tensor_scalar(
            com.rndA[:, 0:ncols], stage[:, sl, 0:ncols], mwcol[:, 1:2], None, ALU.is_gt),
            after=[t_w, t_mw])
        tc[sl] = p.do("vector", lambda E, c=c, sl=sl: E.scalar_tensor_tensor(
            wtern_view[:, c, 0:ncols], stage[:, sl, 0:ncols], mwcol[:, 2:3],
            com.rndA[:, 0:ncols], ALU.is_lt, ALU.subtract),
            after=[wtern_war if c == 0 else None])
        tern_done.append(tc[sl])
    return tern_done


def build_l0_ffn(p, stk, com, dram, x_in, x_out, h2_dram, tok_x_in=None, dbg=None):
    nc = p.nc
    L = p.last

    wtern = sb(stk, nc, "c_wtern", [128, 32768], BF16)
    w1t = wtern[:].rearrange("p (c f) -> p c f", f=4096)
    w2t = wtern[:].rearrange("p (c f) -> p c f", f=1024)
    mw1c = sb(stk, nc, "c_mw1", [128, 3], F32)
    mw2c = sb(stk, nc, "c_mw2", [128, 3], F32)
    cwc = sb(stk, nc, "c_cw", [128, 3, 32], F32)
    cbc = sb(stk, nc, "c_cb", [128, 32], F32)

    psT = [ps(stk, nc, "c_psT%d" % i, [128, 128], BF16) for i in range(2)]
    psH = [ps(stk, nc, "c_psH%d" % i, [128, 512], F32) for i in range(2)]
    psO = [ps(stk, nc, "c_psO%d" % i, [128, 512], F32) for i in range(4)]

    t_cw = p.dma("sync", cwc[:], dram["conv_w"][:], sig="dw")
    t_cb = p.dma("sync", cbc[:], dram["conv_b"][:], sig="dw")
    dwtot = ("dw", p.cnt["dw"])

    w1_r = dram["w1_t"].rearrange("(c p) f -> c p f", p=128)
    w2_r = dram["w2_t"].rearrange("(c p) f -> c p f", p=128)

    x_sb = [sb(stk, nc, "c_x%d" % i, [128, NST, 1024], F32) for i in range(2)]
    sq_scr = sb(stk, nc, "c_sq", [128, 1024], F32)
    stat = [sb(stk, nc, "c_stat%d" % i, [128, 48], F32) for i in range(2)]

    stk1 = stk.enter_context(contextlib.ExitStack())
    stage = sb(stk1, nc, "c_stage", [128, 2, 1024], F32)
    xn_f = sb(stk1, nc, "c_xn", [128, 1024], F32)
    xq_bf = sb(stk1, nc, "c_xq", [128, NST, 1024], BF16)
    xqT1 = sb(stk1, nc, "c_xqT", [128, 8, 512], BF16)
    cbuf = sb(stk1, nc, "c_cbuf", [128, 2, 520], F32)
    halo = sb(stk1, nc, "c_halo", [128, 32, 2], F32)
    h2s = sb(stk1, nc, "c_h2st", [128, 32, 520], BF16)
    cvscr = sb(stk1, nc, "c_cvscr", [128, 2, 520], F32)

    w1_r8 = dram["w1_t"].rearrange("(c p) f -> c p f", p=128)
    w1_chunks = [w1_r8[c][:, 1024 * h:1024 * (h + 1)] for c in range(8) for h in range(4)]
    tern1 = ternarize(p, com, w1_chunks, 32, 1024, stage, w2t, mw1c, psO[0], "c_w1",
                      gate0=L.get("gate_dl0"), gate1=L.get("gate_dl1"))

    x_t = x_in.rearrange("(k s p) d -> k p s d", p=128, s=NST)
    xo_t = x_out.rearrange("(k s p) d -> k p s d", p=128, s=NST)

    t_load = [None] * NSS
    t_load[0] = p.dma("sync", x_sb[0][:], x_t[0], stream=0, after=[tok_x_in, tern1[-2]])
    h2win = [None] * NSS

    def emit_sweep1(k):
        bi = k % 2
        b_i, w_i = k // SS_PER_SEQ, k % SS_PER_SEQ
        xsb, sta = x_sb[bi], stat[bi]
        if k + 1 < NSS:
            gate = L.get("c_x%d" % ((k + 1) % 2)) if k >= 1 else tern1[-1]
            t_load[k + 1] = p.dma("sync", x_sb[(k + 1) % 2][:], x_t[k + 1],
                                  stream=(k + 1) % 2, after=[gate, tok_x_in])
        t_hz = None
        if w_i == 0:
            t_hz = p.do("vector", lambda E: E.memset(halo[:], 0.0),
                        after=[L.get("c_halo_rd")])
        tk_xqT = []
        t_tr = None
        t_q = None
        for s_i in range(NST):
            xs = xsb[:, s_i, :]
            rs, nb, tst = ln_stats(p, xs, sq_scr[:], sta, 12 * s_i, 1e-5,
                                   tok_in=[t_load[k], L.get("c_stat%d_%d" % (bi, s_i))])
            am, iv = sta[:, 12 * s_i + 4:12 * s_i + 5], sta[:, 12 * s_i + 5:12 * s_i + 6]
            t_xn = p.do("scalar", lambda E, xs=xs, rs=rs, nb=nb: E.activation(
                xn_f[:], xs, AF.Identity, bias=nb, scale=rs),
                after=[t_load[k], L.get("c_xnw")], hard=[tst])
            t_am = p.do("vector", lambda E, am=am: E.reduce_max(
                am, xn_f[:], axis=AX.X, apply_absolute_value=True), after=[t_xn])
            p.do("vector", lambda E, am=am: E.tensor_scalar_max(am, am, 1e-5))
            t_iv = p.do("vector", lambda E, am=am, iv=iv: E.reciprocal(iv, am))
            t_iv = p.do("vector", lambda E, iv=iv: E.tensor_scalar_mul(iv, iv, 127.0))
            am127 = sta[:, 12 * s_i + 6:12 * s_i + 7]
            t_a7 = p.do("vector", lambda E, am=am, am127=am127: E.tensor_scalar_mul(
                am127, am, 1.0 / 127.0))
            t_sc = p.do("scalar", lambda E, iv=iv: E.activation(
                sq_scr[:], xn_f[:], AF.Identity, scale=iv),
                after=[L.get("c_sqw")], hard=[t_iv])
            L["c_xnw"] = t_sc
            p.do("vector", lambda E: E.tensor_copy(com.i32scr[:, 0:1024], sq_scr[:]),
                 after=[t_sc, L.get("com_i32")])
            p.do("vector", lambda E: E.tensor_sub(com.rndB[:], sq_scr[:], com.i32scr[:, 0:1024]))
            p.do("vector", lambda E: E.scalar_tensor_tensor(
                com.rndA[:], com.rndB[:], 0.5, com.i32scr[:, 0:1024], ALU.is_gt, ALU.add))
            t_r = p.do("vector", lambda E: E.scalar_tensor_tensor(
                sq_scr[:], com.rndB[:], -0.5, com.rndA[:], ALU.is_ge, ALU.add))
            L["c_sqw"] = t_r
            t_q = p.do("vector", lambda E, s_i=s_i, am127=am127: E.tensor_scalar(
                xq_bf[:, s_i, :], sq_scr[:], -1.0, am127, ALU.add, ALU.mult),
                after=[L.get("c_xqw")], hard=[t_a7])
            L["com_i32"] = t_q
            L["c_stat%d_%d" % (bi, s_i)] = t_q
            for c in range(8):
                pt = psT[c % 2]
                t_tr = p.do("tensor", lambda E, c=c, s_i=s_i, pt=pt: E.transpose(
                    pt[:], xq_bf[:, s_i, 128 * c:128 * (c + 1)], com.ident[:]),
                    after=[t_q, com.tok_ident, L.get("c_psT%d" % (c % 2))])
                t_ev = p.do("vector", lambda E, c=c, s_i=s_i, pt=pt: E.tensor_copy(
                    xqT1[:, c, 128 * s_i:128 * (s_i + 1)], pt[:]),
                    after=[t_tr, L.get("c_xqT")])
                L["c_psT%d" % (c % 2)] = t_ev
                tk_xqT.append(t_ev)
        L["c_xqw"] = t_tr
        L["c_x%d" % bi] = t_q
        if dbg is not None and k == 0:
            p.dma("sync", dbg["xq"][:], xq_bf[:], sig="dbg", after=[t_tr])
            p.dma("sync", dbg["wt"][:], wtern[:, 0:8192], sig="dbg", after=[t_tr])

        first = (w_i == 0)
        last = (w_i == SS_PER_SEQ - 1)
        ncv = 513 if last else 512
        t_h2 = []
        t_hs = None
        for hc in range(32):
            ph = psH[hc % 2]
            mm = None
            for c in range(8):
                mm = p.do("tensor", lambda E, hc=hc, c=c, ph=ph: E.matmul(
                    ph[:], w1t[:, c, 128 * hc:128 * (hc + 1)], xqT1[:, c, :],
                    start=(c == 0), stop=(c == 7)),
                    after=tk_xqT + [L.get("c_psH%d" % (hc % 2))] +
                          ([tern1[-1]] if k == 0 else []))
            L["c_mm1last"] = mm
            t_ev = p.do("vector", lambda E, hc=hc, ph=ph: E.tensor_copy(
                cbuf[:, hc % 2, 2:514], ph[:]), after=[mm, L.get("c_cbuf%d" % (hc % 2))])
            t_hc = p.do("vector", lambda E, hc=hc: E.tensor_copy(
                cbuf[:, hc % 2, 0:2], halo[:, hc, :]), after=[t_hz])
            t_hs = p.do("vector", lambda E, hc=hc: E.tensor_copy(
                halo[:, hc, :], cbuf[:, hc % 2, 512:514]))
            if last:
                p.do("vector", lambda E, hc=hc: E.memset(cbuf[:, hc % 2, 514:515], 0.0))
            cv = cvscr[:, hc % 2, :]
            t_c1 = p.do("vector", lambda E, hc=hc, ncv=ncv, cv=cv: E.tensor_scalar(
                cv[:, 0:ncv] if False else cv[0:128, 0:ncv], cbuf[:, hc % 2, 0:ncv],
                cwc[:, 0, hc:hc + 1], None, ALU.mult),
                after=[L.get("com_cvscr%d" % (hc % 2))], hard=[dwtot])
            p.do("vector", lambda E, hc=hc, ncv=ncv, cv=cv: E.scalar_tensor_tensor(
                cv[0:128, 0:ncv], cbuf[:, hc % 2, 1:1 + ncv], cwc[:, 1, hc:hc + 1],
                cv[0:128, 0:ncv], ALU.mult, ALU.add))
            t_c3 = p.do("vector", lambda E, hc=hc, ncv=ncv, cv=cv: E.scalar_tensor_tensor(
                cv[0:128, 0:ncv], cbuf[:, hc % 2, 2:2 + ncv], cwc[:, 2, hc:hc + 1],
                cv[0:128, 0:ncv], ALU.mult, ALU.add))
            if dbg is not None and k == 0 and hc == 0:
                t_d = p.dma("sync", dbg["cbuf"][:], cbuf[:, 0, 0:514], sig="dbg", after=[t_c3])
                L["c_cbuf0"] = t_d
            L["c_cbuf%d" % (hc % 2)] = t_c3 if not (dbg is not None and k == 0 and hc == 0) else t_d
            t_si = p.do("scalar", lambda E, hc=hc, ncv=ncv, cv=cv: E.activation(
                h2s[:, hc, 0:ncv], cv[0:128, 0:ncv], AF.Silu,
                bias=cbc[:, hc:hc + 1], scale=mw1c[:, 0:1]),
                after=[t_c3, L.get("c_h2st")], hard=[dwtot, tern1[0]])
            L["com_cvscr%d" % (hc % 2)] = t_si
            L["c_psH%d" % (hc % 2)] = t_ev
            t_h2.append(t_si)
        L["c_halo_rd"] = t_hc
        L["c_halo"] = t_hs
        if first:
            off, lo, ln_ = 2048 * b_i, 1, 511
        elif last:
            off, lo, ln_ = 2048 * b_i + 512 * w_i - 1, 0, 513
        else:
            off, lo, ln_ = 2048 * b_i + 512 * w_i - 1, 0, 512
        t_st = []
        for g in range(4):
            t_st.append(p.dma(
                "gpsimd",
                h2_dram[8 * g:8 * (g + 1), :, off:off + ln_].rearrange("c p t -> p c t"),
                h2s[:, 8 * g:8 * (g + 1), lo:lo + ln_], stream=bi,
                after=t_h2 + [L.get("c_h2w%d" % bi)]))
        h2win[k] = t_st[-1]
        L["c_h2w%d" % bi] = t_st[-1]
        L["c_h2st"] = t_st[-1]

    for k in range(NSS):
        emit_sweep1(k)

    w2_chunks = [w2_r[c] for c in range(32)]
    tern2 = ternarize(p, com, w2_chunks, 32, 1024, stage, w2t, mw2c, psO[1], "c_w2",
                      gate0=L.get("c_x0"), gate1=L.get("c_x1"),
                      wtern_war=L.get("c_mm1last"))
    stk1.close()
    p.barrier()

    amrow = sb(stk, nc, "c_amrow", [1, 512], BF16)
    s2row = sb(stk, nc, "c_s2row", [1, 512], F32)
    s2rep = sb(stk, nc, "c_s2rep", [128, 512], F32)
    h2l1 = sb(stk, nc, "c_h2l", [128, 32, 512], BF16)
    h2l = [h2l1, h2l1]
    xq2 = sb(stk, nc, "c_xq2", [128, 32, 512], BF16)
    mx = sb(stk, nc, "c_mx", [128, 512], BF16)
    mxc = sb(stk, nc, "c_mxc", [128, 4], BF16)
    out_sb1 = sb(stk, nc, "c_out", [128, NST, 1024], F32)
    out_sb = [out_sb1, out_sb1]
    oscol = sb(stk, nc, "c_oscol", [128, NST], F32)
    w2csrow = sb(stk, nc, "c_w2csrow", [1, 1024], BF16)
    w2csrep = sb(stk, nc, "c_w2csrep", [128, 1024], F32)
    onescol_b = sb(stk, nc, "c_onescol_b", [128, 1], BF16)
    t_ocb = p.do("vector", lambda E: E.memset(onescol_b[:], 1.0))
    t_cs = {}
    for half in range(2):
        mmcs = None
        for hc in range(32):
            mmcs = p.do("tensor", lambda E, half=half, hc=hc: E.matmul(
                psH[half][0:1, 0:512], onescol_b[:, 0:1], w2t[:, hc, 512 * half:512 * (half + 1)],
                start=(hc == 0), stop=(hc == 31)),
                after=[t_ocb, tern2[-1], tern2[-2]], hard=[t_ocb])
        t_cs[half] = p.do("vector", lambda E, half=half: E.tensor_copy(
            w2csrow[0:1, 512 * half:512 * (half + 1)], psH[half][0:1, 0:512]), after=[mmcs])
    t_rp1 = p.do("tensor", lambda E: E.matmul(
        psH[0][:], com.onesrow_b[0:1, 0:128], w2csrow[0:1, 0:512], start=True, stop=True),
        after=[t_cs[0], t_cs[1], com.tok_ones])
    t_rp2 = p.do("tensor", lambda E: E.matmul(
        psH[1][:], com.onesrow_b[0:1, 0:128], w2csrow[0:1, 512:1024], start=True, stop=True),
        after=[t_cs[0], t_cs[1], com.tok_ones])
    t_cr1 = p.do("vector", lambda E: E.tensor_copy(w2csrep[:, 0:512], psH[0][:]), after=[t_rp1])
    t_cr2 = p.do("vector", lambda E: E.tensor_copy(w2csrep[:, 512:1024], psH[1][:]), after=[t_rp2])
    L["c_psH0"] = t_cr1
    L["c_psH1"] = t_cr2

    def h2gate(k):
        toks = [h2win[k]]
        if k % SS_PER_SEQ != SS_PER_SEQ - 1:
            toks.append(h2win[k + 1])
        return toks

    h2l_t = [None] * NSS
    h2l_t[0] = p.dma("sync", h2l[0][:],
                     h2_dram[:, :, 0:512].rearrange("c p t -> p c t"), stream=0,
                     after=[tern2[-2]] + h2gate(0))
    x2_t = [None] * NSS
    x2_t[0] = p.dma("sync", x_sb[0][:], x_t[0], sig="dx0", after=[tern2[-1]])

    def emit_sweep2(k):
        bi = k % 2
        xsb, sta, osb, h2b = x_sb[bi], stat[bi], out_sb[bi], h2l[bi]
        t_mx = p.do("vector", lambda E: E.scalar_tensor_tensor(
            mx[:], h2b[:, 0, :], -1.0, h2b[:, 0, :], ALU.mult, ALU.max),
            after=[h2l_t[k], L.get("c_mx")])
        for hc in range(1, 32):
            p.do("vector", lambda E, hc=hc: E.tensor_max(
                mx[:], mx[:], h2b[:, hc, :]))
            t_mx = p.do("vector", lambda E, hc=hc: E.scalar_tensor_tensor(
                mx[:], h2b[:, hc, :], -1.0, mx[:], ALU.mult, ALU.max))
        t_am = []
        for s_i in range(NST):
            pt = psT[s_i % 2]
            t_tr = p.do("tensor", lambda E, s_i=s_i, pt=pt: E.transpose(
                pt[:], mx[:, 128 * s_i:128 * (s_i + 1)], com.ident[:]),
                after=[t_mx, com.tok_ident, L.get("c_psT%d" % (s_i % 2))])
            t_rm = p.do("vector", lambda E, s_i=s_i, pt=pt: E.reduce_max(
                oscol[:, s_i:s_i + 1], pt[:], axis=AX.X),
                after=[t_tr, L.get("c_oscol")])
            L["c_psT%d" % (s_i % 2)] = t_rm
            t_am.append(t_rm)
        t_rowp = []
        for s_i in range(NST):
            pt = psT[s_i % 2]
            t_cc = p.do("vector", lambda E, s_i=s_i: E.tensor_scalar_add(
                mxc[:, s_i:s_i + 1], oscol[:, s_i:s_i + 1], 0.0), after=[t_am[s_i]])
            t_tr = p.do("tensor", lambda E, s_i=s_i, pt=pt: E.transpose(
                pt[0:1, 0:128], mxc[:, s_i:s_i + 1], com.ident[:]),
                after=[t_cc, L.get("c_psT%d" % (s_i % 2))])
            t_ev = p.do("vector", lambda E, s_i=s_i, pt=pt: E.tensor_copy(
                amrow[0:1, 128 * s_i:128 * (s_i + 1)], pt[0:1, 0:128]),
                after=[t_tr])
            L["c_psT%d" % (s_i % 2)] = t_ev
            t_rowp.append(t_ev)
        L["c_mx"] = t_rowp[-1]
        t_s1 = p.do("vector", lambda E: E.tensor_scalar_max(s2row[:], amrow[:], 1e-5),
                    after=t_rowp)
        p.do("vector", lambda E: E.reciprocal(s2row[:], s2row[:]))
        t_s3 = p.do("vector", lambda E: E.tensor_scalar_mul(s2row[:], s2row[:], 127.0))
        t_rp = p.do("tensor", lambda E: E.matmul(
            psH[0][:], com.onesrow_f[0:1, 0:128], s2row[0:1, :], start=True, stop=True),
            after=[t_s3, com.tok_ones, L.get("c_psH0")])
        t_re = p.do("vector", lambda E: E.tensor_copy(s2rep[:], psH[0][:]), after=[t_rp])
        L["c_psH0"] = t_re
        t_xq2 = []
        for hc in range(32):
            p.do("vector", lambda E, hc=hc: E.tensor_tensor(
                com.rndA[:, 0:512], h2b[:, hc, :], s2rep[:], ALU.mult),
                after=[t_re, L.get("com_i32")])
            p.do("vector", lambda E: E.tensor_copy(com.i32scr[:, 0:512], com.rndA[:, 0:512]))
            p.do("vector", lambda E: E.tensor_sub(
                com.rndB[:, 0:512], com.rndA[:, 0:512], com.i32scr[:, 0:512]))
            p.do("vector", lambda E: E.scalar_tensor_tensor(
                com.rndA[:, 0:512], com.rndB[:, 0:512], 0.5, com.i32scr[:, 0:512],
                ALU.is_gt, ALU.add))
            t_b = p.do("vector", lambda E, hc=hc: E.scalar_tensor_tensor(
                xq2[:, hc, :], com.rndB[:, 0:512], -0.5, com.rndA[:, 0:512],
                ALU.is_ge, ALU.add), after=[L.get("c_xq2")])
            L["com_i32"] = t_b
            t_xq2.append(t_b)
        L["c2_h2l"] = t_xq2[-1]
        if k + 1 < NSS:
            nb_ = (k + 1) % 2
            h2l_t[k + 1] = p.dma("sync", h2l[nb_][:],
                                 h2_dram[:, :, 512 * (k + 1):512 * (k + 2)].rearrange("c p t -> p c t"),
                                 stream=nb_, after=[t_xq2[-1]] + h2gate(k + 1))
            gate2 = L.get("c2_x%d" % nb_) if k >= 1 else None
            x2_t[k + 1] = p.dma("sync", x_sb[nb_][:], x_t[k + 1],
                                sig="dx%d" % nb_, after=[gate2])
        t_os = []
        for s_i in range(NST):
            p.do("vector", lambda E, s_i=s_i: E.tensor_scalar_max(
                oscol[:, s_i:s_i + 1], oscol[:, s_i:s_i + 1], 1e-5), after=[t_am[s_i]])
            t_o2 = p.do("vector", lambda E, s_i=s_i: E.scalar_tensor_tensor(
                oscol[:, s_i:s_i + 1], oscol[:, s_i:s_i + 1], 1.0 / 127.0,
                mw2c[:, 0:1], ALU.mult, ALU.mult), hard=[tern2[0]])
            t_os.append(t_o2)
        t_out = []
        last_mm = None
        t_elast = None
        for s_i in range(NST):
            t_ep = []
            for half in range(2):
                pok = 2 * (s_i % 2) + half
                pO = psO[pok]
                mm = None
                for hc in range(32):
                    mm = p.do("tensor", lambda E, s_i=s_i, half=half, hc=hc, pO=pO: E.matmul(
                        pO[:], xq2[:, hc, 128 * s_i:128 * (s_i + 1)],
                        w2t[:, hc, 512 * half:512 * (half + 1)],
                        start=(hc == 0), stop=(hc == 31)),
                        after=[t_xq2[-1], L.get("c_psO%d" % pok)] +
                              ([tern2[-1]] if k == 0 else []))
                last_mm = mm
                epib = com.rndA if half == 0 else com.rndB
                t_sub = p.do("vector", lambda E, half=half, pO=pO, epib=epib: E.tensor_sub(
                    epib[:, 512:1024], pO[:], w2csrep[:, 512 * half:512 * (half + 1)]),
                    after=[mm, t_cr1, t_cr2, L.get("c_epi%d" % half)])
                t_e = p.do("scalar", lambda E, s_i=s_i, half=half, osb=osb, epib=epib: E.activation(
                    osb[:, s_i, 512 * half:512 * (half + 1)], epib[:, 512:1024], AF.Identity,
                    scale=oscol[:, s_i:s_i + 1]),
                    after=[t_sub, L.get("c_out")], hard=[t_os[s_i]])
                L["c_epi%d" % half] = t_e
                L["c_psO%d" % pok] = t_sub
                t_elast = t_e
                t_ep.append(t_e)
            rs3, nb3, tst3 = ln_stats(p, osb[:, s_i, :], sq_scr[:], sta,
                                      12 * s_i + 6, 1e-5, tok_in=t_ep)
            t_ln = p.do("vector", lambda E, s_i=s_i, rs3=rs3, nb3=nb3, osb=osb: E.tensor_scalar(
                osb[:, s_i, :], osb[:, s_i, :], rs3, nb3, ALU.mult, ALU.add),
                hard=[tst3])
            t_fin = p.do("vector", lambda E, s_i=s_i, osb=osb, xsb=xsb: E.tensor_add(
                osb[:, s_i, :], osb[:, s_i, :], xsb[:, s_i, :]), after=[x2_t[k]])
            t_out.append(t_fin)
        L["c_oscol"] = t_elast
        L["c_xq2"] = last_mm
        L["c2_x%d" % bi] = t_out[-1]
        t_str = p.dma("gpsimd", xo_t[k], osb[:], stream=bi, after=t_out)
        L["c_out"] = t_str
        return t_str

    tok_store = []
    for k in range(NSS):
        tok_store.append(emit_sweep2(k))
    return tok_store





def build_l1_attn(p, stk, com, dram, x_in, x_out, ctx_dram, tok_x_in=None):
    nc = p.nc
    L = p.last

    stage = sb(stk, nc, "b_stage", [128, 2, 1024], F32)
    terns = {}
    mws = {}
    for nm in ("q", "k", "v", "o"):
        terns[nm] = sb(stk, nc, "b_tern%s" % nm, [128, 8, 1024], BF16)
        mws[nm] = sb(stk, nc, "b_mw%s" % nm, [128, 3], F32)
    identf = sb(stk, nc, "b_identf", [128, 128], F32)
    psS = [ps(stk, nc, "b_psS%d" % i, [128, 1024], F32) for i in range(2)]
    psC = [ps(stk, nc, "b_psC%d" % i, [128, 128], F32) for i in range(2)]

    t_if = p.do("vector", lambda E: E.memset(identf[:], 1.0))
    tok_identf = p.do(
        "gpsimd",
        lambda E: E.affine_select(identf[:], identf[:], [[-1, 128]],
                                  ALU.is_equal, 0.0, base=0, channel_multiplier=1),
        after=[t_if])

    w_r = {nm: dram["w%s_t" % nm].rearrange("(c p) e -> c p e", p=128)
           for nm in ("q", "k", "v", "o")}
    g0, g1 = L.get("gate_dl0"), L.get("gate_dl1")
    tern_t = {}
    for nm in ("q", "k", "v", "o"):
        tern_t[nm] = ternarize(p, com, [w_r[nm][c] for c in range(8)], 8, 1024,
                               stage, terns[nm], mws[nm], psS[0], "b_" + nm,
                               gate0=g0, gate1=g1, wtern_war=None)
        g0, g1 = tern_t[nm][-2], tern_t[nm][-1]

    x_sb = [sb(stk, nc, "b_x%d" % i, [128, NST, 1024], F32) for i in range(2)]
    xn_f = sb(stk, nc, "b_xn", [128, 1024], F32)
    xqT = sb(stk, nc, "b_xqT", [128, 8, 512], BF16)
    qtok = sb(stk, nc, "b_qtok", [128, 16, 64], BF16)
    ktok = sb(stk, nc, "b_ktok", [128, 16, 64], BF16)
    vtok = sb(stk, nc, "b_vtok", [128, 16, 64], BF16)
    prodA = sb(stk, nc, "b_prodA", [128, 2, 16, 64], BF16)
    prodB = sb(stk, nc, "b_prodB", [128, 2, 64, 16], BF16)
    scb = sb(stk, nc, "b_sc", [128, 16, 16], F32)
    attnb = sb(stk, nc, "b_attn", [128, 16, 16], BF16)
    smx = sb(stk, nc, "b_smx", [128, 16], F32)
    ctxs = sb(stk, nc, "b_ctxs", [128, 2, 1024], F32)
    sq_scr = sb(stk, nc, "b_sq", [128, 1024], F32)
    stat = [sb(stk, nc, "b_stat%d" % i, [128, 48], F32) for i in range(2)]

    x_t = x_in.rearrange("(k s p) d -> k p s d", p=128, s=NST)
    xo_t = x_out.rearrange("(k s p) d -> k p s d", p=128, s=NST)

    tok_store = []
    t_load = [None] * NSS
    t_load[0] = p.dma("sync", x_sb[0][:], x_t[0], stream=0,
                      after=[tok_x_in, tern_t["o"][-2]])
    ctxw = [None] * (NSS * NST)

    def emit_sweep1(k):
        bi = k % 2
        xsb, sta = x_sb[bi], stat[bi]
        if k + 1 < NSS:
            gate = L.get("b_x%d" % ((k + 1) % 2)) if k >= 1 else tern_t["o"][-1]
            t_load[k + 1] = p.dma("sync", x_sb[(k + 1) % 2][:], x_t[k + 1],
                                  stream=(k + 1) % 2, after=[gate, tok_x_in])
        t_q = None
        for s_i in range(NST):
            xs = xsb[:, s_i, :]
            rs, nb, tst = ln_stats(p, xs, sq_scr[:], sta, 12 * s_i, 1e-5,
                                   tok_in=[t_load[k], L.get("b_stat%d_%d" % (bi, s_i))])
            t_xn = p.do("scalar", lambda E, xs=xs, rs=rs, nb=nb: E.activation(
                xn_f[:], xs, AF.Identity, bias=nb, scale=rs),
                after=[t_load[k], L.get("b_xnw")], hard=[tst])
            rs2, trs2 = rms_stats(p, xn_f[:], sq_scr[:], sta, 12 * s_i + 4,
                                  1e-6, tok_in=[t_xn])
            t_xr = p.do("scalar", lambda E, rs2=rs2: E.activation(
                sq_scr[:], xn_f[:], AF.Identity, scale=rs2),
                after=[L.get("b_sqw")], hard=[trs2])
            L["b_xnw"] = t_xr
            am = sta[:, 12 * s_i + 6:12 * s_i + 7]
            iv = sta[:, 12 * s_i + 7:12 * s_i + 8]
            am127 = sta[:, 12 * s_i + 8:12 * s_i + 9]
            p.do("vector", lambda E, am=am: E.reduce_max(
                am, sq_scr[:], axis=AX.X, apply_absolute_value=True), after=[t_xr])
            p.do("vector", lambda E, am=am: E.tensor_scalar_max(am, am, 1e-5))
            p.do("vector", lambda E, am=am, iv=iv: E.reciprocal(iv, am))
            p.do("vector", lambda E, iv=iv: E.tensor_scalar_mul(iv, iv, 127.0))
            t_a7 = p.do("vector", lambda E, am=am, am127=am127: E.tensor_scalar_mul(
                am127, am, 1.0 / 127.0))
            t_z = p.do("scalar", lambda E, iv=iv: E.activation(
                xn_f[:], sq_scr[:], AF.Identity, scale=iv),
                after=[L.get("b_xnw2")], hard=[t_a7])
            L["b_sqw"] = t_z
            p.do("vector", lambda E: E.tensor_copy(com.i32scr[:, 0:1024], xn_f[:]),
                 after=[t_z, L.get("com_i32")])
            p.do("vector", lambda E: E.tensor_sub(com.rndB[:], xn_f[:],
                                                  com.i32scr[:, 0:1024]))
            p.do("vector", lambda E: E.scalar_tensor_tensor(
                com.rndA[:], com.rndB[:], 0.5, com.i32scr[:, 0:1024],
                ALU.is_gt, ALU.add))
            t_r = p.do("vector", lambda E: E.scalar_tensor_tensor(
                com.rndB[:], com.rndB[:], -0.5, com.rndA[:], ALU.is_ge, ALU.add))
            L["com_i32"] = t_r
            t_q = p.do("vector", lambda E, am127=am127: E.tensor_scalar(
                xn_f[:], com.rndB[:], -1.0, am127, ALU.add, ALU.mult),
                after=[L.get("b_xnw")])
            L["b_xnw2"] = t_q
            L["b_stat%d_%d" % (bi, s_i)] = t_q
            tk_xqT = []
            t_tr = None
            for c in range(8):
                pt = psC[c % 2]
                t_tr = p.do("tensor", lambda E, c=c, pt=pt: E.transpose(
                    pt[:], xn_f[:, 128 * c:128 * (c + 1)], identf[:]),
                    after=[t_q, tok_identf, L.get("war_b_psC%d" % (c % 2))])
                t_ev = p.do("vector", lambda E, c=c, pt=pt: E.tensor_scalar_add(
                    xqT[:, c, 0:128], pt[:], 0.0), after=[t_tr, L.get("b_xqTw")])
                L["war_b_psC%d" % (c % 2)] = t_ev
                tk_xqT.append(t_ev)
            L["b_xnw"] = t_tr
            t_pr = {}
            for nm, dst in (("q", qtok), ("k", ktok), ("v", vtok)):
                for half in range(2):
                    pg = psS[half]
                    mm = None
                    for c in range(8):
                        mm = p.do("tensor", lambda E, nm=nm, half=half, c=c, pg=pg: E.matmul(
                            pg[:, 0:512], xqT[:, c, 0:128],
                            terns[nm][:, c, 512 * half:512 * (half + 1)],
                            start=(c == 0), stop=(c == 7)),
                            after=tk_xqT + [L.get("war_b_psS%d" % half)] +
                                  ([tern_t[nm][-1]] if k == 0 and s_i == 0 else []))
                    t_pr[(nm, half)] = p.do("scalar", lambda E, nm=nm, half=half, pg=pg, dst=dst: E.activation(
                        dst[:, 8 * half:8 * (half + 1), :].rearrange("p a b -> p (a b)"),
                        pg[:, 0:512], AF.Identity, scale=mws[nm][:, 0:1]),
                        after=[mm, L.get("b_%sw" % nm)], hard=[tern_t[nm][0]])
                    L["war_b_psS%d" % half] = t_pr[(nm, half)]
            L["b_xqTw"] = mm
            t_sc = []
            for h in range(16):
                t_m = p.do("gpsimd", lambda E, h=h: E.tensor_tensor(
                    prodA[:, h % 2, :, :],
                    qtok[:, h, :].unsqueeze(1).broadcast_to([128, 16, 64]),
                    ktok[:], ALU.mult),
                    after=[t_pr[("q", h // 8)], t_pr[("k", 0)], t_pr[("k", 1)],
                           L.get("b_prA%d" % (h % 2))])
                t_s = p.do("vector", lambda E, h=h: E.reduce_sum(
                    scb[:, h, :], prodA[:, h % 2, :, :], axis=AX.X), after=[t_m])
                L["b_prA%d" % (h % 2)] = t_s
                t_sc.append(t_s)
            t_mx = p.do("vector", lambda E: E.reduce_max(
                smx[:], scb[:], axis=AX.X), after=t_sc)
            t_sb = p.do("vector", lambda E: E.tensor_sub(
                scb[:], scb[:], smx[:].unsqueeze(2).broadcast_to([128, 16, 16])))
            t_ex = p.do("scalar", lambda E: E.activation(
                scb[:].rearrange("p a b -> p (a b)"),
                scb[:].rearrange("p a b -> p (a b)"), AF.Exp, scale=0.125),
                after=[t_sb, L.get("b_scw")], hard=[t_sb])
            t_sm = p.do("vector", lambda E: E.reduce_sum(
                smx[:], scb[:], axis=AX.X), after=[t_ex])
            t_rc = p.do("vector", lambda E: E.reciprocal(smx[:], smx[:]))
            t_at = p.do("vector", lambda E: E.tensor_tensor(
                attnb[:], scb[:], smx[:].unsqueeze(2).broadcast_to([128, 16, 16]),
                ALU.mult), hard=[t_rc])
            L["b_scw"] = t_at
            vview = vtok[:].transpose([0, 2, 1])
            t_cx = []
            for h in range(16):
                t_m = p.do("gpsimd", lambda E, h=h: E.tensor_tensor(
                    prodB[:, h % 2, :, :],
                    attnb[:, h, :].unsqueeze(1).broadcast_to([128, 64, 16]),
                    vview, ALU.mult),
                    after=[t_at, t_pr[("v", 0)], t_pr[("v", 1)],
                           L.get("b_prB%d" % (h % 2))])
                t_c = p.do("vector", lambda E, h=h, s_i=s_i: E.reduce_sum(
                    ctxs[:, s_i % 2, 64 * h:64 * (h + 1)], prodB[:, h % 2, :, :],
                    axis=AX.X),
                    after=[t_m, L.get("b_ctxs%d" % (s_i % 2))])
                L["b_prB%d" % (h % 2)] = t_c
                t_cx.append(t_c)
            g = NST * k + s_i
            t_cw = p.dma("gpsimd", ctx_dram[128 * g:128 * (g + 1), :],
                         ctxs[:, s_i % 2, :], sig="dc%d" % (s_i % 2),
                         after=t_cx + [L.get("b_cwg%d" % (s_i % 2))])
            ctxw[g] = t_cw
            L["b_cwg%d" % (s_i % 2)] = t_cw
            L["b_ctxs%d" % (s_i % 2)] = t_cw
        L["b_x%d" % bi] = t_q

    for k in range(NSS):
        emit_sweep1(k)

    ctxl = sb(stk, nc, "b_ctxl", [128, 1024], F32)
    ctxq_f = sb(stk, nc, "b_cq", [128, 1024], F32)
    ctxqT = sb(stk, nc, "b_cqT", [128, 8, 128], BF16)
    out_sb = [sb(stk, nc, "b_out%d" % i, [128, NST, 1024], F32) for i in range(2)]

    x2_t = [None] * NSS

    def emit_sweep2(k):
        bi = k % 2
        sta, osb = stat[bi], out_sb[bi]
        b_i = k // SS_PER_SEQ
        gate = L.get("b_x2f%d" % bi) if k >= 2 else L.get("b_x%d" % bi)
        x2_t[k] = p.dma("sync", x_sb[bi][:], x_t[k], stream=bi, after=[gate])
        t_out = []
        for s_i in range(NST):
            a = 4 * (k % SS_PER_SEQ) + s_i
            base = 2048 * b_i
            src = ctx_dram[base:base + 2048, 64 * a:64 * (a + 1)]
            src = src.rearrange("(r j) d -> r j d", r=128)
            t_cl = p.dma("sync", ctxl[:].rearrange("p (j d) -> p j d", d=64), src,
                         sig="dx0",
                         after=[L.get("b_ctxl"),
                                ("dc0", p.cnt["dc0"]), ("dc1", p.cnt["dc1"])])
            rs2, trs2 = rms_stats(p, ctxl[:], sq_scr[:], sta, 12 * s_i, 1e-6,
                                  tok_in=[t_cl])
            t_xr = p.do("scalar", lambda E, rs2=rs2: E.activation(
                sq_scr[:], ctxl[:], AF.Identity, scale=rs2),
                after=[t_cl, L.get("b_sqw")], hard=[trs2])
            am = sta[:, 12 * s_i + 4:12 * s_i + 5]
            iv = sta[:, 12 * s_i + 5:12 * s_i + 6]
            osc = sta[:, 12 * s_i + 6:12 * s_i + 7]
            p.do("vector", lambda E, am=am: E.reduce_max(
                am, sq_scr[:], axis=AX.X, apply_absolute_value=True), after=[t_xr])
            p.do("vector", lambda E, am=am: E.tensor_scalar_max(am, am, 1e-5))
            p.do("vector", lambda E, am=am, iv=iv: E.reciprocal(iv, am))
            t_iv = p.do("vector", lambda E, iv=iv: E.tensor_scalar_mul(iv, iv, 127.0))
            t_os = p.do("vector", lambda E, am=am, osc=osc: E.scalar_tensor_tensor(
                osc, am, 1.0 / 127.0, mws["o"][:, 0:1], ALU.mult, ALU.mult),
                hard=[tern_t["o"][0]])
            t_z = p.do("scalar", lambda E, iv=iv: E.activation(
                ctxq_f[:], sq_scr[:], AF.Identity, scale=iv),
                after=[L.get("b_cqw")], hard=[t_iv])
            L["b_sqw"] = t_z
            L["b_ctxl"] = t_xr
            p.do("vector", lambda E: E.tensor_copy(com.i32scr[:, 0:1024], ctxq_f[:]),
                 after=[t_z, L.get("com_i32")])
            p.do("vector", lambda E: E.tensor_sub(com.rndB[:], ctxq_f[:],
                                                  com.i32scr[:, 0:1024]))
            p.do("vector", lambda E: E.scalar_tensor_tensor(
                com.rndA[:], com.rndB[:], 0.5, com.i32scr[:, 0:1024],
                ALU.is_gt, ALU.add))
            t_r = p.do("vector", lambda E: E.scalar_tensor_tensor(
                com.rndB[:], com.rndB[:], -0.5, com.rndA[:], ALU.is_ge, ALU.add))
            L["com_i32"] = t_r
            t_q = p.do("vector", lambda E: E.tensor_scalar_add(
                ctxq_f[:], com.rndB[:], -1.0), after=[L.get("b_cqw2")])
            L["b_cqw"] = t_q
            tk_cT = []
            for c in range(8):
                pt = psC[c % 2]
                t_tr = p.do("tensor", lambda E, c=c, pt=pt: E.transpose(
                    pt[:], ctxq_f[:, 128 * c:128 * (c + 1)], identf[:]),
                    after=[t_q, tok_identf, L.get("war_b_psC%d" % (c % 2))])
                t_ev = p.do("vector", lambda E, c=c, pt=pt: E.tensor_scalar_add(
                    ctxqT[:, c, :], pt[:], 0.0), after=[t_tr, L.get("b_cqTw")])
                L["war_b_psC%d" % (c % 2)] = t_ev
                tk_cT.append(t_ev)
            L["b_cqw2"] = t_tr
            t_ep = []
            for half in range(2):
                pg = psS[half]
                mm = None
                for c in range(8):
                    mm = p.do("tensor", lambda E, c=c, half=half, pg=pg: E.matmul(
                        pg[:, 0:512], ctxqT[:, c, :],
                        terns["o"][:, c, 512 * half:512 * (half + 1)],
                        start=(c == 0), stop=(c == 7)),
                        after=tk_cT + [L.get("war_b_psS%d" % half)])
                t_e = p.do("scalar", lambda E, s_i=s_i, half=half, pg=pg, osc=osc: E.activation(
                    osb[:, s_i, 512 * half:512 * (half + 1)], pg[:, 0:512],
                    AF.Identity, scale=osc),
                    after=[mm, L.get("b_outw%d" % bi)], hard=[t_os])
                L["war_b_psS%d" % half] = t_e
                L["b_cqTw"] = mm
                t_ep.append(t_e)
            rs3, nb3, tst3 = ln_stats(p, osb[:, s_i, :], sq_scr[:], sta,
                                      12 * s_i + 7, 1e-5, tok_in=t_ep)
            t_ln = p.do("vector", lambda E, s_i=s_i, rs3=rs3, nb3=nb3, osb=osb: E.tensor_scalar(
                osb[:, s_i, :], osb[:, s_i, :], rs3, nb3, ALU.mult, ALU.add),
                hard=[tst3])
            t_fin = p.do("vector", lambda E, s_i=s_i, osb=osb, bi=bi: E.tensor_add(
                osb[:, s_i, :], osb[:, s_i, :], x_sb[bi][:, s_i, :]),
                after=[x2_t[k]])
            t_out.append(t_fin)
        L["b_x2f%d" % bi] = t_out[-1]
        t_str = p.dma("gpsimd", xo_t[k], osb[:], stream=bi,
                      after=t_out + [L.get("b_outw%d" % bi)])
        L["b_outw%d" % bi] = t_str
        tok_store.append(t_str)

    for k in range(NSS):
        emit_sweep2(k)
    return tok_store



def _prep_inputs(inputs):
    f32 = np.float32

    def t(a):
        return np.ascontiguousarray(np.asarray(a, f32).T)

    def cols(a, nch):
        return np.ascontiguousarray(np.asarray(a, f32).reshape(nch, 128).T)

    g = {}
    g["x"] = np.ascontiguousarray(np.asarray(inputs["L0_attn_tokens"], f32).reshape(B, S, D))
    g["a_w_in_t"] = t(inputs["L0_attn_W_in"])
    g["a_w_out_t"] = t(inputs["L0_attn_W_out"])
    g["a_w_skip_t"] = t(inputs["L0_attn_W_skip"])
    g["a_norm_w"] = cols(inputs["L0_attn_norm_w"], 8)
    g["a_b_in"] = cols(inputs["L0_attn_b_in"], 2)
    g["a_b_out"] = np.ascontiguousarray(np.asarray(inputs["L0_attn_b_out"], f32)).reshape(1, 1024)
    g["a_b_skip"] = np.ascontiguousarray(np.asarray(inputs["L0_attn_b_skip"], f32)).reshape(1, 1024)
    g["a_log_decay"] = cols(inputs["L0_attn_log_decay"], 2)
    g["c_w1t"] = t(inputs["L0_ffn_w1"])
    g["c_w2t"] = t(inputs["L0_ffn_w2"])
    cw = np.asarray(inputs["L0_ffn_conv_w"], f32)[:, 0, :]
    g["c_cwd"] = np.ascontiguousarray(cw.reshape(32, 128, 3).transpose(1, 2, 0))
    g["c_cbd"] = cols(inputs["L0_ffn_conv_b"], 32)
    g["b_wqt"] = t(inputs["L1_attn_wq"])
    g["b_wkt"] = t(inputs["L1_attn_wk"])
    g["b_wvt"] = t(inputs["L1_attn_wv"])
    g["b_wot"] = t(inputs["L1_attn_wo"])
    g["f_u1d"] = np.ascontiguousarray(np.asarray(inputs["L1_ffn_U1"], f32))
    g["f_v1d"] = np.ascontiguousarray(np.asarray(inputs["L1_ffn_V1"], f32))
    g["f_u2d"] = np.ascontiguousarray(np.asarray(inputs["L1_ffn_U2"], f32))
    g["f_v2d"] = np.ascontiguousarray(np.asarray(inputs["L1_ffn_V2"], f32))
    g["f_b1d"] = cols(inputs["L1_ffn_b1"], 32)
    g["f_b2d"] = np.ascontiguousarray(np.asarray(inputs["L1_ffn_b2"], f32)).reshape(1, 1024)
    return g


def _l0_attn_dram(nc):
    d = {}
    d["w_in_t"] = nc.dram_tensor("a_w_in_t", [1024, 256], F32, kind="ExternalInput").ap()
    d["w_out_t"] = nc.dram_tensor("a_w_out_t", [256, 1024], F32, kind="ExternalInput").ap()
    d["w_skip_t"] = nc.dram_tensor("a_w_skip_t", [1024, 1024], F32, kind="ExternalInput").ap()
    d["norm_w"] = nc.dram_tensor("a_norm_w", [128, 8], F32, kind="ExternalInput").ap()
    d["b_in"] = nc.dram_tensor("a_b_in", [128, 2], F32, kind="ExternalInput").ap()
    d["b_out"] = nc.dram_tensor("a_b_out", [1, 1024], F32, kind="ExternalInput").ap()
    d["b_skip"] = nc.dram_tensor("a_b_skip", [1, 1024], F32, kind="ExternalInput").ap()
    d["log_decay"] = nc.dram_tensor("a_log_decay", [128, 2], F32, kind="ExternalInput").ap()
    return d


def _l0_ffn_dram(nc):
    d = {}
    d["w1_t"] = nc.dram_tensor("c_w1t", [1024, 4096], F32, kind="ExternalInput").ap()
    d["w2_t"] = nc.dram_tensor("c_w2t", [4096, 1024], F32, kind="ExternalInput").ap()
    d["conv_w"] = nc.dram_tensor("c_cwd", [128, 3, 32], F32, kind="ExternalInput").ap()
    d["conv_b"] = nc.dram_tensor("c_cbd", [128, 32], F32, kind="ExternalInput").ap()
    return d


def _l1_attn_dram(nc):
    d = {}
    for nm in ("wq", "wk", "wv", "wo"):
        d[nm + "_t"] = nc.dram_tensor("b_" + nm + "t", [1024, 1024], F32,
                                      kind="ExternalInput").ap()
    return d


def _l1_ffn_dram(nc):
    d = {}
    d["u1"] = nc.dram_tensor("f_u1d", [1024, 256], F32, kind="ExternalInput").ap()
    d["v1"] = nc.dram_tensor("f_v1d", [256, 4096], F32, kind="ExternalInput").ap()
    d["u2"] = nc.dram_tensor("f_u2d", [4096, 256], F32, kind="ExternalInput").ap()
    d["v2"] = nc.dram_tensor("f_v2d", [256, 1024], F32, kind="ExternalInput").ap()
    d["b1"] = nc.dram_tensor("f_b1d", [128, 32], F32, kind="ExternalInput").ap()
    d["b2"] = nc.dram_tensor("f_b2d", [1, 1024], F32, kind="ExternalInput").ap()
    return d


def _block_prog(which, debug=False):
    nc = bass.Bass("TRN2", target_bir_lowering=False)
    x_in = nc.dram_tensor("x", [TOK, 1024], F32, kind="ExternalInput").ap()
    x_out = nc.dram_tensor("y", [TOK, 1024], F32, kind="ExternalOutput").ap()
    p = Prog(nc)
    with contextlib.ExitStack() as stk:
        com = Common(p, stk)
        if which == "l0a":
            toks = build_l0_attn(p, stk, com, _l0_attn_dram(nc), x_in, x_out)
        elif which == "l0f":
            h2_dram = nc.dram_tensor("c_h2", [32, 128, TOK], BF16, kind="Internal").ap()
            toks = build_l0_ffn(p, stk, com, _l0_ffn_dram(nc), x_in, x_out, h2_dram)
        elif which == "l1a":
            ctx_dram = nc.dram_tensor("b_ctx", [TOK, 1024], F32, kind="Internal").ap()
            toks = build_l1_attn(p, stk, com, _l1_attn_dram(nc), x_in, x_out, ctx_dram)
        elif which == "l1f":
            toks = build_l1_ffn(p, stk, com, _l1_ffn_dram(nc), x_in, x_out)
        p.wait_all("sync", toks + [("dw", p.cnt["dw"])])
        p.build()
    return nc


def build_l0_attn_prog(debug=False):
    return _block_prog("l0a")


def build_l0_ffn_prog(debug=False):
    return _block_prog("l0f")


def build_l1_attn_prog():
    return _block_prog("l1a")


def build_l1_ffn_prog():
    return _block_prog("l1f")


_KEYS = {
    "l0a": ["a_w_in_t", "a_w_out_t", "a_w_skip_t", "a_norm_w", "a_b_in",
            "a_b_out", "a_b_skip", "a_log_decay"],
    "l0f": ["c_w1t", "c_w2t", "c_cwd", "c_cbd"],
    "l1a": ["b_wqt", "b_wkt", "b_wvt", "b_wot"],
    "l1f": ["f_u1d", "f_v1d", "f_u2d", "f_v2d", "f_b1d", "f_b2d"],
}


def kernel(**inputs):
    g = _prep_inputs(inputs)
    xs = [np.ascontiguousarray(g["x"][BPC * ci:BPC * (ci + 1)].reshape(TOK, D))
          for ci in range(NCORE)]
    for which in ("l0a", "l0f", "l1a", "l1f"):
        nc = _block_prog(which)
        in_maps = []
        for ci in range(NCORE):
            m = {k: g[k] for k in _KEYS[which]}
            m["x"] = xs[ci]
            in_maps.append(m)
        res = run_bass_kernel_spmd(nc, in_maps, core_ids=list(range(NCORE)))
        xs = [np.ascontiguousarray(r["y"]) for r in res.results]
    return np.concatenate([x.reshape(BPC, S, D) for x in xs], axis=0).reshape(B, S, D)


# version 04
# speedup vs baseline: 1.0018x; 1.0018x over previous
import contextlib
import numpy as np
import concourse.bass as bass
import concourse.mybir as mybir
from concourse.bass_utils import run_bass_kernel_spmd

F32 = mybir.dt.float32
BF16 = mybir.dt.bfloat16
I32 = mybir.dt.int32
AF = mybir.ActivationFunctionType
ALU = mybir.AluOpType
AX = mybir.AxisListType

NCORE = 8
B, S, D = 16, 2048, 1024
BPC = B // NCORE
TOK = BPC * S
SS = 512
NSS = TOK // SS
NST = SS // 128
SS_PER_SEQ = S // SS


ENG_SEM = {"sync": "dl0", "gpsimd": "g", "tensor": "t", "vector": "v", "scalar": "s"}


class Prog:

    def __init__(self, nc):
        self.nc = nc
        self.ops = {e: [] for e in ("sync", "gpsimd", "tensor", "vector", "scalar")}
        self.cnt = {"dw": 0, "dl0": 0, "dl1": 0, "ds0": 0, "ds1": 0,
                    "dx0": 0, "dx1": 0, "dc0": 0, "dc1": 0, "dbg": 0, "t": 0, "v": 0, "s": 0, "g": 0}
        self.sems = {}
        self.last = {}
        self.prev = {}

    def do(self, eng, build, after=(), hard=(), sig=None, amt=1, chain=True):
        if sig is None:
            sig = ENG_SEM[eng]
        hard = list(hard)
        if chain and eng in ("vector", "scalar", "gpsimd"):
            prev = self.prev.get(eng)
            if prev is not None:
                hard.append(prev)
        self.cnt[sig] += amt
        val = self.cnt[sig]
        if amt == 0:
            val = None
        own = ENG_SEM[eng] if eng not in ("sync", "gpsimd") else "-"
        seen = {}
        for t in list(after):
            if t is None or t[0] == own:
                continue
            if seen.get(t[0], -1) < t[1]:
                seen[t[0]] = t[1]
        for t in list(hard):
            if t is None:
                continue
            if seen.get(t[0], -1) < t[1]:
                seen[t[0]] = t[1]
        waits = sorted(seen.items())

        def run(E, waits=waits, build=build, sig=sig, amt=amt):
            for sname, v in waits:
                E.wait_ge(self.sems[sname], v)
            ins = build(E)
            if amt and ins is not None:
                ins.then_inc(self.sems[sig], amt)
            elif amt:
                E.sem_inc(self.sems[sig], amt)

        self.ops[eng].append(run)
        tok = (sig, val) if val is not None else None
        if eng in ("vector", "scalar", "gpsimd") and val is not None and sig in ("v", "s", "g"):
            self.prev[eng] = tok
        return tok

    def dma(self, eng, out, in_, after=(), stream=0, sig=None, **kw):
        if sig is None:
            sig = ("dl%d" if eng == "sync" else "ds%d") % stream
        return self.do(eng, lambda E: E.dma_start(out=out, in_=in_, **kw),
                       after=after, sig=sig, amt=16)

    def wait_all(self, eng, toks):
        def run(E, toks=[t for t in toks if t is not None]):
            seen = {}
            for sname, v in toks:
                if seen.get(sname, -1) < v:
                    seen[sname] = v
            for sname, v in sorted(seen.items()):
                E.wait_ge(self.sems[sname], v)
        self.ops[eng].append(run)

    def barrier(self):
        snap = [(k, v) for k, v in self.cnt.items() if v > 0]
        for eng in self.ops:
            own = ENG_SEM[eng]
            self.wait_all(eng, [(k, v) for k, v in snap if k != own])

    def build(self):
        nc = self.nc
        with contextlib.ExitStack() as st:
            for name in self.cnt:
                self.sems[name] = st.enter_context(nc.semaphore(name))
            with nc.Block() as block:
                for eng in self.ops:
                    def make(eng):
                        def body(E):
                            for run in self.ops[eng]:
                                run(E)
                        return body
                    getattr(block, eng)(make(eng))


def sb(stk, nc, name, shape, dt):
    return stk.enter_context(nc.sbuf_tensor(name, shape, dt))


def ps(stk, nc, name, shape, dt):
    return stk.enter_context(nc.psum_tensor(name, shape, dt))


class Common:

    def __init__(self, p, stk):
        nc = p.nc
        self.ident = sb(stk, nc, "ident", [128, 128], BF16)
        self.onesrow_f = sb(stk, nc, "onesrow_f", [1, 512], F32)
        self.onesrow_b = sb(stk, nc, "onesrow_b", [1, 128], BF16)
        self.onescol_f = sb(stk, nc, "onescol_f", [128, 1], F32)
        self.rowscratch = sb(stk, nc, "rowscratch", [1, 16], F32)
        self.i32scr = sb(stk, nc, "i32scr", [128, 1024], I32)
        self.rowacc = sb(stk, nc, "rowacc", [128, 2], F32)
        self.rndA = sb(stk, nc, "rndA", [128, 1024], F32)
        self.rndB = sb(stk, nc, "rndB", [128, 1024], F32)
        t0 = p.do("vector", lambda E: E.memset(self.ident[:], 1.0))
        p.do("vector", lambda E: E.memset(self.onescol_f[:], 1.0))
        p.do("vector", lambda E: E.memset(self.onesrow_b[:], 1.0))
        self.tok_ones = p.do("vector", lambda E: E.memset(self.onesrow_f[:], 1.0))
        self.tok_ident = p.do(
            "gpsimd",
            lambda E: E.affine_select(self.ident[:], self.ident[:], [[-1, 128]],
                                      ALU.is_equal, 0.0, base=0, channel_multiplier=1),
            after=[t0])


def ln_stats(p, x_ap, sq_scr, stat, c0, eps, tok_in=(), n=1024.0):
    su, sq, rs, nb = (stat[:, c0 + i:c0 + i + 1] for i in range(4))
    p.do("vector", lambda E: E.reduce_sum(su, x_ap, axis=AX.X), after=tok_in)
    p.do("vector", lambda E: E.scalar_tensor_tensor(
        sq_scr, x_ap, 1.0, x_ap, ALU.mult, ALU.mult, accum_out=sq), after=tok_in)
    p.do("vector", lambda E: E.tensor_scalar_mul(su, su, 1.0 / n))
    p.do("vector", lambda E: E.tensor_scalar_mul(sq, sq, 1.0 / n))
    p.do("vector", lambda E: E.scalar_tensor_tensor(rs, su, 1.0, su, ALU.mult, ALU.mult))
    p.do("vector", lambda E: E.tensor_sub(sq, sq, rs))
    t7 = p.do("vector", lambda E: E.tensor_scalar_add(sq, sq, eps))
    t8 = p.do("scalar", lambda E: E.activation(rs, sq, AF.Sqrt), after=[t7])
    p.do("vector", lambda E: E.reciprocal(rs, rs), after=[t8])
    t10 = p.do("vector", lambda E: E.scalar_tensor_tensor(nb, su, -1.0, rs, ALU.mult, ALU.mult))
    return rs, nb, t10


def rms_stats(p, x_ap, sq_scr, stat, c0, eps, tok_in=(), n=1024.0):
    sq, rs = stat[:, c0:c0 + 1], stat[:, c0 + 1:c0 + 2]
    p.do("vector", lambda E: E.scalar_tensor_tensor(
        sq_scr, x_ap, 1.0, x_ap, ALU.mult, ALU.mult, accum_out=sq), after=tok_in)
    t2 = p.do("vector", lambda E: E.tensor_scalar(sq, sq, 1.0 / n, eps, ALU.mult, ALU.add))
    t3 = p.do("scalar", lambda E: E.activation(rs, sq, AF.Sqrt), after=[t2])
    t4 = p.do("vector", lambda E: E.reciprocal(rs, rs), after=[t3])
    return rs, t4



def build_l0_attn(p, stk, com, dram, x_in, x_out, tok_x_in=None, dbg=None):
    nc = p.nc
    L = p.last

    wstage = sb(stk, nc, "a_wstage", [128, 2, 1024], F32)
    winT = sb(stk, nc, "a_winT", [128, 8, 256], BF16)
    woutT = sb(stk, nc, "a_woutT", [128, 2, 1024], BF16)
    wskipT = sb(stk, nc, "a_wskipT", [128, 8, 1024], BF16)
    normw = sb(stk, nc, "a_normw", [128, 8], F32)
    b_in_c = sb(stk, nc, "a_binc", [128, 2], F32)
    brow = sb(stk, nc, "a_brow", [1, 2048], F32)
    bias_rep = sb(stk, nc, "a_biasrep", [128, 1024], F32)
    dec_c = sb(stk, nc, "a_decc", [128, 2], F32)
    dec_s = sb(stk, nc, "a_decs", [128, 2], F32)
    decseg = [sb(stk, nc, "a_decseg%d" % i, [128, 256], F32) for i in range(2)]

    psU = [ps(stk, nc, "a_psU%d" % i, [128, 512], F32) for i in range(2)]
    psT = [ps(stk, nc, "a_psT%d" % i, [128, 128], BF16) for i in range(2)]
    psY = [ps(stk, nc, "a_psY%d" % i, [128, 512], F32) for i in range(4)]

    t_nw = p.dma("sync", normw[:], dram["norm_w"][:], sig="dw")
    t_bi = p.dma("sync", b_in_c[:], dram["b_in"][:], sig="dw")
    t_bo = p.dma("sync", brow[0:1, 0:1024], dram["b_out"][:], sig="dw")
    t_bs = p.dma("sync", brow[0:1, 1024:2048], dram["b_skip"][:], sig="dw")
    t_ld = p.dma("sync", dec_c[:], dram["log_decay"][:], sig="dw")
    dwtot = ("dw", p.cnt["dw"])
    t_nw = t_bi = t_bo = t_bs = t_ld = dwtot

    win_r = dram["w_in_t"].rearrange("(c p) n -> c p n", p=128)
    wsk_r = dram["w_skip_t"].rearrange("(c p) e -> c p e", p=128)
    wout_r = dram["w_out_t"].rearrange("(c p) e -> c p e", p=128)
    tk_win, tk_wsk, tk_wout = [], [], []
    for c in range(8):
        sl = c % 2
        t_w = p.dma("sync", wstage[:, sl, 0:256], win_r[c], stream=sl,
                    after=[tk_win[-2] if c >= 2 else None])
        tk_win.append(p.do("vector", lambda E, c=c, sl=sl: E.tensor_scalar_mul(
            winT[:, c, :], wstage[:, sl, 0:256], normw[:, c:c + 1]), after=[dwtot, t_w]))
    for c in range(8):
        sl = c % 2
        t_w = p.dma("sync", wstage[:, sl, :], wsk_r[c], stream=sl,
                    after=[tk_wsk[-2] if c >= 2 else tk_win[6 + c]])
        tk_wsk.append(p.do("vector", lambda E, c=c, sl=sl: E.tensor_scalar_mul(
            wskipT[:, c, :], wstage[:, sl, :], normw[:, c:c + 1]), after=[t_w], hard=[dwtot]))
    for c in range(2):
        sl = c % 2
        t_w = p.dma("sync", wstage[:, sl, :], wout_r[c], stream=sl,
                    after=[tk_wsk[6 + c]])
        tk_wout.append(p.do("vector", lambda E, c=c, sl=sl: E.tensor_scalar_add(
            woutT[:, c, :], wstage[:, sl, :], 0.0), after=[t_w]))

    t_ba = p.do("vector", lambda E: E.tensor_add(
        brow[0:1, 0:1024], brow[0:1, 0:1024], brow[0:1, 1024:2048]), after=[dwtot])
    tb0 = p.do("tensor", lambda E: E.matmul(
        psY[0][:], com.onesrow_f[0:1, 0:128], brow[0:1, 0:512], start=True, stop=True),
        after=[t_ba, com.tok_ones])
    tb1 = p.do("tensor", lambda E: E.matmul(
        psY[1][:], com.onesrow_f[0:1, 0:128], brow[0:1, 512:1024], start=True, stop=True),
        after=[t_ba, com.tok_ones])
    tb_e0 = p.do("vector", lambda E: E.tensor_copy(bias_rep[:, 0:512], psY[0][:]), after=[tb0])
    tb_e1 = p.do("vector", lambda E: E.tensor_copy(bias_rep[:, 512:1024], psY[1][:]), after=[tb1])
    L["a_psY0"] = tb_e0
    L["a_psY1"] = tb_e1
    t_sg = p.do("scalar", lambda E: E.activation(dec_s[:], dec_c[:], AF.Sigmoid), after=[dwtot])
    t_ds = []
    for i in range(2):
        t_ms = p.do("vector", lambda E, i=i: E.memset(decseg[i][:], 1.0))
        t_ds.append(p.do("vector", lambda E, i=i: E.tensor_scalar_mul(
            decseg[i][:], decseg[i][:], dec_s[:, i:i + 1]), after=[t_sg], hard=[t_sg, t_ms]))


    x_sb = [sb(stk, nc, "a_x%d" % i, [128, NST, 1024], F32) for i in range(2)]
    tok_sb = sb(stk, nc, "a_tok", [128, NST, 1024], F32)
    xr_bf = sb(stk, nc, "a_xr", [128, NST, 1024], BF16)
    xrT = [sb(stk, nc, "a_xrT%d" % i, [128, 8, 512], BF16) for i in range(2)]
    useg = [[sb(stk, nc, "a_useg%d_%d" % (i, j), [128, 256], F32) for j in range(4)]
            for i in range(2)]
    stseg = [[sb(stk, nc, "a_stseg%d_%d" % (i, j), [128, 256], F32) for j in range(4)]
             for i in range(2)]
    stbf = [[sb(stk, nc, "a_stbf%d_%d" % (i, j), [128, 256], BF16) for j in range(4)]
            for i in range(2)]
    out_sb = [sb(stk, nc, "a_out%d" % i, [128, NST, 1024], F32) for i in range(2)]
    sq_scr = sb(stk, nc, "a_sq", [128, 1024], F32)
    stat = [sb(stk, nc, "a_stat%d" % i, [128, 48], F32) for i in range(2)]
    carry = [sb(stk, nc, "a_carry%d" % i, [128, 1], F32) for i in range(2)]

    x_t = x_in.rearrange("(k s p) d -> k p s d", p=128, s=NST)
    xo_t = x_out.rearrange("(k s p) d -> k p s d", p=128, s=NST)

    tok_store = []
    t_load = [None] * NSS
    t_load[0] = p.dma("sync", x_sb[0][:], x_t[0], stream=0,
                      after=[tok_x_in, tk_wout[0]])

    def emit_ss(k):
        bi = k % 2
        xsb, xrt, osb, sta = x_sb[bi], xrT[bi], out_sb[bi], stat[bi]
        use, sts, stb = useg[bi], stseg[bi], stbf[bi]
        if k + 1 < NSS:
            gate = L.get("a_x%d" % ((k + 1) % 2)) if k >= 1 else tk_wout[1]
            t_load[k + 1] = p.dma("sync", x_sb[(k + 1) % 2][:], x_t[k + 1],
                                  stream=(k + 1) % 2, after=[gate, tok_x_in])
        tk_xrT = []
        t_tr = None
        for s_i in range(NST):
            xs = xsb[:, s_i, :]
            rs, nb, tst = ln_stats(p, xs, sq_scr[:], sta, 12 * s_i, 1e-5,
                                   tok_in=[t_load[k], L.get("a_stat%d_%d" % (bi, s_i))])
            t_tok = p.do("scalar", lambda E, xs=xs, s_i=s_i, rs=rs, nb=nb: E.activation(
                tok_sb[:, s_i, :], xs, AF.Identity, bias=nb, scale=rs),
                after=[t_load[k], L.get("a_tok")], hard=[tst])
            rs2, trs2 = rms_stats(p, tok_sb[:, s_i, :], sq_scr[:], sta, 12 * s_i + 4,
                                  1e-4, tok_in=[t_tok])
            t_cast = p.do("scalar", lambda E, s_i=s_i, rs2=rs2: E.activation(
                xr_bf[:, s_i, :], tok_sb[:, s_i, :], AF.Identity, scale=rs2),
                after=[L.get("a_xr")], hard=[trs2])
            L["a_stat%d_%d" % (bi, s_i)] = t_cast
            for c in range(8):
                pt = psT[c % 2]
                t_tr = p.do("tensor", lambda E, c=c, s_i=s_i, pt=pt: E.transpose(
                    pt[:], xr_bf[:, s_i, 128 * c:128 * (c + 1)], com.ident[:]),
                    after=[t_cast, com.tok_ident, L.get("a_psT%d" % (c % 2))])
                t_ev = p.do("vector", lambda E, c=c, s_i=s_i, pt=pt, xrt=xrt: E.tensor_copy(
                    xrt[:, c, 128 * s_i:128 * (s_i + 1)], pt[:]),
                    after=[t_tr, L.get("a_xrT%d" % bi)])
                L["a_psT%d" % (c % 2)] = t_ev
                tk_xrT.append(t_ev)
        L["a_xr"] = t_tr

        t_uev = {}
        for nch in range(2):
            mm = None
            for c in range(8):
                mm = p.do("tensor", lambda E, nch=nch, c=c, xrt=xrt: E.matmul(
                    psU[nch][:], winT[:, c, 128 * nch:128 * (nch + 1)], xrt[:, c, :],
                    start=(c == 0), stop=(c == 7)),
                    after=tk_xrT + [L.get("a_psU%d" % nch)] + tk_win)
            for sg in range(2):
                t_uev[(nch, sg)] = p.do("scalar", lambda E, nch=nch, sg=sg, use=use: E.activation(
                    use[2 * nch + sg][:], psU[nch][:, 256 * sg:256 * (sg + 1)],
                    AF.Identity, bias=b_in_c[:, nch:nch + 1], scale=1.0),
                    after=[mm, dwtot, L.get("a_useg%d_%d" % (bi, 2 * nch + sg))])
            L["a_psU%d" % nch] = t_uev[(nch, 1)]

        seq_start = (k % SS_PER_SEQ) == 0
        t_sc = {}
        for nch in range(2):
            for sg in range(2):
                j = 2 * nch + sg
                if sg == 0 and seq_start:
                    init, hard_init = 0.0, []
                else:
                    init, hard_init = carry[nch][:, 0:1], [L.get("a_carrywr%d" % nch)]
                t_sc[j] = p.do("vector", lambda E, j=j, nch=nch, init=init, sts=sts, use=use: (
                    E.tensor_tensor_scan(sts[j][:], decseg[nch][:],
                                         use[j][:], init, ALU.mult, ALU.add)),
                    after=[t_uev[(nch, sg)], t_ds[nch]], hard=hard_init)
                L["a_useg%d_%d" % (bi, j)] = t_sc[j]
                L["a_carrywr%d" % nch] = p.do(
                    "vector", lambda E, j=j, nch=nch, sts=sts: E.tensor_copy(
                        carry[nch][:], sts[j][:, 255:256]))
        t_st = {}
        for j in range(4):
            t_st[j] = p.do("vector", lambda E, j=j, stb=stb, sts=sts: E.tensor_scalar_add(
                stb[j][:], sts[j][:], 0.0), after=[t_sc[j], L.get("a_ymm%d" % bi)])

        t_out = []
        last_mm = None
        t_inner = None
        for s_i in range(NST):
            t_ep = []
            for half in range(2):
                pyk = 2 * (s_i % 2) + half
                pY = psY[pyk]
                mm = None
                for nch in range(2):
                    j = 2 * nch + (s_i // 2)
                    off = 128 * (s_i % 2)
                    mm = p.do("tensor", lambda E, j=j, nch=nch, off=off, half=half, pY=pY, stb=stb: E.matmul(
                        pY[:], stb[j][:, off:off + 128],
                        woutT[:, nch, 512 * half:512 * (half + 1)],
                        start=(nch == 0), stop=False),
                        after=[t_st[j], L.get("a_psY%d" % pyk)] + tk_wout)
                for c in range(8):
                    mm = p.do("tensor", lambda E, c=c, s_i=s_i, half=half, pY=pY, xrt=xrt: E.matmul(
                        pY[:], xrt[:, c, 128 * s_i:128 * (s_i + 1)],
                        wskipT[:, c, 512 * half:512 * (half + 1)],
                        start=False, stop=(c == 7)),
                        after=tk_xrT + tk_wsk)
                last_mm = mm
                t_e = p.do("vector", lambda E, s_i=s_i, half=half, pY=pY, osb=osb: E.tensor_add(
                    osb[:, s_i, 512 * half:512 * (half + 1)], pY[:],
                    bias_rep[:, 512 * half:512 * (half + 1)]),
                    after=[mm, tb_e0, tb_e1, L.get("a_out%d" % bi)])
                L["a_psY%d" % pyk] = t_e
                t_ep.append(t_e)
            t_inner = p.do("vector", lambda E, s_i=s_i, osb=osb: E.tensor_add(
                osb[:, s_i, :], osb[:, s_i, :], tok_sb[:, s_i, :]),
                after=t_ep)
            rs3, nb3, tst3 = ln_stats(p, osb[:, s_i, :], sq_scr[:], sta,
                                      12 * s_i + 6, 1e-5, tok_in=[t_inner])
            t_ln = p.do("vector", lambda E, s_i=s_i, rs3=rs3, nb3=nb3, osb=osb: E.tensor_scalar(
                osb[:, s_i, :], osb[:, s_i, :], rs3, nb3, ALU.mult, ALU.add),
                hard=[tst3])
            t_fin = p.do("vector", lambda E, s_i=s_i, osb=osb, xsb=xsb: E.tensor_add(
                osb[:, s_i, :], osb[:, s_i, :], xsb[:, s_i, :]))
            t_out.append(t_fin)
        L["a_ymm%d" % bi] = last_mm
        L["a_xrT%d" % bi] = last_mm
        L["a_tok"] = t_inner
        L["a_x%d" % bi] = t_out[-1]
        if k == 0 and dbg is not None:
            p.dma("sync", dbg["tok"][:], tok_sb[:], sig="dbg", after=[t_inner])
            p.dma("sync", dbg["xr"][:], xr_bf[:], sig="dbg", after=[L["a_xr"]])
            p.dma("sync", dbg["xrT"][:], xrt[:], sig="dbg", after=[L.get("a_psU1")])
            p.dma("sync", dbg["useg"][:], use[0][:], sig="dbg", after=[t_sc[0]])
            p.dma("sync", dbg["stseg"][:], sts[0][:], sig="dbg", after=[t_st[0]])
            p.dma("sync", dbg["dec"][:], decseg[0][:], sig="dbg", after=[t_ds[0]])
            p.dma("sync", dbg["decs"][:], dec_s[:], sig="dbg", after=[t_ds[0]])
        t_str = p.dma("gpsimd", xo_t[k], osb[:], stream=bi, after=t_out)
        L["a_out%d" % bi] = t_str
        tok_store.append(t_str)

    for k in range(NSS):
        emit_ss(k)
    return tok_store



def build_l1_ffn(p, stk, com, dram, x_in, x_out, tok_x_in=None):
    nc = p.nc
    L = p.last

    stage = sb(stk, nc, "f_stage", [128, 8192], F32)
    u1b = sb(stk, nc, "f_u1", [128, 8, 256], BF16)
    v1b = sb(stk, nc, "f_v1", [128, 2, 4096], BF16)
    u2b = sb(stk, nc, "f_u2", [128, 32, 256], BF16)
    v2b = sb(stk, nc, "f_v2", [128, 2, 1024], BF16)
    b1c = sb(stk, nc, "f_b1c", [128, 32], F32)
    brow = sb(stk, nc, "f_brow", [1, 1024], F32)
    b2rep = sb(stk, nc, "f_b2rep", [128, 1024], F32)

    psP = ps(stk, nc, "f_psP", [128, 512], F32)
    psH = [ps(stk, nc, "f_psH%d" % i, [128, 512], F32) for i in range(2)]
    psQ = ps(stk, nc, "f_psQ", [128, 512], F32)
    psO = [ps(stk, nc, "f_psO%d" % i, [128, 512], F32) for i in range(2)]
    psT = [ps(stk, nc, "f_psT%d" % i, [128, 128], BF16) for i in range(2)]

    t_b1 = p.dma("sync", b1c[:], dram["b1"][:], sig="dw")
    t_b2 = p.dma("sync", brow[:], dram["b2"][:], sig="dw")
    dwtot = ("dw", p.cnt["dw"])

    def stg(nelem):
        return stage[:, 0:nelem]

    t_w = p.dma("sync", stg(2048).rearrange("p (c n) -> p c n", n=256),
                dram["u1"].rearrange("(c p) n -> p c n", p=128))
    c_u1 = p.do("vector", lambda E: E.tensor_scalar_add(
        u1b[:], stg(2048).rearrange("p (c n) -> p c n", n=256), 0.0), after=[t_w])
    t_w = p.dma("sync", stg(8192).rearrange("p (c f) -> p c f", f=4096),
                dram["v1"].rearrange("(c p) f -> p c f", p=128), after=[c_u1])
    c_v1 = p.do("vector", lambda E: E.tensor_scalar_add(
        v1b[:], stg(8192).rearrange("p (c f) -> p c f", f=4096), 0.0), after=[t_w])
    t_w = p.dma("sync", stg(8192).rearrange("p (c n) -> p c n", n=256),
                dram["u2"].rearrange("(c p) n -> p c n", p=128), after=[c_v1])
    c_u2 = p.do("vector", lambda E: E.tensor_scalar_add(
        u2b[:], stg(8192).rearrange("p (c n) -> p c n", n=256), 0.0), after=[t_w])
    t_w = p.dma("sync", stg(2048).rearrange("p (c f) -> p c f", f=1024),
                dram["v2"].rearrange("(c p) f -> p c f", p=128), after=[c_u2])
    c_v2 = p.do("vector", lambda E: E.tensor_scalar_add(
        v2b[:], stg(2048).rearrange("p (c f) -> p c f", f=1024), 0.0), after=[t_w])
    tb0 = p.do("tensor", lambda E: E.matmul(
        psO[0][:], com.onesrow_f[0:1, 0:128], brow[0:1, 0:512], start=True, stop=True),
        after=[dwtot, com.tok_ones])
    tb1 = p.do("tensor", lambda E: E.matmul(
        psO[1][:], com.onesrow_f[0:1, 0:128], brow[0:1, 512:1024], start=True, stop=True),
        after=[dwtot, com.tok_ones])
    tb_e0 = p.do("vector", lambda E: E.tensor_copy(b2rep[:, 0:512], psO[0][:]), after=[tb0])
    tb_e1 = p.do("vector", lambda E: E.tensor_copy(b2rep[:, 512:1024], psO[1][:]), after=[tb1])
    L["f_psO0"] = tb_e0
    L["f_psO1"] = tb_e1

    x_sb = [sb(stk, nc, "f_x%d" % i, [128, NST, 1024], F32) for i in range(2)]
    xn_bf = sb(stk, nc, "f_xn", [128, NST, 1024], BF16)
    xnT1 = sb(stk, nc, "f_xnT", [128, 8, 512], BF16)
    xnT = [xnT1, xnT1]
    p_sb = sb(stk, nc, "f_p", [128, NST, 256], BF16)
    p_fm = [sb(stk, nc, "f_pfm%d" % i, [128, 2, 512], BF16) for i in range(2)]
    h_sb = sb(stk, nc, "f_h", [128, 32, 512], BF16)
    q_fm = sb(stk, nc, "f_qfm", [128, 2, 512], BF16)
    out_sb1 = sb(stk, nc, "f_out", [128, NST, 1024], F32)
    out_sb = [out_sb1, out_sb1]
    sq_scr = sb(stk, nc, "f_sq", [128, 1024], F32)
    stat = [sb(stk, nc, "f_stat%d" % i, [128, 48], F32) for i in range(2)]

    x_t = x_in.rearrange("(k s p) d -> k p s d", p=128, s=NST)
    xo_t = x_out.rearrange("(k s p) d -> k p s d", p=128, s=NST)

    tok_store = []
    t_load = [None] * NSS
    t_load[0] = p.dma("sync", x_sb[0][:], x_t[0], stream=0, after=[tok_x_in, c_v2])

    def emit_ss(k):
        bi = k % 2
        xsb, osb, sta, xnt, pfm = x_sb[bi], out_sb[bi], stat[bi], xnT[bi], p_fm[bi]
        if k + 1 < NSS:
            gate = L.get("f_x%d" % ((k + 1) % 2)) if k >= 1 else c_v2
            t_load[k + 1] = p.dma("sync", x_sb[(k + 1) % 2][:], x_t[k + 1],
                                  stream=(k + 1) % 2, after=[gate, tok_x_in])
        tk_xnT = []
        t_tr = None
        for s_i in range(NST):
            xs = xsb[:, s_i, :]
            rs, nb, tst = ln_stats(p, xs, sq_scr[:], sta, 12 * s_i, 1e-5,
                                   tok_in=[t_load[k], L.get("f_stat%d_%d" % (bi, s_i))])
            t_xn = p.do("scalar", lambda E, xs=xs, s_i=s_i, rs=rs, nb=nb: E.activation(
                xn_bf[:, s_i, :], xs, AF.Identity, bias=nb, scale=rs),
                after=[t_load[k], L.get("f_xn")], hard=[tst])
            L["f_stat%d_%d" % (bi, s_i)] = t_xn
            for c in range(8):
                pt = psT[c % 2]
                t_tr = p.do("tensor", lambda E, c=c, s_i=s_i, pt=pt: E.transpose(
                    pt[:], xn_bf[:, s_i, 128 * c:128 * (c + 1)], com.ident[:]),
                    after=[t_xn, com.tok_ident, L.get("f_psT%d" % (c % 2))])
                t_ev = p.do("vector", lambda E, c=c, s_i=s_i, pt=pt, xnt=xnt: E.tensor_copy(
                    xnt[:, c, 128 * s_i:128 * (s_i + 1)], pt[:]),
                    after=[t_tr, L.get("f_xnT")])
                L["f_psT%d" % (c % 2)] = t_ev
                tk_xnT.append(t_ev)
        L["f_xn"] = t_tr

        t_pev = []
        for s_i in range(NST):
            mm = None
            for c in range(8):
                mm = p.do("tensor", lambda E, c=c, s_i=s_i, xnt=xnt: E.matmul(
                    psP[:, 0:256], xnt[:, c, 128 * s_i:128 * (s_i + 1)], u1b[:, c, :],
                    start=(c == 0), stop=(c == 7)),
                    after=tk_xnT + [L.get("f_psP"), c_u1])
            t_pe = p.do("vector", lambda E, s_i=s_i: E.tensor_scalar_add(
                p_sb[:, s_i, :], psP[:, 0:256], 0.0), after=[mm, L.get("f_p")])
            L["f_psP"] = t_pe
            L["f_xnT"] = mm
            t_pev.append(t_pe)
        tk_pfm = []
        for s_i in range(NST):
            for rc in range(2):
                pt = psT[rc % 2]
                t_tr = p.do("tensor", lambda E, s_i=s_i, rc=rc, pt=pt: E.transpose(
                    pt[:], p_sb[:, s_i, 128 * rc:128 * (rc + 1)], com.ident[:]),
                    after=[t_pev[s_i], com.tok_ident, L.get("f_psT%d" % (rc % 2))])
                t_ev = p.do("vector", lambda E, s_i=s_i, rc=rc, pt=pt, pfm=pfm: E.tensor_copy(
                    pfm[:, rc, 128 * s_i:128 * (s_i + 1)], pt[:]),
                    after=[t_tr, L.get("f_pfm%d" % bi)])
                L["f_psT%d" % (rc % 2)] = t_ev
                tk_pfm.append(t_ev)
        L["f_p"] = t_tr

        t_hev = []
        for hc in range(32):
            ph = psH[hc % 2]
            mm = None
            for rc in range(2):
                mm = p.do("tensor", lambda E, hc=hc, rc=rc, ph=ph, pfm=pfm: E.matmul(
                    ph[:], v1b[:, rc, 128 * hc:128 * (hc + 1)], pfm[:, rc, :],
                    start=(rc == 0), stop=(rc == 1)),
                    after=tk_pfm + [L.get("f_psH%d" % (hc % 2)), c_v1])
            t_h = p.do("scalar", lambda E, hc=hc, ph=ph: E.activation(
                h_sb[:, hc, :], ph[:], AF.Gelu_apprx_tanh, bias=b1c[:, hc:hc + 1], scale=1.0),
                after=[mm, dwtot, L.get("f_h")])
            L["f_psH%d" % (hc % 2)] = t_h
            t_hev.append(t_h)

        t_qev = []
        for rc in range(2):
            mm = None
            for hc in range(32):
                mm = p.do("tensor", lambda E, rc=rc, hc=hc: E.matmul(
                    psQ[:], u2b[:, hc, 128 * rc:128 * (rc + 1)], h_sb[:, hc, :],
                    start=(hc == 0), stop=(hc == 31)),
                    after=t_hev + [L.get("f_psQ"), c_u2])
            t_q = p.do("vector", lambda E, rc=rc: E.tensor_scalar_add(
                q_fm[:, rc, :], psQ[:], 0.0), after=[mm, L.get("f_qfm")])
            L["f_psQ"] = t_q
            t_qev.append(t_q)
        L["f_h"] = mm

        t_out = []
        last_mm = None
        for s_i in range(NST):
            t_ep = []
            for half in range(2):
                pO = psO[half]
                mm = None
                for rc in range(2):
                    mm = p.do("tensor", lambda E, s_i=s_i, half=half, rc=rc, pO=pO: E.matmul(
                        pO[:], q_fm[:, rc, 128 * s_i:128 * (s_i + 1)],
                        v2b[:, rc, 512 * half:512 * (half + 1)],
                        start=(rc == 0), stop=(rc == 1)),
                        after=t_qev + [L.get("f_psO%d" % half), c_v2])
                last_mm = mm
                t_e = p.do("vector", lambda E, s_i=s_i, half=half, pO=pO, osb=osb: E.tensor_add(
                    osb[:, s_i, 512 * half:512 * (half + 1)], pO[:],
                    b2rep[:, 512 * half:512 * (half + 1)]),
                    after=[mm, tb_e0, tb_e1, L.get("f_out")])
                L["f_psO%d" % half] = t_e
                t_ep.append(t_e)
            rs3, nb3, tst3 = ln_stats(p, osb[:, s_i, :], sq_scr[:], sta,
                                      12 * s_i + 6, 1e-5, tok_in=t_ep)
            t_ln = p.do("vector", lambda E, s_i=s_i, rs3=rs3, nb3=nb3, osb=osb: E.tensor_scalar(
                osb[:, s_i, :], osb[:, s_i, :], rs3, nb3, ALU.mult, ALU.add),
                hard=[tst3])
            t_fin = p.do("vector", lambda E, s_i=s_i, osb=osb, xsb=xsb: E.tensor_add(
                osb[:, s_i, :], osb[:, s_i, :], xsb[:, s_i, :]))
            t_out.append(t_fin)
        L["f_qfm"] = last_mm
        L["f_x%d" % bi] = t_out[-1]
        t_str = p.dma("gpsimd", xo_t[k], osb[:], stream=bi, after=t_out)
        L["f_out"] = t_str
        tok_store.append(t_str)

    for k in range(NSS):
        emit_ss(k)
    return tok_store





def ternarize(p, com, w_r, n_chunks, ncols, stage, wtern_view, mwcol, psrep, pref,
              gate0=None, gate1=None, wtern_war=None):
    L = p.last
    acc = com.rowacc
    t0 = p.do("vector", lambda E: E.memset(acc[:, 0:1], 0.0))
    tc = [gate0, gate1]
    for c in range(n_chunks):
        sl = c % 2
        t_w = p.dma("sync", stage[:, sl, 0:ncols], w_r[c], stream=sl, after=[tc[sl]])
        p.do("vector", lambda E, sl=sl: E.reduce_sum(
            acc[:, 1:2], stage[:, sl, 0:ncols], axis=AX.X, apply_absolute_value=True),
            after=[t_w], hard=[t0])
        tc[sl] = p.do("vector", lambda E: E.tensor_add(acc[:, 0:1], acc[:, 0:1], acc[:, 1:2]))
    pskey = "war_" + psrep.name
    t_mm = p.do("tensor", lambda E: E.matmul(
        psrep[0:1, 0:1], com.onescol_f[:, 0:1], acc[:, 0:1], start=True, stop=True),
        after=[tc[0], tc[1], com.tok_ones, L.get(pskey)])
    row = com.rowscratch
    p.do("vector", lambda E, n=n_chunks * 128 * ncols: E.tensor_scalar(
        row[0:1, 0:1], psrep[0:1, 0:1], 1.0 / n, 1e-5, ALU.mult, ALU.max), after=[t_mm])
    p.do("vector", lambda E: E.tensor_scalar_mul(row[0:1, 1:2], row[0:1, 0:1], -1.0))
    p.do("vector", lambda E: E.tensor_scalar_mul(row[0:1, 2:3], row[0:1, 0:1], 0.5))
    t2 = p.do("vector", lambda E: E.tensor_scalar_mul(row[0:1, 3:4], row[0:1, 0:1], -0.5))
    t_rep = p.do("tensor", lambda E: E.matmul(
        psrep[:, 0:3], com.onesrow_f[0:1, 0:128], row[0:1, 1:4], start=True, stop=True),
        after=[t2])
    t_mw = p.do("vector", lambda E: E.tensor_copy(mwcol[:], psrep[:, 0:3]), after=[t_rep])
    L[pskey] = t_mw
    tern_done = []
    tc = [t_mw, t_mw]
    for c in range(n_chunks):
        sl = c % 2
        t_w = p.dma("sync", stage[:, sl, 0:ncols], w_r[c], stream=sl, after=[tc[sl]])
        t_a = p.do("vector", lambda E, sl=sl: E.tensor_scalar(
            com.rndA[:, 0:ncols], stage[:, sl, 0:ncols], mwcol[:, 1:2], None, ALU.is_gt),
            after=[t_w, t_mw])
        tc[sl] = p.do("vector", lambda E, c=c, sl=sl: E.scalar_tensor_tensor(
            wtern_view[:, c, 0:ncols], stage[:, sl, 0:ncols], mwcol[:, 2:3],
            com.rndA[:, 0:ncols], ALU.is_lt, ALU.subtract),
            after=[wtern_war if c == 0 else None])
        tern_done.append(tc[sl])
    return tern_done


def build_l0_ffn(p, stk, com, dram, x_in, x_out, h2_dram, tok_x_in=None, dbg=None):
    nc = p.nc
    L = p.last

    wtern = sb(stk, nc, "c_wtern", [128, 32768], BF16)
    w1t = wtern[:].rearrange("p (c f) -> p c f", f=4096)
    w2t = wtern[:].rearrange("p (c f) -> p c f", f=1024)
    mw1c = sb(stk, nc, "c_mw1", [128, 3], F32)
    mw2c = sb(stk, nc, "c_mw2", [128, 3], F32)
    cwc = sb(stk, nc, "c_cw", [128, 3, 32], F32)
    cbc = sb(stk, nc, "c_cb", [128, 32], F32)

    psT = [ps(stk, nc, "c_psT%d" % i, [128, 128], BF16) for i in range(2)]
    psH = [ps(stk, nc, "c_psH%d" % i, [128, 512], F32) for i in range(2)]
    psO = [ps(stk, nc, "c_psO%d" % i, [128, 512], F32) for i in range(4)]

    t_cw = p.dma("sync", cwc[:], dram["conv_w"][:], sig="dw")
    t_cb = p.dma("sync", cbc[:], dram["conv_b"][:], sig="dw")
    dwtot = ("dw", p.cnt["dw"])

    w1_r = dram["w1_t"].rearrange("(c p) f -> c p f", p=128)
    w2_r = dram["w2_t"].rearrange("(c p) f -> c p f", p=128)

    x_sb = [sb(stk, nc, "c_x%d" % i, [128, NST, 1024], F32) for i in range(2)]
    sq_scr = sb(stk, nc, "c_sq", [128, 1024], F32)
    stat = [sb(stk, nc, "c_stat%d" % i, [128, 48], F32) for i in range(2)]

    stk1 = stk.enter_context(contextlib.ExitStack())
    stage = sb(stk1, nc, "c_stage", [128, 2, 1024], F32)
    xn_f = sb(stk1, nc, "c_xn", [128, 1024], F32)
    xq_bf = sb(stk1, nc, "c_xq", [128, NST, 1024], BF16)
    xqT1 = sb(stk1, nc, "c_xqT", [128, 8, 512], BF16)
    cbuf = sb(stk1, nc, "c_cbuf", [128, 2, 520], F32)
    halo = sb(stk1, nc, "c_halo", [128, 32, 2], F32)
    h2s = sb(stk1, nc, "c_h2st", [128, 32, 520], BF16)
    cvscr = sb(stk1, nc, "c_cvscr", [128, 2, 520], F32)

    w1_r8 = dram["w1_t"].rearrange("(c p) f -> c p f", p=128)
    w1_chunks = [w1_r8[c][:, 1024 * h:1024 * (h + 1)] for c in range(8) for h in range(4)]
    tern1 = ternarize(p, com, w1_chunks, 32, 1024, stage, w2t, mw1c, psO[0], "c_w1",
                      gate0=L.get("gate_dl0"), gate1=L.get("gate_dl1"))

    x_t = x_in.rearrange("(k s p) d -> k p s d", p=128, s=NST)
    xo_t = x_out.rearrange("(k s p) d -> k p s d", p=128, s=NST)

    t_load = [None] * NSS
    t_load[0] = p.dma("sync", x_sb[0][:], x_t[0], stream=0, after=[tok_x_in, tern1[-2]])
    h2win = [None] * NSS

    def emit_sweep1(k):
        bi = k % 2
        b_i, w_i = k // SS_PER_SEQ, k % SS_PER_SEQ
        xsb, sta = x_sb[bi], stat[bi]
        if k + 1 < NSS:
            gate = L.get("c_x%d" % ((k + 1) % 2)) if k >= 1 else tern1[-1]
            t_load[k + 1] = p.dma("sync", x_sb[(k + 1) % 2][:], x_t[k + 1],
                                  stream=(k + 1) % 2, after=[gate, tok_x_in])
        t_hz = None
        if w_i == 0:
            t_hz = p.do("vector", lambda E: E.memset(halo[:], 0.0),
                        after=[L.get("c_halo_rd")])
        tk_xqT = []
        t_tr = None
        t_q = None
        for s_i in range(NST):
            xs = xsb[:, s_i, :]
            rs, nb, tst = ln_stats(p, xs, sq_scr[:], sta, 12 * s_i, 1e-5,
                                   tok_in=[t_load[k], L.get("c_stat%d_%d" % (bi, s_i))])
            am, iv = sta[:, 12 * s_i + 4:12 * s_i + 5], sta[:, 12 * s_i + 5:12 * s_i + 6]
            t_xn = p.do("scalar", lambda E, xs=xs, rs=rs, nb=nb: E.activation(
                xn_f[:], xs, AF.Identity, bias=nb, scale=rs),
                after=[t_load[k], L.get("c_xnw")], hard=[tst])
            t_am = p.do("vector", lambda E, am=am: E.reduce_max(
                am, xn_f[:], axis=AX.X, apply_absolute_value=True), after=[t_xn])
            p.do("vector", lambda E, am=am: E.tensor_scalar_max(am, am, 1e-5))
            t_iv = p.do("vector", lambda E, am=am, iv=iv: E.reciprocal(iv, am))
            t_iv = p.do("vector", lambda E, iv=iv: E.tensor_scalar_mul(iv, iv, 127.0))
            am127 = sta[:, 12 * s_i + 6:12 * s_i + 7]
            t_a7 = p.do("vector", lambda E, am=am, am127=am127: E.tensor_scalar_mul(
                am127, am, 1.0 / 127.0))
            t_sc = p.do("scalar", lambda E, iv=iv: E.activation(
                sq_scr[:], xn_f[:], AF.Identity, scale=iv),
                after=[L.get("c_sqw")], hard=[t_iv])
            L["c_xnw"] = t_sc
            p.do("vector", lambda E: E.tensor_copy(com.i32scr[:, 0:1024], sq_scr[:]),
                 after=[t_sc, L.get("com_i32")])
            p.do("vector", lambda E: E.tensor_sub(com.rndB[:], sq_scr[:], com.i32scr[:, 0:1024]))
            p.do("vector", lambda E: E.scalar_tensor_tensor(
                com.rndA[:], com.rndB[:], 0.5, com.i32scr[:, 0:1024], ALU.is_gt, ALU.add))
            t_r = p.do("vector", lambda E: E.scalar_tensor_tensor(
                sq_scr[:], com.rndB[:], -0.5, com.rndA[:], ALU.is_ge, ALU.add))
            L["c_sqw"] = t_r
            t_q = p.do("vector", lambda E, s_i=s_i, am127=am127: E.tensor_scalar(
                xq_bf[:, s_i, :], sq_scr[:], -1.0, am127, ALU.add, ALU.mult),
                after=[L.get("c_xqw")], hard=[t_a7])
            L["com_i32"] = t_q
            L["c_stat%d_%d" % (bi, s_i)] = t_q
            for c in range(8):
                pt = psT[c % 2]
                t_tr = p.do("tensor", lambda E, c=c, s_i=s_i, pt=pt: E.transpose(
                    pt[:], xq_bf[:, s_i, 128 * c:128 * (c + 1)], com.ident[:]),
                    after=[t_q, com.tok_ident, L.get("c_psT%d" % (c % 2))])
                t_ev = p.do("vector", lambda E, c=c, s_i=s_i, pt=pt: E.tensor_copy(
                    xqT1[:, c, 128 * s_i:128 * (s_i + 1)], pt[:]),
                    after=[t_tr, L.get("c_xqT")])
                L["c_psT%d" % (c % 2)] = t_ev
                tk_xqT.append(t_ev)
        L["c_xqw"] = t_tr
        L["c_x%d" % bi] = t_q
        if dbg is not None and k == 0:
            p.dma("sync", dbg["xq"][:], xq_bf[:], sig="dbg", after=[t_tr])
            p.dma("sync", dbg["wt"][:], wtern[:, 0:8192], sig="dbg", after=[t_tr])

        first = (w_i == 0)
        last = (w_i == SS_PER_SEQ - 1)
        ncv = 513 if last else 512
        t_h2 = []
        t_hs = None
        for hc in range(32):
            ph = psH[hc % 2]
            mm = None
            for c in range(8):
                mm = p.do("tensor", lambda E, hc=hc, c=c, ph=ph: E.matmul(
                    ph[:], w1t[:, c, 128 * hc:128 * (hc + 1)], xqT1[:, c, :],
                    start=(c == 0), stop=(c == 7)),
                    after=tk_xqT + [L.get("c_psH%d" % (hc % 2))] +
                          ([tern1[-1]] if k == 0 else []))
            L["c_mm1last"] = mm
            t_ev = p.do("vector", lambda E, hc=hc, ph=ph: E.tensor_copy(
                cbuf[:, hc % 2, 2:514], ph[:]), after=[mm, L.get("c_cbuf%d" % (hc % 2))])
            t_hc = p.do("vector", lambda E, hc=hc: E.tensor_copy(
                cbuf[:, hc % 2, 0:2], halo[:, hc, :]), after=[t_hz])
            t_hs = p.do("vector", lambda E, hc=hc: E.tensor_copy(
                halo[:, hc, :], cbuf[:, hc % 2, 512:514]))
            if last:
                p.do("vector", lambda E, hc=hc: E.memset(cbuf[:, hc % 2, 514:515], 0.0))
            cv = cvscr[:, hc % 2, :]
            t_c1 = p.do("vector", lambda E, hc=hc, ncv=ncv, cv=cv: E.tensor_scalar(
                cv[:, 0:ncv] if False else cv[0:128, 0:ncv], cbuf[:, hc % 2, 0:ncv],
                cwc[:, 0, hc:hc + 1], None, ALU.mult),
                after=[L.get("com_cvscr%d" % (hc % 2))], hard=[dwtot])
            p.do("vector", lambda E, hc=hc, ncv=ncv, cv=cv: E.scalar_tensor_tensor(
                cv[0:128, 0:ncv], cbuf[:, hc % 2, 1:1 + ncv], cwc[:, 1, hc:hc + 1],
                cv[0:128, 0:ncv], ALU.mult, ALU.add))
            t_c3 = p.do("vector", lambda E, hc=hc, ncv=ncv, cv=cv: E.scalar_tensor_tensor(
                cv[0:128, 0:ncv], cbuf[:, hc % 2, 2:2 + ncv], cwc[:, 2, hc:hc + 1],
                cv[0:128, 0:ncv], ALU.mult, ALU.add))
            if dbg is not None and k == 0 and hc == 0:
                t_d = p.dma("sync", dbg["cbuf"][:], cbuf[:, 0, 0:514], sig="dbg", after=[t_c3])
                L["c_cbuf0"] = t_d
            L["c_cbuf%d" % (hc % 2)] = t_c3 if not (dbg is not None and k == 0 and hc == 0) else t_d
            t_si = p.do("scalar", lambda E, hc=hc, ncv=ncv, cv=cv: E.activation(
                h2s[:, hc, 0:ncv], cv[0:128, 0:ncv], AF.Silu,
                bias=cbc[:, hc:hc + 1], scale=mw1c[:, 0:1]),
                after=[t_c3, L.get("c_h2st")], hard=[dwtot, tern1[0]])
            L["com_cvscr%d" % (hc % 2)] = t_si
            L["c_psH%d" % (hc % 2)] = t_ev
            t_h2.append(t_si)
        L["c_halo_rd"] = t_hc
        L["c_halo"] = t_hs
        if first:
            off, lo, ln_ = 2048 * b_i, 1, 511
        elif last:
            off, lo, ln_ = 2048 * b_i + 512 * w_i - 1, 0, 513
        else:
            off, lo, ln_ = 2048 * b_i + 512 * w_i - 1, 0, 512
        t_st = []
        for g in range(4):
            t_st.append(p.dma(
                "gpsimd",
                h2_dram[8 * g:8 * (g + 1), :, off:off + ln_].rearrange("c p t -> p c t"),
                h2s[:, 8 * g:8 * (g + 1), lo:lo + ln_], stream=bi,
                after=t_h2 + [L.get("c_h2w%d" % bi)]))
        h2win[k] = t_st[-1]
        L["c_h2w%d" % bi] = t_st[-1]
        L["c_h2st"] = t_st[-1]

    for k in range(NSS):
        emit_sweep1(k)

    w2_chunks = [w2_r[c] for c in range(32)]
    tern2 = ternarize(p, com, w2_chunks, 32, 1024, stage, w2t, mw2c, psO[1], "c_w2",
                      gate0=L.get("c_x0"), gate1=L.get("c_x1"),
                      wtern_war=L.get("c_mm1last"))
    stk1.close()
    p.barrier()

    amrow = sb(stk, nc, "c_amrow", [1, 512], BF16)
    s2row = sb(stk, nc, "c_s2row", [1, 512], F32)
    s2rep = sb(stk, nc, "c_s2rep", [128, 512], F32)
    h2l1 = sb(stk, nc, "c_h2l", [128, 32, 512], BF16)
    h2l = [h2l1, h2l1]
    xq2 = sb(stk, nc, "c_xq2", [128, 32, 512], BF16)
    mx = sb(stk, nc, "c_mx", [128, 512], BF16)
    mxc = sb(stk, nc, "c_mxc", [128, 4], BF16)
    out_sb1 = sb(stk, nc, "c_out", [128, NST, 1024], F32)
    out_sb = [out_sb1, out_sb1]
    oscol = sb(stk, nc, "c_oscol", [128, NST], F32)
    w2csrow = sb(stk, nc, "c_w2csrow", [1, 1024], BF16)
    w2csrep = sb(stk, nc, "c_w2csrep", [128, 1024], F32)
    onescol_b = sb(stk, nc, "c_onescol_b", [128, 1], BF16)
    t_ocb = p.do("vector", lambda E: E.memset(onescol_b[:], 1.0))
    t_cs = {}
    for half in range(2):
        mmcs = None
        for hc in range(32):
            mmcs = p.do("tensor", lambda E, half=half, hc=hc: E.matmul(
                psH[half][0:1, 0:512], onescol_b[:, 0:1], w2t[:, hc, 512 * half:512 * (half + 1)],
                start=(hc == 0), stop=(hc == 31)),
                after=[t_ocb, tern2[-1], tern2[-2]], hard=[t_ocb])
        t_cs[half] = p.do("vector", lambda E, half=half: E.tensor_copy(
            w2csrow[0:1, 512 * half:512 * (half + 1)], psH[half][0:1, 0:512]), after=[mmcs])
    t_rp1 = p.do("tensor", lambda E: E.matmul(
        psH[0][:], com.onesrow_b[0:1, 0:128], w2csrow[0:1, 0:512], start=True, stop=True),
        after=[t_cs[0], t_cs[1], com.tok_ones])
    t_rp2 = p.do("tensor", lambda E: E.matmul(
        psH[1][:], com.onesrow_b[0:1, 0:128], w2csrow[0:1, 512:1024], start=True, stop=True),
        after=[t_cs[0], t_cs[1], com.tok_ones])
    t_cr1 = p.do("vector", lambda E: E.tensor_copy(w2csrep[:, 0:512], psH[0][:]), after=[t_rp1])
    t_cr2 = p.do("vector", lambda E: E.tensor_copy(w2csrep[:, 512:1024], psH[1][:]), after=[t_rp2])
    L["c_psH0"] = t_cr1
    L["c_psH1"] = t_cr2

    def h2gate(k):
        toks = [h2win[k]]
        if k % SS_PER_SEQ != SS_PER_SEQ - 1:
            toks.append(h2win[k + 1])
        return toks

    h2l_t = [None] * NSS
    h2l_t[0] = p.dma("sync", h2l[0][:],
                     h2_dram[:, :, 0:512].rearrange("c p t -> p c t"), stream=0,
                     after=[tern2[-2]] + h2gate(0))
    x2_t = [None] * NSS
    x2_t[0] = p.dma("sync", x_sb[0][:], x_t[0], sig="dx0", after=[tern2[-1]])

    def emit_sweep2(k):
        bi = k % 2
        xsb, sta, osb, h2b = x_sb[bi], stat[bi], out_sb[bi], h2l[bi]
        t_mx = p.do("vector", lambda E: E.scalar_tensor_tensor(
            mx[:], h2b[:, 0, :], -1.0, h2b[:, 0, :], ALU.mult, ALU.max),
            after=[h2l_t[k], L.get("c_mx")])
        for hc in range(1, 32):
            p.do("vector", lambda E, hc=hc: E.tensor_max(
                mx[:], mx[:], h2b[:, hc, :]))
            t_mx = p.do("vector", lambda E, hc=hc: E.scalar_tensor_tensor(
                mx[:], h2b[:, hc, :], -1.0, mx[:], ALU.mult, ALU.max))
        t_am = []
        for s_i in range(NST):
            pt = psT[s_i % 2]
            t_tr = p.do("tensor", lambda E, s_i=s_i, pt=pt: E.transpose(
                pt[:], mx[:, 128 * s_i:128 * (s_i + 1)], com.ident[:]),
                after=[t_mx, com.tok_ident, L.get("c_psT%d" % (s_i % 2))])
            t_rm = p.do("vector", lambda E, s_i=s_i, pt=pt: E.reduce_max(
                oscol[:, s_i:s_i + 1], pt[:], axis=AX.X),
                after=[t_tr, L.get("c_oscol")])
            L["c_psT%d" % (s_i % 2)] = t_rm
            t_am.append(t_rm)
        t_rowp = []
        for s_i in range(NST):
            pt = psT[s_i % 2]
            t_cc = p.do("vector", lambda E, s_i=s_i: E.tensor_scalar_add(
                mxc[:, s_i:s_i + 1], oscol[:, s_i:s_i + 1], 0.0), after=[t_am[s_i]])
            t_tr = p.do("tensor", lambda E, s_i=s_i, pt=pt: E.transpose(
                pt[0:1, 0:128], mxc[:, s_i:s_i + 1], com.ident[:]),
                after=[t_cc, L.get("c_psT%d" % (s_i % 2))])
            t_ev = p.do("vector", lambda E, s_i=s_i, pt=pt: E.tensor_copy(
                amrow[0:1, 128 * s_i:128 * (s_i + 1)], pt[0:1, 0:128]),
                after=[t_tr])
            L["c_psT%d" % (s_i % 2)] = t_ev
            t_rowp.append(t_ev)
        L["c_mx"] = t_rowp[-1]
        t_s1 = p.do("vector", lambda E: E.tensor_scalar_max(s2row[:], amrow[:], 1e-5),
                    after=t_rowp)
        p.do("vector", lambda E: E.reciprocal(s2row[:], s2row[:]))
        t_s3 = p.do("vector", lambda E: E.tensor_scalar_mul(s2row[:], s2row[:], 127.0))
        t_rp = p.do("tensor", lambda E: E.matmul(
            psH[0][:], com.onesrow_f[0:1, 0:128], s2row[0:1, :], start=True, stop=True),
            after=[t_s3, com.tok_ones, L.get("c_psH0")])
        t_re = p.do("vector", lambda E: E.tensor_copy(s2rep[:], psH[0][:]), after=[t_rp])
        L["c_psH0"] = t_re
        t_xq2 = []
        for hc in range(32):
            p.do("vector", lambda E, hc=hc: E.tensor_tensor(
                com.rndA[:, 0:512], h2b[:, hc, :], s2rep[:], ALU.mult),
                after=[t_re, L.get("com_i32")])
            p.do("vector", lambda E: E.tensor_copy(com.i32scr[:, 0:512], com.rndA[:, 0:512]))
            p.do("vector", lambda E: E.tensor_sub(
                com.rndB[:, 0:512], com.rndA[:, 0:512], com.i32scr[:, 0:512]))
            p.do("vector", lambda E: E.scalar_tensor_tensor(
                com.rndA[:, 0:512], com.rndB[:, 0:512], 0.5, com.i32scr[:, 0:512],
                ALU.is_gt, ALU.add))
            t_b = p.do("vector", lambda E, hc=hc: E.scalar_tensor_tensor(
                xq2[:, hc, :], com.rndB[:, 0:512], -0.5, com.rndA[:, 0:512],
                ALU.is_ge, ALU.add), after=[L.get("c_xq2")])
            L["com_i32"] = t_b
            t_xq2.append(t_b)
        L["c2_h2l"] = t_xq2[-1]
        if k + 1 < NSS:
            nb_ = (k + 1) % 2
            h2l_t[k + 1] = p.dma("sync", h2l[nb_][:],
                                 h2_dram[:, :, 512 * (k + 1):512 * (k + 2)].rearrange("c p t -> p c t"),
                                 stream=nb_, after=[t_xq2[-1]] + h2gate(k + 1))
            gate2 = L.get("c2_x%d" % nb_) if k >= 1 else None
            x2_t[k + 1] = p.dma("sync", x_sb[nb_][:], x_t[k + 1],
                                sig="dx%d" % nb_, after=[gate2])
        t_os = []
        for s_i in range(NST):
            p.do("vector", lambda E, s_i=s_i: E.tensor_scalar_max(
                oscol[:, s_i:s_i + 1], oscol[:, s_i:s_i + 1], 1e-5), after=[t_am[s_i]])
            t_o2 = p.do("vector", lambda E, s_i=s_i: E.scalar_tensor_tensor(
                oscol[:, s_i:s_i + 1], oscol[:, s_i:s_i + 1], 1.0 / 127.0,
                mw2c[:, 0:1], ALU.mult, ALU.mult), hard=[tern2[0]])
            t_os.append(t_o2)
        t_out = []
        last_mm = None
        t_elast = None
        for s_i in range(NST):
            t_ep = []
            for half in range(2):
                pok = 2 * (s_i % 2) + half
                pO = psO[pok]
                mm = None
                for hc in range(32):
                    mm = p.do("tensor", lambda E, s_i=s_i, half=half, hc=hc, pO=pO: E.matmul(
                        pO[:], xq2[:, hc, 128 * s_i:128 * (s_i + 1)],
                        w2t[:, hc, 512 * half:512 * (half + 1)],
                        start=(hc == 0), stop=(hc == 31)),
                        after=[t_xq2[-1], L.get("c_psO%d" % pok)] +
                              ([tern2[-1]] if k == 0 else []))
                last_mm = mm
                epib = com.rndA if half == 0 else com.rndB
                t_sub = p.do("vector", lambda E, half=half, pO=pO, epib=epib: E.tensor_sub(
                    epib[:, 512:1024], pO[:], w2csrep[:, 512 * half:512 * (half + 1)]),
                    after=[mm, t_cr1, t_cr2, L.get("c_epi%d" % half)])
                t_e = p.do("scalar", lambda E, s_i=s_i, half=half, osb=osb, epib=epib: E.activation(
                    osb[:, s_i, 512 * half:512 * (half + 1)], epib[:, 512:1024], AF.Identity,
                    scale=oscol[:, s_i:s_i + 1]),
                    after=[t_sub, L.get("c_out")], hard=[t_os[s_i]])
                L["c_epi%d" % half] = t_e
                L["c_psO%d" % pok] = t_sub
                t_elast = t_e
                t_ep.append(t_e)
            rs3, nb3, tst3 = ln_stats(p, osb[:, s_i, :], sq_scr[:], sta,
                                      12 * s_i + 6, 1e-5, tok_in=t_ep)
            t_ln = p.do("vector", lambda E, s_i=s_i, rs3=rs3, nb3=nb3, osb=osb: E.tensor_scalar(
                osb[:, s_i, :], osb[:, s_i, :], rs3, nb3, ALU.mult, ALU.add),
                hard=[tst3])
            t_fin = p.do("vector", lambda E, s_i=s_i, osb=osb, xsb=xsb: E.tensor_add(
                osb[:, s_i, :], osb[:, s_i, :], xsb[:, s_i, :]), after=[x2_t[k]])
            t_out.append(t_fin)
        L["c_oscol"] = t_elast
        L["c_xq2"] = last_mm
        L["c2_x%d" % bi] = t_out[-1]
        t_str = p.dma("gpsimd", xo_t[k], osb[:], stream=bi, after=t_out)
        L["c_out"] = t_str
        return t_str

    tok_store = []
    for k in range(NSS):
        tok_store.append(emit_sweep2(k))
    return tok_store





def build_l1_attn(p, stk, com, dram, x_in, x_out, ctx_dram, tok_x_in=None):
    nc = p.nc
    L = p.last

    stage = sb(stk, nc, "b_stage", [128, 2, 1024], F32)
    terns = {}
    mws = {}
    for nm in ("q", "k", "v", "o"):
        terns[nm] = sb(stk, nc, "b_tern%s" % nm, [128, 8, 1024], BF16)
        mws[nm] = sb(stk, nc, "b_mw%s" % nm, [128, 3], F32)
    identf = sb(stk, nc, "b_identf", [128, 128], F32)
    psS = [ps(stk, nc, "b_psS%d" % i, [128, 1024], F32) for i in range(2)]
    psC = [ps(stk, nc, "b_psC%d" % i, [128, 128], F32) for i in range(2)]

    t_if = p.do("vector", lambda E: E.memset(identf[:], 1.0))
    tok_identf = p.do(
        "gpsimd",
        lambda E: E.affine_select(identf[:], identf[:], [[-1, 128]],
                                  ALU.is_equal, 0.0, base=0, channel_multiplier=1),
        after=[t_if])

    w_r = {nm: dram["w%s_t" % nm].rearrange("(c p) e -> c p e", p=128)
           for nm in ("q", "k", "v", "o")}
    g0, g1 = L.get("gate_dl0"), L.get("gate_dl1")
    tern_t = {}
    for nm in ("q", "k", "v", "o"):
        tern_t[nm] = ternarize(p, com, [w_r[nm][c] for c in range(8)], 8, 1024,
                               stage, terns[nm], mws[nm], psS[0], "b_" + nm,
                               gate0=g0, gate1=g1, wtern_war=None)
        g0, g1 = tern_t[nm][-2], tern_t[nm][-1]

    x_sb = [sb(stk, nc, "b_x%d" % i, [128, NST, 1024], F32) for i in range(2)]
    xn_f = sb(stk, nc, "b_xn", [128, 1024], F32)
    xqT = sb(stk, nc, "b_xqT", [128, 8, 512], BF16)
    qtok = sb(stk, nc, "b_qtok", [128, 16, 64], BF16)
    ktok = sb(stk, nc, "b_ktok", [128, 16, 64], BF16)
    vtok = sb(stk, nc, "b_vtok", [128, 16, 64], BF16)
    prodA = sb(stk, nc, "b_prodA", [128, 2, 16, 64], BF16)
    prodB = sb(stk, nc, "b_prodB", [128, 2, 64, 16], BF16)
    scb = sb(stk, nc, "b_sc", [128, 16, 16], F32)
    attnb = sb(stk, nc, "b_attn", [128, 16, 16], BF16)
    smx = sb(stk, nc, "b_smx", [128, 16], F32)
    ctxs = sb(stk, nc, "b_ctxs", [128, 2, 1024], F32)
    sq_scr = sb(stk, nc, "b_sq", [128, 1024], F32)
    stat = [sb(stk, nc, "b_stat%d" % i, [128, 48], F32) for i in range(2)]

    x_t = x_in.rearrange("(k s p) d -> k p s d", p=128, s=NST)
    xo_t = x_out.rearrange("(k s p) d -> k p s d", p=128, s=NST)

    tok_store = []
    t_load = [None] * NSS
    t_load[0] = p.dma("sync", x_sb[0][:], x_t[0], stream=0,
                      after=[tok_x_in, tern_t["o"][-2]])
    ctxw = [None] * (NSS * NST)

    def emit_sweep1(k):
        bi = k % 2
        xsb, sta = x_sb[bi], stat[bi]
        if k + 1 < NSS:
            gate = L.get("b_x%d" % ((k + 1) % 2)) if k >= 1 else tern_t["o"][-1]
            t_load[k + 1] = p.dma("sync", x_sb[(k + 1) % 2][:], x_t[k + 1],
                                  stream=(k + 1) % 2, after=[gate, tok_x_in])
        t_q = None
        for s_i in range(NST):
            xs = xsb[:, s_i, :]
            rs, nb, tst = ln_stats(p, xs, sq_scr[:], sta, 12 * s_i, 1e-5,
                                   tok_in=[t_load[k], L.get("b_stat%d_%d" % (bi, s_i))])
            t_xn = p.do("scalar", lambda E, xs=xs, rs=rs, nb=nb: E.activation(
                xn_f[:], xs, AF.Identity, bias=nb, scale=rs),
                after=[t_load[k], L.get("b_xnw")], hard=[tst])
            rs2, trs2 = rms_stats(p, xn_f[:], sq_scr[:], sta, 12 * s_i + 4,
                                  1e-6, tok_in=[t_xn])
            t_xr = p.do("scalar", lambda E, rs2=rs2: E.activation(
                sq_scr[:], xn_f[:], AF.Identity, scale=rs2),
                after=[L.get("b_sqw")], hard=[trs2])
            L["b_xnw"] = t_xr
            am = sta[:, 12 * s_i + 6:12 * s_i + 7]
            iv = sta[:, 12 * s_i + 7:12 * s_i + 8]
            am127 = sta[:, 12 * s_i + 8:12 * s_i + 9]
            p.do("vector", lambda E, am=am: E.reduce_max(
                am, sq_scr[:], axis=AX.X, apply_absolute_value=True), after=[t_xr])
            p.do("vector", lambda E, am=am: E.tensor_scalar_max(am, am, 1e-5))
            p.do("vector", lambda E, am=am, iv=iv: E.reciprocal(iv, am))
            p.do("vector", lambda E, iv=iv: E.tensor_scalar_mul(iv, iv, 127.0))
            t_a7 = p.do("vector", lambda E, am=am, am127=am127: E.tensor_scalar_mul(
                am127, am, 1.0 / 127.0))
            t_z = p.do("scalar", lambda E, iv=iv: E.activation(
                xn_f[:], sq_scr[:], AF.Identity, scale=iv),
                after=[L.get("b_xnw2")], hard=[t_a7])
            L["b_sqw"] = t_z
            p.do("vector", lambda E: E.tensor_copy(com.i32scr[:, 0:1024], xn_f[:]),
                 after=[t_z, L.get("com_i32")])
            p.do("vector", lambda E: E.tensor_sub(com.rndB[:], xn_f[:],
                                                  com.i32scr[:, 0:1024]))
            p.do("vector", lambda E: E.scalar_tensor_tensor(
                com.rndA[:], com.rndB[:], 0.5, com.i32scr[:, 0:1024],
                ALU.is_gt, ALU.add))
            t_r = p.do("vector", lambda E: E.scalar_tensor_tensor(
                com.rndB[:], com.rndB[:], -0.5, com.rndA[:], ALU.is_ge, ALU.add))
            L["com_i32"] = t_r
            t_q = p.do("vector", lambda E, am127=am127: E.tensor_scalar(
                xn_f[:], com.rndB[:], -1.0, am127, ALU.add, ALU.mult),
                after=[L.get("b_xnw")])
            L["b_xnw2"] = t_q
            L["b_stat%d_%d" % (bi, s_i)] = t_q
            tk_xqT = []
            t_tr = None
            for c in range(8):
                pt = psC[c % 2]
                t_tr = p.do("tensor", lambda E, c=c, pt=pt: E.transpose(
                    pt[:], xn_f[:, 128 * c:128 * (c + 1)], identf[:]),
                    after=[t_q, tok_identf, L.get("war_b_psC%d" % (c % 2))])
                t_ev = p.do("vector", lambda E, c=c, pt=pt: E.tensor_scalar_add(
                    xqT[:, c, 0:128], pt[:], 0.0), after=[t_tr, L.get("b_xqTw")])
                L["war_b_psC%d" % (c % 2)] = t_ev
                tk_xqT.append(t_ev)
            L["b_xnw"] = t_tr
            t_pr = {}
            for nm, dst in (("q", qtok), ("k", ktok), ("v", vtok)):
                for half in range(2):
                    pg = psS[half]
                    mm = None
                    for c in range(8):
                        mm = p.do("tensor", lambda E, nm=nm, half=half, c=c, pg=pg: E.matmul(
                            pg[:, 0:512], xqT[:, c, 0:128],
                            terns[nm][:, c, 512 * half:512 * (half + 1)],
                            start=(c == 0), stop=(c == 7)),
                            after=tk_xqT + [L.get("war_b_psS%d" % half)] +
                                  ([tern_t[nm][-1]] if k == 0 and s_i == 0 else []))
                    t_pr[(nm, half)] = p.do("scalar", lambda E, nm=nm, half=half, pg=pg, dst=dst: E.activation(
                        dst[:, 8 * half:8 * (half + 1), :].rearrange("p a b -> p (a b)"),
                        pg[:, 0:512], AF.Identity, scale=mws[nm][:, 0:1]),
                        after=[mm, L.get("b_%sw" % nm)], hard=[tern_t[nm][0]])
                    L["war_b_psS%d" % half] = t_pr[(nm, half)]
            L["b_xqTw"] = mm
            t_sc = []
            for h in range(16):
                t_m = p.do("gpsimd", lambda E, h=h: E.tensor_tensor(
                    prodA[:, h % 2, :, :],
                    qtok[:, h, :].unsqueeze(1).broadcast_to([128, 16, 64]),
                    ktok[:], ALU.mult),
                    after=[t_pr[("q", h // 8)], t_pr[("k", 0)], t_pr[("k", 1)],
                           L.get("b_prA%d" % (h % 2))])
                t_s = p.do("vector", lambda E, h=h: E.reduce_sum(
                    scb[:, h, :], prodA[:, h % 2, :, :], axis=AX.X), after=[t_m])
                L["b_prA%d" % (h % 2)] = t_s
                t_sc.append(t_s)
            t_mx = p.do("vector", lambda E: E.reduce_max(
                smx[:], scb[:], axis=AX.X), after=t_sc)
            t_sb = p.do("vector", lambda E: E.tensor_sub(
                scb[:], scb[:], smx[:].unsqueeze(2).broadcast_to([128, 16, 16])))
            t_ex = p.do("scalar", lambda E: E.activation(
                scb[:].rearrange("p a b -> p (a b)"),
                scb[:].rearrange("p a b -> p (a b)"), AF.Exp, scale=0.125),
                after=[t_sb, L.get("b_scw")], hard=[t_sb])
            t_sm = p.do("vector", lambda E: E.reduce_sum(
                smx[:], scb[:], axis=AX.X), after=[t_ex])
            t_rc = p.do("vector", lambda E: E.reciprocal(smx[:], smx[:]))
            t_at = p.do("vector", lambda E: E.tensor_tensor(
                attnb[:], scb[:], smx[:].unsqueeze(2).broadcast_to([128, 16, 16]),
                ALU.mult), hard=[t_rc])
            L["b_scw"] = t_at
            vview = vtok[:].transpose([0, 2, 1])
            t_cx = []
            for h in range(16):
                t_m = p.do("gpsimd", lambda E, h=h: E.tensor_tensor(
                    prodB[:, h % 2, :, :],
                    attnb[:, h, :].unsqueeze(1).broadcast_to([128, 64, 16]),
                    vview, ALU.mult),
                    after=[t_at, t_pr[("v", 0)], t_pr[("v", 1)],
                           L.get("b_prB%d" % (h % 2))])
                t_c = p.do("vector", lambda E, h=h, s_i=s_i: E.reduce_sum(
                    ctxs[:, s_i % 2, 64 * h:64 * (h + 1)], prodB[:, h % 2, :, :],
                    axis=AX.X),
                    after=[t_m, L.get("b_ctxs%d" % (s_i % 2))])
                L["b_prB%d" % (h % 2)] = t_c
                t_cx.append(t_c)
            g = NST * k + s_i
            t_cw = p.dma("gpsimd", ctx_dram[128 * g:128 * (g + 1), :],
                         ctxs[:, s_i % 2, :], sig="dc%d" % (s_i % 2),
                         after=t_cx + [L.get("b_cwg%d" % (s_i % 2))])
            ctxw[g] = t_cw
            L["b_cwg%d" % (s_i % 2)] = t_cw
            L["b_ctxs%d" % (s_i % 2)] = t_cw
        L["b_x%d" % bi] = t_q

    for k in range(NSS):
        emit_sweep1(k)

    ctxl = sb(stk, nc, "b_ctxl", [128, 1024], F32)
    ctxq_f = sb(stk, nc, "b_cq", [128, 1024], F32)
    ctxqT = sb(stk, nc, "b_cqT", [128, 8, 128], BF16)
    out_sb = [sb(stk, nc, "b_out%d" % i, [128, NST, 1024], F32) for i in range(2)]

    x2_t = [None] * NSS

    def emit_sweep2(k):
        bi = k % 2
        sta, osb = stat[bi], out_sb[bi]
        b_i = k // SS_PER_SEQ
        gate = L.get("b_x2f%d" % bi) if k >= 2 else L.get("b_x%d" % bi)
        x2_t[k] = p.dma("sync", x_sb[bi][:], x_t[k], stream=bi, after=[gate])
        t_out = []
        for s_i in range(NST):
            a = 4 * (k % SS_PER_SEQ) + s_i
            base = 2048 * b_i
            src = ctx_dram[base:base + 2048, 64 * a:64 * (a + 1)]
            src = src.rearrange("(r j) d -> r j d", r=128)
            t_cl = p.dma("sync", ctxl[:].rearrange("p (j d) -> p j d", d=64), src,
                         sig="dx0",
                         after=[L.get("b_ctxl"),
                                ("dc0", p.cnt["dc0"]), ("dc1", p.cnt["dc1"])])
            rs2, trs2 = rms_stats(p, ctxl[:], sq_scr[:], sta, 12 * s_i, 1e-6,
                                  tok_in=[t_cl])
            t_xr = p.do("scalar", lambda E, rs2=rs2: E.activation(
                sq_scr[:], ctxl[:], AF.Identity, scale=rs2),
                after=[t_cl, L.get("b_sqw")], hard=[trs2])
            am = sta[:, 12 * s_i + 4:12 * s_i + 5]
            iv = sta[:, 12 * s_i + 5:12 * s_i + 6]
            osc = sta[:, 12 * s_i + 6:12 * s_i + 7]
            p.do("vector", lambda E, am=am: E.reduce_max(
                am, sq_scr[:], axis=AX.X, apply_absolute_value=True), after=[t_xr])
            p.do("vector", lambda E, am=am: E.tensor_scalar_max(am, am, 1e-5))
            p.do("vector", lambda E, am=am, iv=iv: E.reciprocal(iv, am))
            t_iv = p.do("vector", lambda E, iv=iv: E.tensor_scalar_mul(iv, iv, 127.0))
            t_os = p.do("vector", lambda E, am=am, osc=osc: E.scalar_tensor_tensor(
                osc, am, 1.0 / 127.0, mws["o"][:, 0:1], ALU.mult, ALU.mult),
                hard=[tern_t["o"][0]])
            t_z = p.do("scalar", lambda E, iv=iv: E.activation(
                ctxq_f[:], sq_scr[:], AF.Identity, scale=iv),
                after=[L.get("b_cqw")], hard=[t_iv])
            L["b_sqw"] = t_z
            L["b_ctxl"] = t_xr
            p.do("vector", lambda E: E.tensor_copy(com.i32scr[:, 0:1024], ctxq_f[:]),
                 after=[t_z, L.get("com_i32")])
            p.do("vector", lambda E: E.tensor_sub(com.rndB[:], ctxq_f[:],
                                                  com.i32scr[:, 0:1024]))
            p.do("vector", lambda E: E.scalar_tensor_tensor(
                com.rndA[:], com.rndB[:], 0.5, com.i32scr[:, 0:1024],
                ALU.is_gt, ALU.add))
            t_r = p.do("vector", lambda E: E.scalar_tensor_tensor(
                com.rndB[:], com.rndB[:], -0.5, com.rndA[:], ALU.is_ge, ALU.add))
            L["com_i32"] = t_r
            t_q = p.do("vector", lambda E: E.tensor_scalar_add(
                ctxq_f[:], com.rndB[:], -1.0), after=[L.get("b_cqw2")])
            L["b_cqw"] = t_q
            tk_cT = []
            for c in range(8):
                pt = psC[c % 2]
                t_tr = p.do("tensor", lambda E, c=c, pt=pt: E.transpose(
                    pt[:], ctxq_f[:, 128 * c:128 * (c + 1)], identf[:]),
                    after=[t_q, tok_identf, L.get("war_b_psC%d" % (c % 2))])
                t_ev = p.do("vector", lambda E, c=c, pt=pt: E.tensor_scalar_add(
                    ctxqT[:, c, :], pt[:], 0.0), after=[t_tr, L.get("b_cqTw")])
                L["war_b_psC%d" % (c % 2)] = t_ev
                tk_cT.append(t_ev)
            L["b_cqw2"] = t_tr
            t_ep = []
            for half in range(2):
                pg = psS[half]
                mm = None
                for c in range(8):
                    mm = p.do("tensor", lambda E, c=c, half=half, pg=pg: E.matmul(
                        pg[:, 0:512], ctxqT[:, c, :],
                        terns["o"][:, c, 512 * half:512 * (half + 1)],
                        start=(c == 0), stop=(c == 7)),
                        after=tk_cT + [L.get("war_b_psS%d" % half)])
                t_e = p.do("scalar", lambda E, s_i=s_i, half=half, pg=pg, osc=osc: E.activation(
                    osb[:, s_i, 512 * half:512 * (half + 1)], pg[:, 0:512],
                    AF.Identity, scale=osc),
                    after=[mm, L.get("b_outw%d" % bi)], hard=[t_os])
                L["war_b_psS%d" % half] = t_e
                L["b_cqTw"] = mm
                t_ep.append(t_e)
            rs3, nb3, tst3 = ln_stats(p, osb[:, s_i, :], sq_scr[:], sta,
                                      12 * s_i + 7, 1e-5, tok_in=t_ep)
            t_ln = p.do("vector", lambda E, s_i=s_i, rs3=rs3, nb3=nb3, osb=osb: E.tensor_scalar(
                osb[:, s_i, :], osb[:, s_i, :], rs3, nb3, ALU.mult, ALU.add),
                hard=[tst3])
            t_fin = p.do("vector", lambda E, s_i=s_i, osb=osb, bi=bi: E.tensor_add(
                osb[:, s_i, :], osb[:, s_i, :], x_sb[bi][:, s_i, :]),
                after=[x2_t[k]])
            t_out.append(t_fin)
        L["b_x2f%d" % bi] = t_out[-1]
        t_str = p.dma("gpsimd", xo_t[k], osb[:], stream=bi,
                      after=t_out + [L.get("b_outw%d" % bi)])
        L["b_outw%d" % bi] = t_str
        tok_store.append(t_str)

    for k in range(NSS):
        emit_sweep2(k)
    return tok_store



def _prep_inputs(inputs):
    f32 = np.float32

    def t(a):
        return np.ascontiguousarray(np.asarray(a, f32).T)

    def cols(a, nch):
        return np.ascontiguousarray(np.asarray(a, f32).reshape(nch, 128).T)

    g = {}
    g["x"] = np.ascontiguousarray(np.asarray(inputs["L0_attn_tokens"], f32).reshape(B, S, D))
    g["a_w_in_t"] = t(inputs["L0_attn_W_in"])
    g["a_w_out_t"] = t(inputs["L0_attn_W_out"])
    g["a_w_skip_t"] = t(inputs["L0_attn_W_skip"])
    g["a_norm_w"] = cols(inputs["L0_attn_norm_w"], 8)
    g["a_b_in"] = cols(inputs["L0_attn_b_in"], 2)
    g["a_b_out"] = np.ascontiguousarray(np.asarray(inputs["L0_attn_b_out"], f32)).reshape(1, 1024)
    g["a_b_skip"] = np.ascontiguousarray(np.asarray(inputs["L0_attn_b_skip"], f32)).reshape(1, 1024)
    g["a_log_decay"] = cols(inputs["L0_attn_log_decay"], 2)
    g["c_w1t"] = t(inputs["L0_ffn_w1"])
    g["c_w2t"] = t(inputs["L0_ffn_w2"])
    cw = np.asarray(inputs["L0_ffn_conv_w"], f32)[:, 0, :]
    g["c_cwd"] = np.ascontiguousarray(cw.reshape(32, 128, 3).transpose(1, 2, 0))
    g["c_cbd"] = cols(inputs["L0_ffn_conv_b"], 32)
    g["b_wqt"] = t(inputs["L1_attn_wq"])
    g["b_wkt"] = t(inputs["L1_attn_wk"])
    g["b_wvt"] = t(inputs["L1_attn_wv"])
    g["b_wot"] = t(inputs["L1_attn_wo"])
    g["f_u1d"] = np.ascontiguousarray(np.asarray(inputs["L1_ffn_U1"], f32))
    g["f_v1d"] = np.ascontiguousarray(np.asarray(inputs["L1_ffn_V1"], f32))
    g["f_u2d"] = np.ascontiguousarray(np.asarray(inputs["L1_ffn_U2"], f32))
    g["f_v2d"] = np.ascontiguousarray(np.asarray(inputs["L1_ffn_V2"], f32))
    g["f_b1d"] = cols(inputs["L1_ffn_b1"], 32)
    g["f_b2d"] = np.ascontiguousarray(np.asarray(inputs["L1_ffn_b2"], f32)).reshape(1, 1024)
    return g


def _l0_attn_dram(nc):
    d = {}
    d["w_in_t"] = nc.dram_tensor("a_w_in_t", [1024, 256], F32, kind="ExternalInput").ap()
    d["w_out_t"] = nc.dram_tensor("a_w_out_t", [256, 1024], F32, kind="ExternalInput").ap()
    d["w_skip_t"] = nc.dram_tensor("a_w_skip_t", [1024, 1024], F32, kind="ExternalInput").ap()
    d["norm_w"] = nc.dram_tensor("a_norm_w", [128, 8], F32, kind="ExternalInput").ap()
    d["b_in"] = nc.dram_tensor("a_b_in", [128, 2], F32, kind="ExternalInput").ap()
    d["b_out"] = nc.dram_tensor("a_b_out", [1, 1024], F32, kind="ExternalInput").ap()
    d["b_skip"] = nc.dram_tensor("a_b_skip", [1, 1024], F32, kind="ExternalInput").ap()
    d["log_decay"] = nc.dram_tensor("a_log_decay", [128, 2], F32, kind="ExternalInput").ap()
    return d


def _l0_ffn_dram(nc):
    d = {}
    d["w1_t"] = nc.dram_tensor("c_w1t", [1024, 4096], F32, kind="ExternalInput").ap()
    d["w2_t"] = nc.dram_tensor("c_w2t", [4096, 1024], F32, kind="ExternalInput").ap()
    d["conv_w"] = nc.dram_tensor("c_cwd", [128, 3, 32], F32, kind="ExternalInput").ap()
    d["conv_b"] = nc.dram_tensor("c_cbd", [128, 32], F32, kind="ExternalInput").ap()
    return d


def _l1_attn_dram(nc):
    d = {}
    for nm in ("wq", "wk", "wv", "wo"):
        d[nm + "_t"] = nc.dram_tensor("b_" + nm + "t", [1024, 1024], F32,
                                      kind="ExternalInput").ap()
    return d


def _l1_ffn_dram(nc):
    d = {}
    d["u1"] = nc.dram_tensor("f_u1d", [1024, 256], F32, kind="ExternalInput").ap()
    d["v1"] = nc.dram_tensor("f_v1d", [256, 4096], F32, kind="ExternalInput").ap()
    d["u2"] = nc.dram_tensor("f_u2d", [4096, 256], F32, kind="ExternalInput").ap()
    d["v2"] = nc.dram_tensor("f_v2d", [256, 1024], F32, kind="ExternalInput").ap()
    d["b1"] = nc.dram_tensor("f_b1d", [128, 32], F32, kind="ExternalInput").ap()
    d["b2"] = nc.dram_tensor("f_b2d", [1, 1024], F32, kind="ExternalInput").ap()
    return d


def _block_prog(which, debug=False):
    nc = bass.Bass("TRN2", target_bir_lowering=False)
    x_in = nc.dram_tensor("x", [TOK, 1024], F32, kind="ExternalInput").ap()
    x_out = nc.dram_tensor("y", [TOK, 1024], F32, kind="ExternalOutput").ap()
    p = Prog(nc)
    with contextlib.ExitStack() as stk:
        com = Common(p, stk)
        if which == "l0a":
            toks = build_l0_attn(p, stk, com, _l0_attn_dram(nc), x_in, x_out)
        elif which == "l0f":
            h2_dram = nc.dram_tensor("c_h2", [32, 128, TOK], BF16, kind="Internal").ap()
            toks = build_l0_ffn(p, stk, com, _l0_ffn_dram(nc), x_in, x_out, h2_dram)
        elif which == "l1a":
            ctx_dram = nc.dram_tensor("b_ctx", [TOK, 1024], F32, kind="Internal").ap()
            toks = build_l1_attn(p, stk, com, _l1_attn_dram(nc), x_in, x_out, ctx_dram)
        elif which == "l1f":
            toks = build_l1_ffn(p, stk, com, _l1_ffn_dram(nc), x_in, x_out)
        p.wait_all("sync", toks + [("dw", p.cnt["dw"])])
        p.build()
    return nc


def build_fused_prog():
    nc = bass.Bass("TRN2", target_bir_lowering=False)
    x_in = nc.dram_tensor("x", [TOK, 1024], F32, kind="ExternalInput").ap()
    y = nc.dram_tensor("y", [TOK, 1024], F32, kind="ExternalOutput").ap()
    x1 = nc.dram_tensor("x1", [TOK, 1024], F32, kind="Internal").ap()
    x2 = nc.dram_tensor("x2", [TOK, 1024], F32, kind="Internal").ap()
    x3 = nc.dram_tensor("x3", [TOK, 1024], F32, kind="Internal").ap()
    h2_dram = nc.dram_tensor("c_h2", [32, 128, TOK], BF16, kind="Internal").ap()
    ctx_dram = nc.dram_tensor("b_ctx", [TOK, 1024], F32, kind="Internal").ap()
    da, dc = _l0_attn_dram(nc), _l0_ffn_dram(nc)
    db, df = _l1_attn_dram(nc), _l1_ffn_dram(nc)
    p = Prog(nc)
    with contextlib.ExitStack() as stk:
        com = Common(p, stk)
        s0 = stk.enter_context(contextlib.ExitStack())
        build_l0_attn(p, s0, com, da, x_in, x1)
        s0.close()
        p.barrier()
        s1 = stk.enter_context(contextlib.ExitStack())
        build_l0_ffn(p, s1, com, dc, x1, x2, h2_dram)
        s1.close()
        p.barrier()
        s2 = stk.enter_context(contextlib.ExitStack())
        build_l1_attn(p, s2, com, db, x2, x3, ctx_dram)
        s2.close()
        p.barrier()
        s3 = stk.enter_context(contextlib.ExitStack())
        toks = build_l1_ffn(p, s3, com, df, x3, y)
        s3.close()
        p.wait_all("sync", toks + [("dw", p.cnt["dw"])])
        p.build()
    return nc


_KEYS = {
    "l0a": ["a_w_in_t", "a_w_out_t", "a_w_skip_t", "a_norm_w", "a_b_in",
            "a_b_out", "a_b_skip", "a_log_decay"],
    "l0f": ["c_w1t", "c_w2t", "c_cwd", "c_cbd"],
    "l1a": ["b_wqt", "b_wkt", "b_wvt", "b_wot"],
    "l1f": ["f_u1d", "f_v1d", "f_u2d", "f_v2d", "f_b1d", "f_b2d"],
}


def kernel(**inputs):
    g = _prep_inputs(inputs)
    xs = [np.ascontiguousarray(g["x"][BPC * ci:BPC * (ci + 1)].reshape(TOK, D))
          for ci in range(NCORE)]
    nc = build_fused_prog()
    in_maps = []
    for ci in range(NCORE):
        m = {k: g[k] for ks in _KEYS.values() for k in ks}
        m["x"] = xs[ci]
        in_maps.append(m)
    res = run_bass_kernel_spmd(nc, in_maps, core_ids=list(range(NCORE)))
    return np.concatenate([r["y"].reshape(BPC, S, D) for r in res.results],
                          axis=0).reshape(B, S, D)


# version 05
# speedup vs baseline: 1.0241x; 1.0223x over previous
import contextlib
import numpy as np
import concourse.bass as bass
import concourse.mybir as mybir
from concourse.bass_utils import run_bass_kernel_spmd

F32 = mybir.dt.float32
BF16 = mybir.dt.bfloat16
I32 = mybir.dt.int32
AF = mybir.ActivationFunctionType
ALU = mybir.AluOpType
AX = mybir.AxisListType

NCORE = 8
B, S, D = 16, 2048, 1024
BPC = B // NCORE
TOK = BPC * S
SS = 512
NSS = TOK // SS
NST = SS // 128
SS_PER_SEQ = S // SS


ENG_SEM = {"sync": "dl0", "gpsimd": "g", "tensor": "t", "vector": "v", "scalar": "s"}


class Prog:

    def __init__(self, nc):
        self.nc = nc
        self.ops = {e: [] for e in ("sync", "gpsimd", "tensor", "vector", "scalar")}
        self.cnt = {"dw": 0, "dl0": 0, "dl1": 0, "ds0": 0, "ds1": 0,
                    "dx0": 0, "dx1": 0, "dc0": 0, "dc1": 0, "dbg": 0, "t": 0, "v": 0, "s": 0, "g": 0}
        self.sems = {}
        self.last = {}
        self.prev = {}

    def do(self, eng, build, after=(), hard=(), sig=None, amt=1, chain=True):
        if sig is None:
            sig = ENG_SEM[eng]
        hard = list(hard)
        if chain and eng in ("vector", "scalar", "gpsimd"):
            prev = self.prev.get(eng)
            if prev is not None:
                hard.append(prev)
        self.cnt[sig] += amt
        val = self.cnt[sig]
        if amt == 0:
            val = None
        own = ENG_SEM[eng] if eng not in ("sync", "gpsimd") else "-"
        seen = {}
        for t in list(after):
            if t is None or t[0] == own:
                continue
            if seen.get(t[0], -1) < t[1]:
                seen[t[0]] = t[1]
        for t in list(hard):
            if t is None:
                continue
            if seen.get(t[0], -1) < t[1]:
                seen[t[0]] = t[1]
        waits = sorted(seen.items())

        def run(E, waits=waits, build=build, sig=sig, amt=amt):
            for sname, v in waits:
                E.wait_ge(self.sems[sname], v)
            ins = build(E)
            if amt and ins is not None:
                ins.then_inc(self.sems[sig], amt)
            elif amt:
                E.sem_inc(self.sems[sig], amt)

        self.ops[eng].append(run)
        tok = (sig, val) if val is not None else None
        if eng in ("vector", "scalar", "gpsimd") and val is not None and sig in ("v", "s", "g"):
            self.prev[eng] = tok
        return tok

    def dma(self, eng, out, in_, after=(), stream=0, sig=None, **kw):
        if sig is None:
            sig = ("dl%d" if eng == "sync" else "ds%d") % stream
        return self.do(eng, lambda E: E.dma_start(out=out, in_=in_, **kw),
                       after=after, sig=sig, amt=16)

    def wait_all(self, eng, toks):
        def run(E, toks=[t for t in toks if t is not None]):
            seen = {}
            for sname, v in toks:
                if seen.get(sname, -1) < v:
                    seen[sname] = v
            for sname, v in sorted(seen.items()):
                E.wait_ge(self.sems[sname], v)
        self.ops[eng].append(run)

    def barrier(self):
        snap = [(k, v) for k, v in self.cnt.items() if v > 0]
        for eng in self.ops:
            own = ENG_SEM[eng]
            self.wait_all(eng, [(k, v) for k, v in snap if k != own])

    def build(self):
        nc = self.nc
        with contextlib.ExitStack() as st:
            for name in self.cnt:
                self.sems[name] = st.enter_context(nc.semaphore(name))
            with nc.Block() as block:
                for eng in self.ops:
                    def make(eng):
                        def body(E):
                            for run in self.ops[eng]:
                                run(E)
                        return body
                    getattr(block, eng)(make(eng))


def sb(stk, nc, name, shape, dt):
    return stk.enter_context(nc.sbuf_tensor(name, shape, dt))


def ps(stk, nc, name, shape, dt):
    return stk.enter_context(nc.psum_tensor(name, shape, dt))


class Common:

    def __init__(self, p, stk):
        nc = p.nc
        self.ident = sb(stk, nc, "ident", [128, 128], BF16)
        self.onesrow_f = sb(stk, nc, "onesrow_f", [1, 512], F32)
        self.onesrow_b = sb(stk, nc, "onesrow_b", [1, 128], BF16)
        self.onescol_f = sb(stk, nc, "onescol_f", [128, 1], F32)
        self.rowscratch = sb(stk, nc, "rowscratch", [1, 16], F32)
        self.i32scr = sb(stk, nc, "i32scr", [128, 1024], I32)
        self.rowacc = sb(stk, nc, "rowacc", [128, 2], F32)
        self.rndA = sb(stk, nc, "rndA", [128, 1024], F32)
        self.rndB = sb(stk, nc, "rndB", [128, 1024], F32)
        t0 = p.do("vector", lambda E: E.memset(self.ident[:], 1.0))
        p.do("vector", lambda E: E.memset(self.onescol_f[:], 1.0))
        p.do("vector", lambda E: E.memset(self.onesrow_b[:], 1.0))
        self.tok_ones = p.do("vector", lambda E: E.memset(self.onesrow_f[:], 1.0))
        self.tok_ident = p.do(
            "gpsimd",
            lambda E: E.affine_select(self.ident[:], self.ident[:], [[-1, 128]],
                                      ALU.is_equal, 0.0, base=0, channel_multiplier=1),
            after=[t0])


def ln_stats(p, x_ap, sq_scr, stat, c0, eps, tok_in=(), n=1024.0, expln=False):
    su, sq, rs, nb = (stat[:, c0 + i:c0 + i + 1] for i in range(4))
    p.do("vector", lambda E: E.reduce_sum(su, x_ap, axis=AX.X), after=tok_in)
    p.do("vector", lambda E: E.scalar_tensor_tensor(
        sq_scr, x_ap, 1.0, x_ap, ALU.mult, ALU.mult, accum_out=sq), after=tok_in)
    p.do("vector", lambda E: E.tensor_scalar_mul(su, su, 1.0 / n))
    p.do("vector", lambda E: E.tensor_scalar_mul(sq, sq, 1.0 / n))
    p.do("vector", lambda E: E.scalar_tensor_tensor(rs, su, 1.0, su, ALU.mult, ALU.mult))
    p.do("vector", lambda E: E.tensor_sub(sq, sq, rs))
    t7 = p.do("vector", lambda E: E.tensor_scalar_add(sq, sq, eps))
    if expln:
        t8 = p.do("scalar", lambda E: E.activation(rs, sq, AF.Ln), after=[t7])
        t9 = p.do("scalar", lambda E: E.activation(rs, rs, AF.Exp, scale=-0.5), after=[t8])
        t10 = p.do("vector", lambda E: E.scalar_tensor_tensor(
            nb, su, -1.0, rs, ALU.mult, ALU.mult), hard=[t9])
    else:
        t8 = p.do("scalar", lambda E: E.activation(rs, sq, AF.Sqrt), after=[t7])
        p.do("vector", lambda E: E.reciprocal(rs, rs), after=[t8])
        t10 = p.do("vector", lambda E: E.scalar_tensor_tensor(
            nb, su, -1.0, rs, ALU.mult, ALU.mult))
    return rs, nb, t10


def rms_stats(p, x_ap, sq_scr, stat, c0, eps, tok_in=(), n=1024.0, expln=False):
    sq, rs = stat[:, c0:c0 + 1], stat[:, c0 + 1:c0 + 2]
    p.do("vector", lambda E: E.scalar_tensor_tensor(
        sq_scr, x_ap, 1.0, x_ap, ALU.mult, ALU.mult, accum_out=sq), after=tok_in)
    t2 = p.do("vector", lambda E: E.tensor_scalar(sq, sq, 1.0 / n, eps, ALU.mult, ALU.add))
    if expln:
        t3 = p.do("scalar", lambda E: E.activation(rs, sq, AF.Ln), after=[t2])
        t4 = p.do("scalar", lambda E: E.activation(rs, rs, AF.Exp, scale=-0.5), after=[t3])
        return rs, t4
    t3 = p.do("scalar", lambda E: E.activation(rs, sq, AF.Sqrt), after=[t2])
    t4 = p.do("vector", lambda E: E.reciprocal(rs, rs), after=[t3])
    return rs, t4



def build_l0_attn(p, stk, com, dram, x_in, x_out, tok_x_in=None, dbg=None):
    nc = p.nc
    L = p.last

    wstage = sb(stk, nc, "a_wstage", [128, 2, 1024], F32)
    winT = sb(stk, nc, "a_winT", [128, 8, 256], BF16)
    woutT = sb(stk, nc, "a_woutT", [128, 2, 1024], BF16)
    wskipT = sb(stk, nc, "a_wskipT", [128, 8, 1024], BF16)
    normw = sb(stk, nc, "a_normw", [128, 8], F32)
    b_in_c = sb(stk, nc, "a_binc", [128, 2], F32)
    brow = sb(stk, nc, "a_brow", [1, 2048], F32)
    bias_rep = sb(stk, nc, "a_biasrep", [128, 1024], F32)
    dec_c = sb(stk, nc, "a_decc", [128, 2], F32)
    dec_s = sb(stk, nc, "a_decs", [128, 2], F32)
    decseg = [sb(stk, nc, "a_decseg%d" % i, [128, 256], F32) for i in range(2)]

    psU = [ps(stk, nc, "a_psU%d" % i, [128, 512], F32) for i in range(2)]
    psT = [ps(stk, nc, "a_psT%d" % i, [128, 128], BF16) for i in range(2)]
    psY = [ps(stk, nc, "a_psY%d" % i, [128, 512], F32) for i in range(4)]

    t_nw = p.dma("sync", normw[:], dram["norm_w"][:], sig="dw")
    t_bi = p.dma("sync", b_in_c[:], dram["b_in"][:], sig="dw")
    t_bo = p.dma("sync", brow[0:1, 0:1024], dram["b_out"][:], sig="dw")
    t_bs = p.dma("sync", brow[0:1, 1024:2048], dram["b_skip"][:], sig="dw")
    t_ld = p.dma("sync", dec_c[:], dram["log_decay"][:], sig="dw")
    dwtot = ("dw", p.cnt["dw"])
    t_nw = t_bi = t_bo = t_bs = t_ld = dwtot

    win_r = dram["w_in_t"].rearrange("(c p) n -> c p n", p=128)
    wsk_r = dram["w_skip_t"].rearrange("(c p) e -> c p e", p=128)
    wout_r = dram["w_out_t"].rearrange("(c p) e -> c p e", p=128)
    tk_win, tk_wsk, tk_wout = [], [], []
    for c in range(8):
        sl = c % 2
        t_w = p.dma("sync", wstage[:, sl, 0:256], win_r[c], stream=sl,
                    after=[tk_win[-2] if c >= 2 else None])
        tk_win.append(p.do("vector", lambda E, c=c, sl=sl: E.tensor_scalar_mul(
            winT[:, c, :], wstage[:, sl, 0:256], normw[:, c:c + 1]), after=[dwtot, t_w]))
    for c in range(8):
        sl = c % 2
        t_w = p.dma("sync", wstage[:, sl, :], wsk_r[c], stream=sl,
                    after=[tk_wsk[-2] if c >= 2 else tk_win[6 + c]])
        tk_wsk.append(p.do("vector", lambda E, c=c, sl=sl: E.tensor_scalar_mul(
            wskipT[:, c, :], wstage[:, sl, :], normw[:, c:c + 1]), after=[t_w], hard=[dwtot]))
    for c in range(2):
        sl = c % 2
        t_w = p.dma("sync", wstage[:, sl, :], wout_r[c], stream=sl,
                    after=[tk_wsk[6 + c]])
        tk_wout.append(p.do("vector", lambda E, c=c, sl=sl: E.tensor_scalar_add(
            woutT[:, c, :], wstage[:, sl, :], 0.0), after=[t_w]))

    t_ba = p.do("vector", lambda E: E.tensor_add(
        brow[0:1, 0:1024], brow[0:1, 0:1024], brow[0:1, 1024:2048]), after=[dwtot])
    tb0 = p.do("tensor", lambda E: E.matmul(
        psY[0][:], com.onesrow_f[0:1, 0:128], brow[0:1, 0:512], start=True, stop=True),
        after=[t_ba, com.tok_ones])
    tb1 = p.do("tensor", lambda E: E.matmul(
        psY[1][:], com.onesrow_f[0:1, 0:128], brow[0:1, 512:1024], start=True, stop=True),
        after=[t_ba, com.tok_ones])
    tb_e0 = p.do("vector", lambda E: E.tensor_copy(bias_rep[:, 0:512], psY[0][:]), after=[tb0])
    tb_e1 = p.do("vector", lambda E: E.tensor_copy(bias_rep[:, 512:1024], psY[1][:]), after=[tb1])
    L["a_psY0"] = tb_e0
    L["a_psY1"] = tb_e1
    t_sg = p.do("scalar", lambda E: E.activation(dec_s[:], dec_c[:], AF.Sigmoid), after=[dwtot])
    t_ds = []
    for i in range(2):
        t_ms = p.do("vector", lambda E, i=i: E.memset(decseg[i][:], 1.0))
        t_ds.append(p.do("vector", lambda E, i=i: E.tensor_scalar_mul(
            decseg[i][:], decseg[i][:], dec_s[:, i:i + 1]), after=[t_sg], hard=[t_sg, t_ms]))


    x_sb = [sb(stk, nc, "a_x%d" % i, [128, NST, 1024], F32) for i in range(2)]
    tok_sb = sb(stk, nc, "a_tok", [128, NST, 1024], F32)
    xr_bf = sb(stk, nc, "a_xr", [128, NST, 1024], BF16)
    xrT = [sb(stk, nc, "a_xrT%d" % i, [128, 8, 512], BF16) for i in range(2)]
    useg = [[sb(stk, nc, "a_useg%d_%d" % (i, j), [128, 256], F32) for j in range(4)]
            for i in range(2)]
    stseg = [[sb(stk, nc, "a_stseg%d_%d" % (i, j), [128, 256], F32) for j in range(4)]
             for i in range(2)]
    stbf = [[sb(stk, nc, "a_stbf%d_%d" % (i, j), [128, 256], BF16) for j in range(4)]
            for i in range(2)]
    out_sb = [sb(stk, nc, "a_out%d" % i, [128, NST, 1024], F32) for i in range(2)]
    sq_scr = sb(stk, nc, "a_sq", [128, 1024], F32)
    stat = [sb(stk, nc, "a_stat%d" % i, [128, 48], F32) for i in range(2)]
    carry = [sb(stk, nc, "a_carry%d" % i, [128, 1], F32) for i in range(2)]

    x_t = x_in.rearrange("(k s p) d -> k p s d", p=128, s=NST)
    xo_t = x_out.rearrange("(k s p) d -> k p s d", p=128, s=NST)

    tok_store = []
    t_load = [None] * NSS
    t_load[0] = p.dma("sync", x_sb[0][:], x_t[0], stream=0,
                      after=[tok_x_in, tk_wout[0]])

    def emit_ss(k):
        bi = k % 2
        xsb, xrt, osb, sta = x_sb[bi], xrT[bi], out_sb[bi], stat[bi]
        use, sts, stb = useg[bi], stseg[bi], stbf[bi]
        if k + 1 < NSS:
            gate = L.get("a_x%d" % ((k + 1) % 2)) if k >= 1 else tk_wout[1]
            t_load[k + 1] = p.dma("sync", x_sb[(k + 1) % 2][:], x_t[k + 1],
                                  stream=(k + 1) % 2, after=[gate, tok_x_in])
        tk_xrT = []
        t_tr = None
        for s_i in range(NST):
            xs = xsb[:, s_i, :]
            rs, nb, tst = ln_stats(p, xs, sq_scr[:], sta, 12 * s_i, 1e-5,
                                   tok_in=[t_load[k], L.get("a_stat%d_%d" % (bi, s_i))])
            t_tok = p.do("scalar", lambda E, xs=xs, s_i=s_i, rs=rs, nb=nb: E.activation(
                tok_sb[:, s_i, :], xs, AF.Identity, bias=nb, scale=rs),
                after=[t_load[k], L.get("a_tok")], hard=[tst])
            rs2, trs2 = rms_stats(p, tok_sb[:, s_i, :], sq_scr[:], sta, 12 * s_i + 4,
                                  1e-4, tok_in=[t_tok])
            t_cast = p.do("scalar", lambda E, s_i=s_i, rs2=rs2: E.activation(
                xr_bf[:, s_i, :], tok_sb[:, s_i, :], AF.Identity, scale=rs2),
                after=[L.get("a_xr")], hard=[trs2])
            L["a_stat%d_%d" % (bi, s_i)] = t_cast
            for c in range(8):
                pt = psT[c % 2]
                t_tr = p.do("tensor", lambda E, c=c, s_i=s_i, pt=pt: E.transpose(
                    pt[:], xr_bf[:, s_i, 128 * c:128 * (c + 1)], com.ident[:]),
                    after=[t_cast, com.tok_ident, L.get("a_psT%d" % (c % 2))])
                t_ev = p.do("vector", lambda E, c=c, s_i=s_i, pt=pt, xrt=xrt: E.tensor_copy(
                    xrt[:, c, 128 * s_i:128 * (s_i + 1)], pt[:]),
                    after=[t_tr, L.get("a_xrT%d" % bi)])
                L["a_psT%d" % (c % 2)] = t_ev
                tk_xrT.append(t_ev)
        L["a_xr"] = t_tr

        t_uev = {}
        for nch in range(2):
            mm = None
            for c in range(8):
                mm = p.do("tensor", lambda E, nch=nch, c=c, xrt=xrt: E.matmul(
                    psU[nch][:], winT[:, c, 128 * nch:128 * (nch + 1)], xrt[:, c, :],
                    start=(c == 0), stop=(c == 7)),
                    after=tk_xrT + [L.get("a_psU%d" % nch)] + tk_win)
            for sg in range(2):
                t_uev[(nch, sg)] = p.do("scalar", lambda E, nch=nch, sg=sg, use=use: E.activation(
                    use[2 * nch + sg][:], psU[nch][:, 256 * sg:256 * (sg + 1)],
                    AF.Identity, bias=b_in_c[:, nch:nch + 1], scale=1.0),
                    after=[mm, dwtot, L.get("a_useg%d_%d" % (bi, 2 * nch + sg))])
            L["a_psU%d" % nch] = t_uev[(nch, 1)]

        seq_start = (k % SS_PER_SEQ) == 0
        t_sc = {}
        for nch in range(2):
            for sg in range(2):
                j = 2 * nch + sg
                if sg == 0 and seq_start:
                    init, hard_init = 0.0, []
                else:
                    init, hard_init = carry[nch][:, 0:1], [L.get("a_carrywr%d" % nch)]
                t_sc[j] = p.do("vector", lambda E, j=j, nch=nch, init=init, sts=sts, use=use: (
                    E.tensor_tensor_scan(sts[j][:], decseg[nch][:],
                                         use[j][:], init, ALU.mult, ALU.add)),
                    after=[t_uev[(nch, sg)], t_ds[nch]], hard=hard_init)
                L["a_useg%d_%d" % (bi, j)] = t_sc[j]
                L["a_carrywr%d" % nch] = p.do(
                    "vector", lambda E, j=j, nch=nch, sts=sts: E.tensor_copy(
                        carry[nch][:], sts[j][:, 255:256]))
        t_st = {}
        for j in range(4):
            t_st[j] = p.do("vector", lambda E, j=j, stb=stb, sts=sts: E.tensor_scalar_add(
                stb[j][:], sts[j][:], 0.0), after=[t_sc[j], L.get("a_ymm%d" % bi)])

        t_out = []
        last_mm = None
        t_inner = None
        for s_i in range(NST):
            t_ep = []
            for half in range(2):
                pyk = 2 * (s_i % 2) + half
                pY = psY[pyk]
                mm = None
                for nch in range(2):
                    j = 2 * nch + (s_i // 2)
                    off = 128 * (s_i % 2)
                    mm = p.do("tensor", lambda E, j=j, nch=nch, off=off, half=half, pY=pY, stb=stb: E.matmul(
                        pY[:], stb[j][:, off:off + 128],
                        woutT[:, nch, 512 * half:512 * (half + 1)],
                        start=(nch == 0), stop=False),
                        after=[t_st[j], L.get("a_psY%d" % pyk)] + tk_wout)
                for c in range(8):
                    mm = p.do("tensor", lambda E, c=c, s_i=s_i, half=half, pY=pY, xrt=xrt: E.matmul(
                        pY[:], xrt[:, c, 128 * s_i:128 * (s_i + 1)],
                        wskipT[:, c, 512 * half:512 * (half + 1)],
                        start=False, stop=(c == 7)),
                        after=tk_xrT + tk_wsk)
                last_mm = mm
                t_e = p.do("vector", lambda E, s_i=s_i, half=half, pY=pY, osb=osb: E.tensor_add(
                    osb[:, s_i, 512 * half:512 * (half + 1)], pY[:],
                    bias_rep[:, 512 * half:512 * (half + 1)]),
                    after=[mm, tb_e0, tb_e1, L.get("a_out%d" % bi)])
                L["a_psY%d" % pyk] = t_e
                t_ep.append(t_e)
            t_inner = p.do("vector", lambda E, s_i=s_i, osb=osb: E.tensor_add(
                osb[:, s_i, :], osb[:, s_i, :], tok_sb[:, s_i, :]),
                after=t_ep)
            rs3, nb3, tst3 = ln_stats(p, osb[:, s_i, :], sq_scr[:], sta,
                                      12 * s_i + 6, 1e-5, tok_in=[t_inner])
            t_ln = p.do("vector", lambda E, s_i=s_i, rs3=rs3, nb3=nb3, osb=osb: E.tensor_scalar(
                osb[:, s_i, :], osb[:, s_i, :], rs3, nb3, ALU.mult, ALU.add),
                hard=[tst3])
            t_fin = p.do("vector", lambda E, s_i=s_i, osb=osb, xsb=xsb: E.tensor_add(
                osb[:, s_i, :], osb[:, s_i, :], xsb[:, s_i, :]))
            t_out.append(t_fin)
        L["a_ymm%d" % bi] = last_mm
        L["a_xrT%d" % bi] = last_mm
        L["a_tok"] = t_inner
        L["a_x%d" % bi] = t_out[-1]
        if k == 0 and dbg is not None:
            p.dma("sync", dbg["tok"][:], tok_sb[:], sig="dbg", after=[t_inner])
            p.dma("sync", dbg["xr"][:], xr_bf[:], sig="dbg", after=[L["a_xr"]])
            p.dma("sync", dbg["xrT"][:], xrt[:], sig="dbg", after=[L.get("a_psU1")])
            p.dma("sync", dbg["useg"][:], use[0][:], sig="dbg", after=[t_sc[0]])
            p.dma("sync", dbg["stseg"][:], sts[0][:], sig="dbg", after=[t_st[0]])
            p.dma("sync", dbg["dec"][:], decseg[0][:], sig="dbg", after=[t_ds[0]])
            p.dma("sync", dbg["decs"][:], dec_s[:], sig="dbg", after=[t_ds[0]])
        t_str = p.dma("gpsimd", xo_t[k], osb[:], stream=bi, after=t_out)
        L["a_out%d" % bi] = t_str
        tok_store.append(t_str)

    for k in range(NSS):
        emit_ss(k)
    return tok_store



def build_l1_ffn(p, stk, com, dram, x_in, x_out, tok_x_in=None):
    nc = p.nc
    L = p.last

    stage = sb(stk, nc, "f_stage", [128, 8192], F32)
    u1b = sb(stk, nc, "f_u1", [128, 8, 256], BF16)
    v1b = sb(stk, nc, "f_v1", [128, 2, 4096], BF16)
    u2b = sb(stk, nc, "f_u2", [128, 32, 256], BF16)
    v2b = sb(stk, nc, "f_v2", [128, 2, 1024], BF16)
    b1c = sb(stk, nc, "f_b1c", [128, 32], F32)
    brow = sb(stk, nc, "f_brow", [1, 1024], F32)
    b2rep = sb(stk, nc, "f_b2rep", [128, 1024], F32)

    psP = ps(stk, nc, "f_psP", [128, 512], F32)
    psH = [ps(stk, nc, "f_psH%d" % i, [128, 512], F32) for i in range(2)]
    psQ = ps(stk, nc, "f_psQ", [128, 512], F32)
    psO = [ps(stk, nc, "f_psO%d" % i, [128, 512], F32) for i in range(2)]
    psT = [ps(stk, nc, "f_psT%d" % i, [128, 128], BF16) for i in range(2)]

    t_b1 = p.dma("sync", b1c[:], dram["b1"][:], sig="dw")
    t_b2 = p.dma("sync", brow[:], dram["b2"][:], sig="dw")
    dwtot = ("dw", p.cnt["dw"])

    def stg(nelem):
        return stage[:, 0:nelem]

    t_w = p.dma("sync", stg(2048).rearrange("p (c n) -> p c n", n=256),
                dram["u1"].rearrange("(c p) n -> p c n", p=128))
    c_u1 = p.do("vector", lambda E: E.tensor_scalar_add(
        u1b[:], stg(2048).rearrange("p (c n) -> p c n", n=256), 0.0), after=[t_w])
    t_w = p.dma("sync", stg(8192).rearrange("p (c f) -> p c f", f=4096),
                dram["v1"].rearrange("(c p) f -> p c f", p=128), after=[c_u1])
    c_v1 = p.do("vector", lambda E: E.tensor_scalar_add(
        v1b[:], stg(8192).rearrange("p (c f) -> p c f", f=4096), 0.0), after=[t_w])
    t_w = p.dma("sync", stg(8192).rearrange("p (c n) -> p c n", n=256),
                dram["u2"].rearrange("(c p) n -> p c n", p=128), after=[c_v1])
    c_u2 = p.do("vector", lambda E: E.tensor_scalar_add(
        u2b[:], stg(8192).rearrange("p (c n) -> p c n", n=256), 0.0), after=[t_w])
    t_w = p.dma("sync", stg(2048).rearrange("p (c f) -> p c f", f=1024),
                dram["v2"].rearrange("(c p) f -> p c f", p=128), after=[c_u2])
    c_v2 = p.do("vector", lambda E: E.tensor_scalar_add(
        v2b[:], stg(2048).rearrange("p (c f) -> p c f", f=1024), 0.0), after=[t_w])
    tb0 = p.do("tensor", lambda E: E.matmul(
        psO[0][:], com.onesrow_f[0:1, 0:128], brow[0:1, 0:512], start=True, stop=True),
        after=[dwtot, com.tok_ones])
    tb1 = p.do("tensor", lambda E: E.matmul(
        psO[1][:], com.onesrow_f[0:1, 0:128], brow[0:1, 512:1024], start=True, stop=True),
        after=[dwtot, com.tok_ones])
    tb_e0 = p.do("vector", lambda E: E.tensor_copy(b2rep[:, 0:512], psO[0][:]), after=[tb0])
    tb_e1 = p.do("vector", lambda E: E.tensor_copy(b2rep[:, 512:1024], psO[1][:]), after=[tb1])
    L["f_psO0"] = tb_e0
    L["f_psO1"] = tb_e1

    x_sb = [sb(stk, nc, "f_x%d" % i, [128, NST, 1024], F32) for i in range(2)]
    xn_bf = sb(stk, nc, "f_xn", [128, NST, 1024], BF16)
    xnT1 = sb(stk, nc, "f_xnT", [128, 8, 512], BF16)
    xnT = [xnT1, xnT1]
    p_sb = sb(stk, nc, "f_p", [128, NST, 256], BF16)
    p_fm = [sb(stk, nc, "f_pfm%d" % i, [128, 2, 512], BF16) for i in range(2)]
    h_sb = sb(stk, nc, "f_h", [128, 32, 512], BF16)
    q_fm = sb(stk, nc, "f_qfm", [128, 2, 512], BF16)
    out_sb1 = sb(stk, nc, "f_out", [128, NST, 1024], F32)
    out_sb = [out_sb1, out_sb1]
    sq_scr = sb(stk, nc, "f_sq", [128, 1024], F32)
    stat = [sb(stk, nc, "f_stat%d" % i, [128, 48], F32) for i in range(2)]

    x_t = x_in.rearrange("(k s p) d -> k p s d", p=128, s=NST)
    xo_t = x_out.rearrange("(k s p) d -> k p s d", p=128, s=NST)

    tok_store = []
    t_load = [None] * NSS
    t_load[0] = p.dma("sync", x_sb[0][:], x_t[0], stream=0, after=[tok_x_in, c_v2])

    def emit_ss(k):
        bi = k % 2
        xsb, osb, sta, xnt, pfm = x_sb[bi], out_sb[bi], stat[bi], xnT[bi], p_fm[bi]
        if k + 1 < NSS:
            gate = L.get("f_x%d" % ((k + 1) % 2)) if k >= 1 else c_v2
            t_load[k + 1] = p.dma("sync", x_sb[(k + 1) % 2][:], x_t[k + 1],
                                  stream=(k + 1) % 2, after=[gate, tok_x_in])
        tk_xnT = []
        t_tr = None
        for s_i in range(NST):
            xs = xsb[:, s_i, :]
            rs, nb, tst = ln_stats(p, xs, sq_scr[:], sta, 12 * s_i, 1e-5,
                                   tok_in=[t_load[k], L.get("f_stat%d_%d" % (bi, s_i))])
            t_xn = p.do("scalar", lambda E, xs=xs, s_i=s_i, rs=rs, nb=nb: E.activation(
                xn_bf[:, s_i, :], xs, AF.Identity, bias=nb, scale=rs),
                after=[t_load[k], L.get("f_xn")], hard=[tst])
            L["f_stat%d_%d" % (bi, s_i)] = t_xn
            for c in range(8):
                pt = psT[c % 2]
                t_tr = p.do("tensor", lambda E, c=c, s_i=s_i, pt=pt: E.transpose(
                    pt[:], xn_bf[:, s_i, 128 * c:128 * (c + 1)], com.ident[:]),
                    after=[t_xn, com.tok_ident, L.get("f_psT%d" % (c % 2))])
                t_ev = p.do("vector", lambda E, c=c, s_i=s_i, pt=pt, xnt=xnt: E.tensor_copy(
                    xnt[:, c, 128 * s_i:128 * (s_i + 1)], pt[:]),
                    after=[t_tr, L.get("f_xnT")])
                L["f_psT%d" % (c % 2)] = t_ev
                tk_xnT.append(t_ev)
        L["f_xn"] = t_tr

        t_pev = []
        for s_i in range(NST):
            mm = None
            for c in range(8):
                mm = p.do("tensor", lambda E, c=c, s_i=s_i, xnt=xnt: E.matmul(
                    psP[:, 0:256], xnt[:, c, 128 * s_i:128 * (s_i + 1)], u1b[:, c, :],
                    start=(c == 0), stop=(c == 7)),
                    after=tk_xnT + [L.get("f_psP"), c_u1])
            t_pe = p.do("vector", lambda E, s_i=s_i: E.tensor_scalar_add(
                p_sb[:, s_i, :], psP[:, 0:256], 0.0), after=[mm, L.get("f_p")])
            L["f_psP"] = t_pe
            L["f_xnT"] = mm
            t_pev.append(t_pe)
        tk_pfm = []
        for s_i in range(NST):
            for rc in range(2):
                pt = psT[rc % 2]
                t_tr = p.do("tensor", lambda E, s_i=s_i, rc=rc, pt=pt: E.transpose(
                    pt[:], p_sb[:, s_i, 128 * rc:128 * (rc + 1)], com.ident[:]),
                    after=[t_pev[s_i], com.tok_ident, L.get("f_psT%d" % (rc % 2))])
                t_ev = p.do("vector", lambda E, s_i=s_i, rc=rc, pt=pt, pfm=pfm: E.tensor_copy(
                    pfm[:, rc, 128 * s_i:128 * (s_i + 1)], pt[:]),
                    after=[t_tr, L.get("f_pfm%d" % bi)])
                L["f_psT%d" % (rc % 2)] = t_ev
                tk_pfm.append(t_ev)
        L["f_p"] = t_tr

        t_hev = []
        for hc in range(32):
            ph = psH[hc % 2]
            mm = None
            for rc in range(2):
                mm = p.do("tensor", lambda E, hc=hc, rc=rc, ph=ph, pfm=pfm: E.matmul(
                    ph[:], v1b[:, rc, 128 * hc:128 * (hc + 1)], pfm[:, rc, :],
                    start=(rc == 0), stop=(rc == 1)),
                    after=tk_pfm + [L.get("f_psH%d" % (hc % 2)), c_v1])
            t_h = p.do("scalar", lambda E, hc=hc, ph=ph: E.activation(
                h_sb[:, hc, :], ph[:], AF.Gelu_apprx_tanh, bias=b1c[:, hc:hc + 1], scale=1.0),
                after=[mm, dwtot, L.get("f_h")])
            L["f_psH%d" % (hc % 2)] = t_h
            t_hev.append(t_h)

        t_qev = []
        for rc in range(2):
            mm = None
            for hc in range(32):
                mm = p.do("tensor", lambda E, rc=rc, hc=hc: E.matmul(
                    psQ[:], u2b[:, hc, 128 * rc:128 * (rc + 1)], h_sb[:, hc, :],
                    start=(hc == 0), stop=(hc == 31)),
                    after=t_hev + [L.get("f_psQ"), c_u2])
            t_q = p.do("vector", lambda E, rc=rc: E.tensor_scalar_add(
                q_fm[:, rc, :], psQ[:], 0.0), after=[mm, L.get("f_qfm")])
            L["f_psQ"] = t_q
            t_qev.append(t_q)
        L["f_h"] = mm

        t_out = []
        last_mm = None
        for s_i in range(NST):
            t_ep = []
            for half in range(2):
                pO = psO[half]
                mm = None
                for rc in range(2):
                    mm = p.do("tensor", lambda E, s_i=s_i, half=half, rc=rc, pO=pO: E.matmul(
                        pO[:], q_fm[:, rc, 128 * s_i:128 * (s_i + 1)],
                        v2b[:, rc, 512 * half:512 * (half + 1)],
                        start=(rc == 0), stop=(rc == 1)),
                        after=t_qev + [L.get("f_psO%d" % half), c_v2])
                last_mm = mm
                t_e = p.do("vector", lambda E, s_i=s_i, half=half, pO=pO, osb=osb: E.tensor_add(
                    osb[:, s_i, 512 * half:512 * (half + 1)], pO[:],
                    b2rep[:, 512 * half:512 * (half + 1)]),
                    after=[mm, tb_e0, tb_e1, L.get("f_out")])
                L["f_psO%d" % half] = t_e
                t_ep.append(t_e)
            rs3, nb3, tst3 = ln_stats(p, osb[:, s_i, :], sq_scr[:], sta,
                                      12 * s_i + 6, 1e-5, tok_in=t_ep)
            t_ln = p.do("vector", lambda E, s_i=s_i, rs3=rs3, nb3=nb3, osb=osb: E.tensor_scalar(
                osb[:, s_i, :], osb[:, s_i, :], rs3, nb3, ALU.mult, ALU.add),
                hard=[tst3])
            t_fin = p.do("vector", lambda E, s_i=s_i, osb=osb, xsb=xsb: E.tensor_add(
                osb[:, s_i, :], osb[:, s_i, :], xsb[:, s_i, :]))
            t_out.append(t_fin)
        L["f_qfm"] = last_mm
        L["f_x%d" % bi] = t_out[-1]
        t_str = p.dma("gpsimd", xo_t[k], osb[:], stream=bi, after=t_out)
        L["f_out"] = t_str
        tok_store.append(t_str)

    for k in range(NSS):
        emit_ss(k)
    return tok_store





def ternarize(p, com, w_r, n_chunks, ncols, stage, wtern_view, mwcol, psrep, pref,
              gate0=None, gate1=None, wtern_war=None):
    L = p.last
    acc = com.rowacc
    t0 = p.do("vector", lambda E: E.memset(acc[:, 0:1], 0.0))
    tc = [gate0, gate1]
    for c in range(n_chunks):
        sl = c % 2
        t_w = p.dma("sync", stage[:, sl, 0:ncols], w_r[c], stream=sl, after=[tc[sl]])
        p.do("vector", lambda E, sl=sl: E.reduce_sum(
            acc[:, 1:2], stage[:, sl, 0:ncols], axis=AX.X, apply_absolute_value=True),
            after=[t_w], hard=[t0])
        tc[sl] = p.do("vector", lambda E: E.tensor_add(acc[:, 0:1], acc[:, 0:1], acc[:, 1:2]))
    pskey = "war_" + psrep.name
    t_mm = p.do("tensor", lambda E: E.matmul(
        psrep[0:1, 0:1], com.onescol_f[:, 0:1], acc[:, 0:1], start=True, stop=True),
        after=[tc[0], tc[1], com.tok_ones, L.get(pskey)])
    row = com.rowscratch
    p.do("vector", lambda E, n=n_chunks * 128 * ncols: E.tensor_scalar(
        row[0:1, 0:1], psrep[0:1, 0:1], 1.0 / n, 1e-5, ALU.mult, ALU.max), after=[t_mm])
    p.do("vector", lambda E: E.tensor_scalar_mul(row[0:1, 1:2], row[0:1, 0:1], -1.0))
    p.do("vector", lambda E: E.tensor_scalar_mul(row[0:1, 2:3], row[0:1, 0:1], 0.5))
    t2 = p.do("vector", lambda E: E.tensor_scalar_mul(row[0:1, 3:4], row[0:1, 0:1], -0.5))
    t_rep = p.do("tensor", lambda E: E.matmul(
        psrep[:, 0:3], com.onesrow_f[0:1, 0:128], row[0:1, 1:4], start=True, stop=True),
        after=[t2])
    t_mw = p.do("vector", lambda E: E.tensor_copy(mwcol[:], psrep[:, 0:3]), after=[t_rep])
    L[pskey] = t_mw
    tern_done = []
    tc = [t_mw, t_mw]
    for c in range(n_chunks):
        sl = c % 2
        t_w = p.dma("sync", stage[:, sl, 0:ncols], w_r[c], stream=sl, after=[tc[sl]])
        t_a = p.do("vector", lambda E, sl=sl: E.tensor_scalar(
            com.rndA[:, 0:ncols], stage[:, sl, 0:ncols], mwcol[:, 1:2], None, ALU.is_gt),
            after=[t_w, t_mw])
        tc[sl] = p.do("vector", lambda E, c=c, sl=sl: E.scalar_tensor_tensor(
            wtern_view[:, c, 0:ncols], stage[:, sl, 0:ncols], mwcol[:, 2:3],
            com.rndA[:, 0:ncols], ALU.is_lt, ALU.subtract),
            after=[wtern_war if c == 0 else None])
        tern_done.append(tc[sl])
    return tern_done


def build_l0_ffn(p, stk, com, dram, x_in, x_out, h2_dram, tok_x_in=None, dbg=None):
    nc = p.nc
    L = p.last

    wtern = sb(stk, nc, "c_wtern", [128, 32768], BF16)
    w1t = wtern[:].rearrange("p (c f) -> p c f", f=4096)
    w2t = wtern[:].rearrange("p (c f) -> p c f", f=1024)
    mw1c = sb(stk, nc, "c_mw1", [128, 3], F32)
    mw2c = sb(stk, nc, "c_mw2", [128, 3], F32)
    cwc = sb(stk, nc, "c_cw", [128, 3, 32], F32)
    cbc = sb(stk, nc, "c_cb", [128, 32], F32)

    psT = [ps(stk, nc, "c_psT%d" % i, [128, 128], BF16) for i in range(2)]
    psH = [ps(stk, nc, "c_psH%d" % i, [128, 512], F32) for i in range(2)]
    psO = [ps(stk, nc, "c_psO%d" % i, [128, 512], F32) for i in range(4)]

    t_cw = p.dma("sync", cwc[:], dram["conv_w"][:], sig="dw")
    t_cb = p.dma("sync", cbc[:], dram["conv_b"][:], sig="dw")
    dwtot = ("dw", p.cnt["dw"])

    w1_r = dram["w1_t"].rearrange("(c p) f -> c p f", p=128)
    w2_r = dram["w2_t"].rearrange("(c p) f -> c p f", p=128)

    x_sb = [sb(stk, nc, "c_x%d" % i, [128, NST, 1024], F32) for i in range(2)]
    sq_scr = sb(stk, nc, "c_sq", [128, 1024], F32)
    stat = [sb(stk, nc, "c_stat%d" % i, [128, 48], F32) for i in range(2)]

    stk1 = stk.enter_context(contextlib.ExitStack())
    stage = sb(stk1, nc, "c_stage", [128, 2, 1024], F32)
    xn_f = sb(stk1, nc, "c_xn", [128, 1024], F32)
    xq_bf = sb(stk1, nc, "c_xq", [128, NST, 1024], BF16)
    xqT1 = sb(stk1, nc, "c_xqT", [128, 8, 512], BF16)
    cbuf = sb(stk1, nc, "c_cbuf", [128, 2, 520], F32)
    halo = sb(stk1, nc, "c_halo", [128, 32, 2], F32)
    h2s = sb(stk1, nc, "c_h2st", [128, 32, 520], BF16)
    cvscr = sb(stk1, nc, "c_cvscr", [128, 2, 520], F32)

    w1_r8 = dram["w1_t"].rearrange("(c p) f -> c p f", p=128)
    w1_chunks = [w1_r8[c][:, 1024 * h:1024 * (h + 1)] for c in range(8) for h in range(4)]
    tern1 = ternarize(p, com, w1_chunks, 32, 1024, stage, w2t, mw1c, psO[0], "c_w1",
                      gate0=L.get("gate_dl0"), gate1=L.get("gate_dl1"))

    x_t = x_in.rearrange("(k s p) d -> k p s d", p=128, s=NST)
    xo_t = x_out.rearrange("(k s p) d -> k p s d", p=128, s=NST)

    t_load = [None] * NSS
    t_load[0] = p.dma("sync", x_sb[0][:], x_t[0], stream=0, after=[tok_x_in, tern1[-2]])
    h2win = [None] * NSS

    def emit_sweep1(k):
        bi = k % 2
        b_i, w_i = k // SS_PER_SEQ, k % SS_PER_SEQ
        xsb, sta = x_sb[bi], stat[bi]
        if k + 1 < NSS:
            gate = L.get("c_x%d" % ((k + 1) % 2)) if k >= 1 else tern1[-1]
            t_load[k + 1] = p.dma("sync", x_sb[(k + 1) % 2][:], x_t[k + 1],
                                  stream=(k + 1) % 2, after=[gate, tok_x_in])
        t_hz = None
        if w_i == 0:
            t_hz = p.do("vector", lambda E: E.memset(halo[:], 0.0),
                        after=[L.get("c_halo_rd")])
        tk_xqT = []
        t_tr = None
        t_q = None
        for s_i in range(NST):
            xs = xsb[:, s_i, :]
            rs, nb, tst = ln_stats(p, xs, sq_scr[:], sta, 12 * s_i, 1e-5,
                                   tok_in=[t_load[k], L.get("c_stat%d_%d" % (bi, s_i))])
            am, iv = sta[:, 12 * s_i + 4:12 * s_i + 5], sta[:, 12 * s_i + 5:12 * s_i + 6]
            t_xn = p.do("scalar", lambda E, xs=xs, rs=rs, nb=nb: E.activation(
                xn_f[:], xs, AF.Identity, bias=nb, scale=rs),
                after=[t_load[k], L.get("c_xnw")], hard=[tst])
            t_am = p.do("vector", lambda E, am=am: E.reduce_max(
                am, xn_f[:], axis=AX.X, apply_absolute_value=True), after=[t_xn])
            p.do("vector", lambda E, am=am: E.tensor_scalar_max(am, am, 1e-5))
            t_iv = p.do("vector", lambda E, am=am, iv=iv: E.reciprocal(iv, am))
            t_iv = p.do("vector", lambda E, iv=iv: E.tensor_scalar_mul(iv, iv, 127.0))
            am127 = sta[:, 12 * s_i + 6:12 * s_i + 7]
            t_a7 = p.do("vector", lambda E, am=am, am127=am127: E.tensor_scalar_mul(
                am127, am, 1.0 / 127.0))
            t_sc = p.do("scalar", lambda E, iv=iv: E.activation(
                sq_scr[:], xn_f[:], AF.Identity, scale=iv),
                after=[L.get("c_sqw")], hard=[t_iv])
            L["c_xnw"] = t_sc
            p.do("vector", lambda E: E.tensor_copy(com.i32scr[:, 0:1024], sq_scr[:]),
                 after=[t_sc, L.get("com_i32")])
            p.do("vector", lambda E: E.tensor_sub(com.rndB[:], sq_scr[:], com.i32scr[:, 0:1024]))
            p.do("vector", lambda E: E.scalar_tensor_tensor(
                com.rndA[:], com.rndB[:], 0.5, com.i32scr[:, 0:1024], ALU.is_gt, ALU.add))
            t_r = p.do("vector", lambda E: E.scalar_tensor_tensor(
                sq_scr[:], com.rndB[:], -0.5, com.rndA[:], ALU.is_ge, ALU.add))
            L["c_sqw"] = t_r
            t_q = p.do("vector", lambda E, s_i=s_i, am127=am127: E.tensor_scalar(
                xq_bf[:, s_i, :], sq_scr[:], -1.0, am127, ALU.add, ALU.mult),
                after=[L.get("c_xqw")], hard=[t_a7])
            L["com_i32"] = t_q
            L["c_stat%d_%d" % (bi, s_i)] = t_q
            for c in range(8):
                pt = psT[c % 2]
                t_tr = p.do("tensor", lambda E, c=c, s_i=s_i, pt=pt: E.transpose(
                    pt[:], xq_bf[:, s_i, 128 * c:128 * (c + 1)], com.ident[:]),
                    after=[t_q, com.tok_ident, L.get("c_psT%d" % (c % 2))])
                t_ev = p.do("vector", lambda E, c=c, s_i=s_i, pt=pt: E.tensor_copy(
                    xqT1[:, c, 128 * s_i:128 * (s_i + 1)], pt[:]),
                    after=[t_tr, L.get("c_xqT")])
                L["c_psT%d" % (c % 2)] = t_ev
                tk_xqT.append(t_ev)
        L["c_xqw"] = t_tr
        L["c_x%d" % bi] = t_q
        if dbg is not None and k == 0:
            p.dma("sync", dbg["xq"][:], xq_bf[:], sig="dbg", after=[t_tr])
            p.dma("sync", dbg["wt"][:], wtern[:, 0:8192], sig="dbg", after=[t_tr])

        first = (w_i == 0)
        last = (w_i == SS_PER_SEQ - 1)
        ncv = 513 if last else 512
        t_h2 = []
        t_hs = None
        for hc in range(32):
            ph = psH[hc % 2]
            mm = None
            for c in range(8):
                mm = p.do("tensor", lambda E, hc=hc, c=c, ph=ph: E.matmul(
                    ph[:], w1t[:, c, 128 * hc:128 * (hc + 1)], xqT1[:, c, :],
                    start=(c == 0), stop=(c == 7)),
                    after=tk_xqT + [L.get("c_psH%d" % (hc % 2))] +
                          ([tern1[-1]] if k == 0 else []))
            L["c_mm1last"] = mm
            t_ev = p.do("vector", lambda E, hc=hc, ph=ph: E.tensor_copy(
                cbuf[:, hc % 2, 2:514], ph[:]), after=[mm, L.get("c_cbuf%d" % (hc % 2))])
            t_hc = p.do("vector", lambda E, hc=hc: E.tensor_copy(
                cbuf[:, hc % 2, 0:2], halo[:, hc, :]), after=[t_hz])
            t_hs = p.do("vector", lambda E, hc=hc: E.tensor_copy(
                halo[:, hc, :], cbuf[:, hc % 2, 512:514]))
            if last:
                p.do("vector", lambda E, hc=hc: E.memset(cbuf[:, hc % 2, 514:515], 0.0))
            cv = cvscr[:, hc % 2, :]
            t_c1 = p.do("vector", lambda E, hc=hc, ncv=ncv, cv=cv: E.tensor_scalar(
                cv[:, 0:ncv] if False else cv[0:128, 0:ncv], cbuf[:, hc % 2, 0:ncv],
                cwc[:, 0, hc:hc + 1], None, ALU.mult),
                after=[L.get("com_cvscr%d" % (hc % 2))], hard=[dwtot])
            p.do("vector", lambda E, hc=hc, ncv=ncv, cv=cv: E.scalar_tensor_tensor(
                cv[0:128, 0:ncv], cbuf[:, hc % 2, 1:1 + ncv], cwc[:, 1, hc:hc + 1],
                cv[0:128, 0:ncv], ALU.mult, ALU.add))
            t_c3 = p.do("vector", lambda E, hc=hc, ncv=ncv, cv=cv: E.scalar_tensor_tensor(
                cv[0:128, 0:ncv], cbuf[:, hc % 2, 2:2 + ncv], cwc[:, 2, hc:hc + 1],
                cv[0:128, 0:ncv], ALU.mult, ALU.add))
            if dbg is not None and k == 0 and hc == 0:
                t_d = p.dma("sync", dbg["cbuf"][:], cbuf[:, 0, 0:514], sig="dbg", after=[t_c3])
                L["c_cbuf0"] = t_d
            L["c_cbuf%d" % (hc % 2)] = t_c3 if not (dbg is not None and k == 0 and hc == 0) else t_d
            t_si = p.do("scalar", lambda E, hc=hc, ncv=ncv, cv=cv: E.activation(
                h2s[:, hc, 0:ncv], cv[0:128, 0:ncv], AF.Silu,
                bias=cbc[:, hc:hc + 1], scale=mw1c[:, 0:1]),
                after=[t_c3, L.get("c_h2st")], hard=[dwtot, tern1[0]])
            L["com_cvscr%d" % (hc % 2)] = t_si
            L["c_psH%d" % (hc % 2)] = t_ev
            t_h2.append(t_si)
        L["c_halo_rd"] = t_hc
        L["c_halo"] = t_hs
        if first:
            off, lo, ln_ = 2048 * b_i, 1, 511
        elif last:
            off, lo, ln_ = 2048 * b_i + 512 * w_i - 1, 0, 513
        else:
            off, lo, ln_ = 2048 * b_i + 512 * w_i - 1, 0, 512
        t_st = []
        for g in range(4):
            t_st.append(p.dma(
                "gpsimd",
                h2_dram[8 * g:8 * (g + 1), :, off:off + ln_].rearrange("c p t -> p c t"),
                h2s[:, 8 * g:8 * (g + 1), lo:lo + ln_], stream=bi,
                after=t_h2 + [L.get("c_h2w%d" % bi)]))
        h2win[k] = t_st[-1]
        L["c_h2w%d" % bi] = t_st[-1]
        L["c_h2st"] = t_st[-1]

    for k in range(NSS):
        emit_sweep1(k)

    w2_chunks = [w2_r[c] for c in range(32)]
    tern2 = ternarize(p, com, w2_chunks, 32, 1024, stage, w2t, mw2c, psO[1], "c_w2",
                      gate0=L.get("c_x0"), gate1=L.get("c_x1"),
                      wtern_war=L.get("c_mm1last"))
    stk1.close()
    p.barrier()

    amrow = sb(stk, nc, "c_amrow", [1, 512], BF16)
    s2row = sb(stk, nc, "c_s2row", [1, 512], F32)
    s2rep = sb(stk, nc, "c_s2rep", [128, 512], F32)
    h2l1 = sb(stk, nc, "c_h2l", [128, 32, 512], BF16)
    h2l = [h2l1, h2l1]
    xq2 = sb(stk, nc, "c_xq2", [128, 32, 512], BF16)
    mx = sb(stk, nc, "c_mx", [128, 512], BF16)
    mxc = sb(stk, nc, "c_mxc", [128, 4], BF16)
    out_sb1 = sb(stk, nc, "c_out", [128, NST, 1024], F32)
    out_sb = [out_sb1, out_sb1]
    oscol = sb(stk, nc, "c_oscol", [128, NST], F32)
    w2csrow = sb(stk, nc, "c_w2csrow", [1, 1024], BF16)
    w2csrep = sb(stk, nc, "c_w2csrep", [128, 1024], F32)
    onescol_b = sb(stk, nc, "c_onescol_b", [128, 1], BF16)
    t_ocb = p.do("vector", lambda E: E.memset(onescol_b[:], 1.0))
    t_cs = {}
    for half in range(2):
        mmcs = None
        for hc in range(32):
            mmcs = p.do("tensor", lambda E, half=half, hc=hc: E.matmul(
                psH[half][0:1, 0:512], onescol_b[:, 0:1], w2t[:, hc, 512 * half:512 * (half + 1)],
                start=(hc == 0), stop=(hc == 31)),
                after=[t_ocb, tern2[-1], tern2[-2]], hard=[t_ocb])
        t_cs[half] = p.do("vector", lambda E, half=half: E.tensor_copy(
            w2csrow[0:1, 512 * half:512 * (half + 1)], psH[half][0:1, 0:512]), after=[mmcs])
    t_rp1 = p.do("tensor", lambda E: E.matmul(
        psH[0][:], com.onesrow_b[0:1, 0:128], w2csrow[0:1, 0:512], start=True, stop=True),
        after=[t_cs[0], t_cs[1], com.tok_ones])
    t_rp2 = p.do("tensor", lambda E: E.matmul(
        psH[1][:], com.onesrow_b[0:1, 0:128], w2csrow[0:1, 512:1024], start=True, stop=True),
        after=[t_cs[0], t_cs[1], com.tok_ones])
    t_cr1 = p.do("vector", lambda E: E.tensor_copy(w2csrep[:, 0:512], psH[0][:]), after=[t_rp1])
    t_cr2 = p.do("vector", lambda E: E.tensor_copy(w2csrep[:, 512:1024], psH[1][:]), after=[t_rp2])
    L["c_psH0"] = t_cr1
    L["c_psH1"] = t_cr2

    def h2gate(k):
        toks = [h2win[k]]
        if k % SS_PER_SEQ != SS_PER_SEQ - 1:
            toks.append(h2win[k + 1])
        return toks

    h2l_t = [None] * NSS
    h2l_t[0] = p.dma("sync", h2l[0][:],
                     h2_dram[:, :, 0:512].rearrange("c p t -> p c t"), stream=0,
                     after=[tern2[-2]] + h2gate(0))
    x2_t = [None] * NSS
    x2_t[0] = p.dma("sync", x_sb[0][:], x_t[0], sig="dx0", after=[tern2[-1]])

    def emit_sweep2(k):
        bi = k % 2
        xsb, sta, osb, h2b = x_sb[bi], stat[bi], out_sb[bi], h2l[bi]
        t_mx = p.do("vector", lambda E: E.scalar_tensor_tensor(
            mx[:], h2b[:, 0, :], -1.0, h2b[:, 0, :], ALU.mult, ALU.max),
            after=[h2l_t[k], L.get("c_mx")])
        for hc in range(1, 32):
            p.do("vector", lambda E, hc=hc: E.tensor_max(
                mx[:], mx[:], h2b[:, hc, :]))
            t_mx = p.do("vector", lambda E, hc=hc: E.scalar_tensor_tensor(
                mx[:], h2b[:, hc, :], -1.0, mx[:], ALU.mult, ALU.max))
        t_am = []
        for s_i in range(NST):
            pt = psT[s_i % 2]
            t_tr = p.do("tensor", lambda E, s_i=s_i, pt=pt: E.transpose(
                pt[:], mx[:, 128 * s_i:128 * (s_i + 1)], com.ident[:]),
                after=[t_mx, com.tok_ident, L.get("c_psT%d" % (s_i % 2))])
            t_rm = p.do("vector", lambda E, s_i=s_i, pt=pt: E.reduce_max(
                oscol[:, s_i:s_i + 1], pt[:], axis=AX.X),
                after=[t_tr, L.get("c_oscol")])
            L["c_psT%d" % (s_i % 2)] = t_rm
            t_am.append(t_rm)
        t_rowp = []
        for s_i in range(NST):
            pt = psT[s_i % 2]
            t_cc = p.do("vector", lambda E, s_i=s_i: E.tensor_scalar_add(
                mxc[:, s_i:s_i + 1], oscol[:, s_i:s_i + 1], 0.0), after=[t_am[s_i]])
            t_tr = p.do("tensor", lambda E, s_i=s_i, pt=pt: E.transpose(
                pt[0:1, 0:128], mxc[:, s_i:s_i + 1], com.ident[:]),
                after=[t_cc, L.get("c_psT%d" % (s_i % 2))])
            t_ev = p.do("vector", lambda E, s_i=s_i, pt=pt: E.tensor_copy(
                amrow[0:1, 128 * s_i:128 * (s_i + 1)], pt[0:1, 0:128]),
                after=[t_tr])
            L["c_psT%d" % (s_i % 2)] = t_ev
            t_rowp.append(t_ev)
        L["c_mx"] = t_rowp[-1]
        t_s1 = p.do("vector", lambda E: E.tensor_scalar_max(s2row[:], amrow[:], 1e-5),
                    after=t_rowp)
        p.do("vector", lambda E: E.reciprocal(s2row[:], s2row[:]))
        t_s3 = p.do("vector", lambda E: E.tensor_scalar_mul(s2row[:], s2row[:], 127.0))
        t_rp = p.do("tensor", lambda E: E.matmul(
            psH[0][:], com.onesrow_f[0:1, 0:128], s2row[0:1, :], start=True, stop=True),
            after=[t_s3, com.tok_ones, L.get("c_psH0")])
        t_re = p.do("vector", lambda E: E.tensor_copy(s2rep[:], psH[0][:]), after=[t_rp])
        L["c_psH0"] = t_re
        t_xq2 = []
        for hc in range(32):
            p.do("vector", lambda E, hc=hc: E.tensor_tensor(
                com.rndA[:, 0:512], h2b[:, hc, :], s2rep[:], ALU.mult),
                after=[t_re, L.get("com_i32")])
            p.do("vector", lambda E: E.tensor_copy(com.i32scr[:, 0:512], com.rndA[:, 0:512]))
            p.do("vector", lambda E: E.tensor_sub(
                com.rndB[:, 0:512], com.rndA[:, 0:512], com.i32scr[:, 0:512]))
            p.do("vector", lambda E: E.scalar_tensor_tensor(
                com.rndA[:, 0:512], com.rndB[:, 0:512], 0.5, com.i32scr[:, 0:512],
                ALU.is_gt, ALU.add))
            t_b = p.do("vector", lambda E, hc=hc: E.scalar_tensor_tensor(
                xq2[:, hc, :], com.rndB[:, 0:512], -0.5, com.rndA[:, 0:512],
                ALU.is_ge, ALU.add), after=[L.get("c_xq2")])
            L["com_i32"] = t_b
            t_xq2.append(t_b)
        L["c2_h2l"] = t_xq2[-1]
        if k + 1 < NSS:
            nb_ = (k + 1) % 2
            h2l_t[k + 1] = p.dma("sync", h2l[nb_][:],
                                 h2_dram[:, :, 512 * (k + 1):512 * (k + 2)].rearrange("c p t -> p c t"),
                                 stream=nb_, after=[t_xq2[-1]] + h2gate(k + 1))
            gate2 = L.get("c2_x%d" % nb_) if k >= 1 else None
            x2_t[k + 1] = p.dma("sync", x_sb[nb_][:], x_t[k + 1],
                                sig="dx%d" % nb_, after=[gate2])
        t_os = []
        for s_i in range(NST):
            p.do("vector", lambda E, s_i=s_i: E.tensor_scalar_max(
                oscol[:, s_i:s_i + 1], oscol[:, s_i:s_i + 1], 1e-5), after=[t_am[s_i]])
            t_o2 = p.do("vector", lambda E, s_i=s_i: E.scalar_tensor_tensor(
                oscol[:, s_i:s_i + 1], oscol[:, s_i:s_i + 1], 1.0 / 127.0,
                mw2c[:, 0:1], ALU.mult, ALU.mult), hard=[tern2[0]])
            t_os.append(t_o2)
        t_out = []
        last_mm = None
        t_elast = None
        for s_i in range(NST):
            t_ep = []
            for half in range(2):
                pok = 2 * (s_i % 2) + half
                pO = psO[pok]
                mm = None
                for hc in range(32):
                    mm = p.do("tensor", lambda E, s_i=s_i, half=half, hc=hc, pO=pO: E.matmul(
                        pO[:], xq2[:, hc, 128 * s_i:128 * (s_i + 1)],
                        w2t[:, hc, 512 * half:512 * (half + 1)],
                        start=(hc == 0), stop=(hc == 31)),
                        after=[t_xq2[-1], L.get("c_psO%d" % pok)] +
                              ([tern2[-1]] if k == 0 else []))
                last_mm = mm
                epib = com.rndA if half == 0 else com.rndB
                t_sub = p.do("vector", lambda E, half=half, pO=pO, epib=epib: E.tensor_sub(
                    epib[:, 512:1024], pO[:], w2csrep[:, 512 * half:512 * (half + 1)]),
                    after=[mm, t_cr1, t_cr2, L.get("c_epi%d" % half)])
                t_e = p.do("scalar", lambda E, s_i=s_i, half=half, osb=osb, epib=epib: E.activation(
                    osb[:, s_i, 512 * half:512 * (half + 1)], epib[:, 512:1024], AF.Identity,
                    scale=oscol[:, s_i:s_i + 1]),
                    after=[t_sub, L.get("c_out")], hard=[t_os[s_i]])
                L["c_epi%d" % half] = t_e
                L["c_psO%d" % pok] = t_sub
                t_elast = t_e
                t_ep.append(t_e)
            rs3, nb3, tst3 = ln_stats(p, osb[:, s_i, :], sq_scr[:], sta,
                                      12 * s_i + 6, 1e-5, tok_in=t_ep)
            t_ln = p.do("vector", lambda E, s_i=s_i, rs3=rs3, nb3=nb3, osb=osb: E.tensor_scalar(
                osb[:, s_i, :], osb[:, s_i, :], rs3, nb3, ALU.mult, ALU.add),
                hard=[tst3])
            t_fin = p.do("vector", lambda E, s_i=s_i, osb=osb, xsb=xsb: E.tensor_add(
                osb[:, s_i, :], osb[:, s_i, :], xsb[:, s_i, :]), after=[x2_t[k]])
            t_out.append(t_fin)
        L["c_oscol"] = t_elast
        L["c_xq2"] = last_mm
        L["c2_x%d" % bi] = t_out[-1]
        t_str = p.dma("gpsimd", xo_t[k], osb[:], stream=bi, after=t_out)
        L["c_out"] = t_str
        return t_str

    tok_store = []
    for k in range(NSS):
        tok_store.append(emit_sweep2(k))
    return tok_store





def build_l1_attn(p, stk, com, dram, x_in, x_out, ctx_dram, tok_x_in=None):
    nc = p.nc
    L = p.last

    stage = sb(stk, nc, "b_stage", [128, 2, 1024], F32)
    terns = {}
    mws = {}
    for nm in ("q", "k", "v", "o"):
        terns[nm] = sb(stk, nc, "b_tern%s" % nm, [128, 8, 1024], BF16)
        mws[nm] = sb(stk, nc, "b_mw%s" % nm, [128, 3], F32)
    identf = sb(stk, nc, "b_identf", [128, 128], F32)
    psS = [ps(stk, nc, "b_psS%d" % i, [128, 1024], F32) for i in range(2)]
    psC = [ps(stk, nc, "b_psC%d" % i, [128, 128], F32) for i in range(2)]

    t_if = p.do("vector", lambda E: E.memset(identf[:], 1.0))
    tok_identf = p.do(
        "gpsimd",
        lambda E: E.affine_select(identf[:], identf[:], [[-1, 128]],
                                  ALU.is_equal, 0.0, base=0, channel_multiplier=1),
        after=[t_if])

    w_r = {nm: dram["w%s_t" % nm].rearrange("(c p) e -> c p e", p=128)
           for nm in ("q", "k", "v", "o")}
    g0, g1 = L.get("gate_dl0"), L.get("gate_dl1")
    tern_t = {}
    for nm in ("q", "k", "v", "o"):
        tern_t[nm] = ternarize(p, com, [w_r[nm][c] for c in range(8)], 8, 1024,
                               stage, terns[nm], mws[nm], psS[0], "b_" + nm,
                               gate0=g0, gate1=g1, wtern_war=None)
        g0, g1 = tern_t[nm][-2], tern_t[nm][-1]

    x_sb = [sb(stk, nc, "b_x%d" % i, [128, NST, 1024], F32) for i in range(2)]
    xn_f = sb(stk, nc, "b_xn", [128, 1024], F32)
    xqT = sb(stk, nc, "b_xqT", [128, 8, 512], BF16)
    qtok = sb(stk, nc, "b_qtok", [128, 16, 64], BF16)
    ktok = sb(stk, nc, "b_ktok", [128, 16, 64], BF16)
    vtok = sb(stk, nc, "b_vtok", [128, 16, 64], BF16)
    prodA = sb(stk, nc, "b_prodA", [128, 2, 2, 16, 64], BF16)
    prodB = sb(stk, nc, "b_prodB", [128, 2, 2, 64, 16], BF16)
    scb = sb(stk, nc, "b_sc", [128, 16, 16], F32)
    attnb = sb(stk, nc, "b_attn", [128, 16, 16], BF16)
    smx = sb(stk, nc, "b_smx", [128, 16], F32)
    ctxs = sb(stk, nc, "b_ctxs", [128, 2, 1024], F32)
    sq_scr = sb(stk, nc, "b_sq", [128, 1024], F32)
    stat = [sb(stk, nc, "b_stat%d" % i, [128, 48], F32) for i in range(2)]

    x_t = x_in.rearrange("(k s p) d -> k p s d", p=128, s=NST)
    xo_t = x_out.rearrange("(k s p) d -> k p s d", p=128, s=NST)

    tok_store = []
    t_load = [None] * NSS
    t_load[0] = p.dma("sync", x_sb[0][:], x_t[0], stream=0,
                      after=[tok_x_in, tern_t["o"][-2]])
    ctxw = [None] * (NSS * NST)

    def emit_sweep1(k):
        bi = k % 2
        xsb, sta = x_sb[bi], stat[bi]
        if k + 1 < NSS:
            gate = L.get("b_x%d" % ((k + 1) % 2)) if k >= 1 else tern_t["o"][-1]
            t_load[k + 1] = p.dma("sync", x_sb[(k + 1) % 2][:], x_t[k + 1],
                                  stream=(k + 1) % 2, after=[gate, tok_x_in])
        t_q = None
        for s_i in range(NST):
            xs = xsb[:, s_i, :]
            rs, nb, tst = ln_stats(p, xs, sq_scr[:], sta, 12 * s_i, 1e-5, expln=True,
                                   tok_in=[t_load[k], L.get("b_stat%d_%d" % (bi, s_i))])
            t_xn = p.do("scalar", lambda E, xs=xs, rs=rs, nb=nb: E.activation(
                xn_f[:], xs, AF.Identity, bias=nb, scale=rs),
                after=[t_load[k], L.get("b_xnw")], hard=[tst])
            rs2, trs2 = rms_stats(p, xn_f[:], sq_scr[:], sta, 12 * s_i + 4,
                                  1e-6, tok_in=[t_xn], expln=True)
            t_xr = p.do("scalar", lambda E, rs2=rs2: E.activation(
                sq_scr[:], xn_f[:], AF.Identity, scale=rs2),
                after=[L.get("b_sqw")], hard=[trs2])
            L["b_xnw"] = t_xr
            am = sta[:, 12 * s_i + 6:12 * s_i + 7]
            iv = sta[:, 12 * s_i + 7:12 * s_i + 8]
            am127 = sta[:, 12 * s_i + 8:12 * s_i + 9]
            p.do("vector", lambda E, am=am: E.reduce_max(
                am, sq_scr[:], axis=AX.X, apply_absolute_value=True), after=[t_xr])
            p.do("vector", lambda E, am=am: E.tensor_scalar_max(am, am, 1e-5))
            p.do("vector", lambda E, am=am, iv=iv: E.reciprocal(iv, am))
            p.do("vector", lambda E, iv=iv: E.tensor_scalar_mul(iv, iv, 127.0))
            t_a7 = p.do("vector", lambda E, am=am, am127=am127: E.tensor_scalar_mul(
                am127, am, 1.0 / 127.0))
            t_z = p.do("scalar", lambda E, iv=iv: E.activation(
                xn_f[:], sq_scr[:], AF.Identity, scale=iv),
                after=[L.get("b_xnw2")], hard=[t_a7])
            L["b_sqw"] = t_z
            p.do("vector", lambda E: E.tensor_copy(com.i32scr[:, 0:1024], xn_f[:]),
                 after=[t_z, L.get("com_i32")])
            p.do("vector", lambda E: E.tensor_sub(com.rndB[:], xn_f[:],
                                                  com.i32scr[:, 0:1024]))
            p.do("vector", lambda E: E.scalar_tensor_tensor(
                com.rndA[:], com.rndB[:], 0.5, com.i32scr[:, 0:1024],
                ALU.is_gt, ALU.add))
            t_r = p.do("vector", lambda E: E.scalar_tensor_tensor(
                com.rndB[:], com.rndB[:], -0.5, com.rndA[:], ALU.is_ge, ALU.add))
            L["com_i32"] = t_r
            t_q = p.do("vector", lambda E, am127=am127: E.tensor_scalar(
                xn_f[:], com.rndB[:], -1.0, am127, ALU.add, ALU.mult),
                after=[L.get("b_xnw")])
            L["b_xnw2"] = t_q
            L["b_stat%d_%d" % (bi, s_i)] = t_q
            tk_xqT = []
            t_tr = None
            for c in range(8):
                pt = psC[c % 2]
                t_tr = p.do("tensor", lambda E, c=c, pt=pt: E.transpose(
                    pt[:], xn_f[:, 128 * c:128 * (c + 1)], identf[:]),
                    after=[t_q, tok_identf, L.get("war_b_psC%d" % (c % 2))])
                t_ev = p.do("vector", lambda E, c=c, pt=pt: E.tensor_scalar_add(
                    xqT[:, c, 0:128], pt[:], 0.0), after=[t_tr, L.get("b_xqTw")])
                L["war_b_psC%d" % (c % 2)] = t_ev
                tk_xqT.append(t_ev)
            L["b_xnw"] = t_tr
            t_pr = {}
            for nm, dst in (("q", qtok), ("k", ktok), ("v", vtok)):
                for half in range(2):
                    pg = psS[half]
                    mm = None
                    for c in range(8):
                        mm = p.do("tensor", lambda E, nm=nm, half=half, c=c, pg=pg: E.matmul(
                            pg[:, 0:512], xqT[:, c, 0:128],
                            terns[nm][:, c, 512 * half:512 * (half + 1)],
                            start=(c == 0), stop=(c == 7)),
                            after=tk_xqT + [L.get("war_b_psS%d" % half)] +
                                  ([tern_t[nm][-1]] if k == 0 and s_i == 0 else []))
                    t_pr[(nm, half)] = p.do("scalar", lambda E, nm=nm, half=half, pg=pg, dst=dst: E.activation(
                        dst[:, 8 * half:8 * (half + 1), :].rearrange("p a b -> p (a b)"),
                        pg[:, 0:512], AF.Identity, scale=mws[nm][:, 0:1]),
                        after=[mm, L.get("b_%sw" % nm)], hard=[tern_t[nm][0]])
                    L["war_b_psS%d" % half] = t_pr[(nm, half)]
            L["b_xqTw"] = mm
            t_sc = []
            for hg in range(8):
                t_m = p.do("gpsimd", lambda E, hg=hg: E.tensor_tensor(
                    prodA[:, hg % 2, :, :, :],
                    qtok[:, 2 * hg:2 * (hg + 1), :].unsqueeze(2)
                        .broadcast_to([128, 2, 16, 64]),
                    ktok[:].unsqueeze(1).broadcast_to([128, 2, 16, 64]), ALU.mult),
                    after=[t_pr[("q", 0)], t_pr[("q", 1)], t_pr[("k", 0)],
                           t_pr[("k", 1)], L.get("b_prA%d" % (hg % 2))])
                t_s = p.do("vector", lambda E, hg=hg: E.reduce_sum(
                    scb[:, 2 * hg:2 * (hg + 1), :], prodA[:, hg % 2, :, :, :],
                    axis=AX.X), after=[t_m])
                L["b_prA%d" % (hg % 2)] = t_s
                t_sc.append(t_s)
            t_mx = p.do("vector", lambda E: E.reduce_max(
                smx[:], scb[:], axis=AX.X), after=t_sc)
            t_sb = p.do("vector", lambda E: E.tensor_sub(
                scb[:], scb[:], smx[:].unsqueeze(2).broadcast_to([128, 16, 16])))
            t_ex = p.do("scalar", lambda E: E.activation(
                scb[:].rearrange("p a b -> p (a b)"),
                scb[:].rearrange("p a b -> p (a b)"), AF.Exp, scale=0.125),
                after=[t_sb, L.get("b_scw")], hard=[t_sb])
            t_sm = p.do("vector", lambda E: E.reduce_sum(
                smx[:], scb[:], axis=AX.X), after=[t_ex])
            t_rc = p.do("vector", lambda E: E.reciprocal(smx[:], smx[:]))
            t_at = p.do("vector", lambda E: E.tensor_tensor(
                attnb[:], scb[:], smx[:].unsqueeze(2).broadcast_to([128, 16, 16]),
                ALU.mult), hard=[t_rc])
            L["b_scw"] = t_at
            vview = vtok[:].transpose([0, 2, 1])
            t_cx = []
            for hg in range(8):
                t_m = p.do("gpsimd", lambda E, hg=hg: E.tensor_tensor(
                    prodB[:, hg % 2, :, :, :],
                    attnb[:, 2 * hg:2 * (hg + 1), :].unsqueeze(2)
                        .broadcast_to([128, 2, 64, 16]),
                    vview.unsqueeze(1).broadcast_to([128, 2, 64, 16]), ALU.mult),
                    after=[t_at, t_pr[("v", 0)], t_pr[("v", 1)],
                           L.get("b_prB%d" % (hg % 2))])
                t_c = p.do("vector", lambda E, hg=hg, s_i=s_i: E.reduce_sum(
                    ctxs[:, s_i % 2, 128 * hg:128 * (hg + 1)].rearrange(
                        "p (a b) -> p a b", b=64),
                    prodB[:, hg % 2, :, :, :], axis=AX.X),
                    after=[t_m, L.get("b_ctxs%d" % (s_i % 2))])
                L["b_prB%d" % (hg % 2)] = t_c
                t_cx.append(t_c)
            g = NST * k + s_i
            t_cw = p.dma("gpsimd", ctx_dram[128 * g:128 * (g + 1), :],
                         ctxs[:, s_i % 2, :], sig="dc%d" % (s_i % 2),
                         after=t_cx + [L.get("b_cwg%d" % (s_i % 2))])
            ctxw[g] = t_cw
            L["b_cwg%d" % (s_i % 2)] = t_cw
            L["b_ctxs%d" % (s_i % 2)] = t_cw
        L["b_x%d" % bi] = t_q

    for k in range(NSS):
        emit_sweep1(k)

    ctxl = sb(stk, nc, "b_ctxl", [128, 1024], F32)
    ctxq_f = sb(stk, nc, "b_cq", [128, 1024], F32)
    ctxqT = sb(stk, nc, "b_cqT", [128, 8, 128], BF16)
    out_sb1 = sb(stk, nc, "b_out", [128, NST, 1024], F32)
    out_sb = [out_sb1, out_sb1]

    x2_t = [None] * NSS

    def emit_sweep2(k):
        bi = k % 2
        sta, osb = stat[bi], out_sb[bi]
        b_i = k // SS_PER_SEQ
        gate = L.get("b_x2f%d" % bi) if k >= 2 else L.get("b_x%d" % bi)
        x2_t[k] = p.dma("sync", x_sb[bi][:], x_t[k], stream=bi, after=[gate])
        t_out = []
        for s_i in range(NST):
            a = 4 * (k % SS_PER_SEQ) + s_i
            base = 2048 * b_i
            src = ctx_dram[base:base + 2048, 64 * a:64 * (a + 1)]
            src = src.rearrange("(r j) d -> r j d", r=128)
            t_cl = p.dma("sync", ctxl[:].rearrange("p (j d) -> p j d", d=64), src,
                         sig="dx0",
                         after=[L.get("b_ctxl"),
                                ("dc0", p.cnt["dc0"]), ("dc1", p.cnt["dc1"])])
            rs2, trs2 = rms_stats(p, ctxl[:], sq_scr[:], sta, 12 * s_i, 1e-6,
                                  tok_in=[t_cl], expln=True)
            t_xr = p.do("scalar", lambda E, rs2=rs2: E.activation(
                sq_scr[:], ctxl[:], AF.Identity, scale=rs2),
                after=[t_cl, L.get("b_sqw")], hard=[trs2])
            am = sta[:, 12 * s_i + 4:12 * s_i + 5]
            iv = sta[:, 12 * s_i + 5:12 * s_i + 6]
            osc = sta[:, 12 * s_i + 6:12 * s_i + 7]
            p.do("vector", lambda E, am=am: E.reduce_max(
                am, sq_scr[:], axis=AX.X, apply_absolute_value=True), after=[t_xr])
            p.do("vector", lambda E, am=am: E.tensor_scalar_max(am, am, 1e-5))
            p.do("vector", lambda E, am=am, iv=iv: E.reciprocal(iv, am))
            t_iv = p.do("vector", lambda E, iv=iv: E.tensor_scalar_mul(iv, iv, 127.0))
            t_os = p.do("vector", lambda E, am=am, osc=osc: E.scalar_tensor_tensor(
                osc, am, 1.0 / 127.0, mws["o"][:, 0:1], ALU.mult, ALU.mult),
                hard=[tern_t["o"][0]])
            t_z = p.do("scalar", lambda E, iv=iv: E.activation(
                ctxq_f[:], sq_scr[:], AF.Identity, scale=iv),
                after=[L.get("b_cqw")], hard=[t_iv])
            L["b_sqw"] = t_z
            L["b_ctxl"] = t_xr
            p.do("vector", lambda E: E.tensor_copy(com.i32scr[:, 0:1024], ctxq_f[:]),
                 after=[t_z, L.get("com_i32")])
            p.do("vector", lambda E: E.tensor_sub(com.rndB[:], ctxq_f[:],
                                                  com.i32scr[:, 0:1024]))
            p.do("vector", lambda E: E.scalar_tensor_tensor(
                com.rndA[:], com.rndB[:], 0.5, com.i32scr[:, 0:1024],
                ALU.is_gt, ALU.add))
            t_r = p.do("vector", lambda E: E.scalar_tensor_tensor(
                com.rndB[:], com.rndB[:], -0.5, com.rndA[:], ALU.is_ge, ALU.add))
            L["com_i32"] = t_r
            t_q = p.do("vector", lambda E: E.tensor_scalar_add(
                ctxq_f[:], com.rndB[:], -1.0), after=[L.get("b_cqw2")])
            L["b_cqw"] = t_q
            tk_cT = []
            for c in range(8):
                pt = psC[c % 2]
                t_tr = p.do("tensor", lambda E, c=c, pt=pt: E.transpose(
                    pt[:], ctxq_f[:, 128 * c:128 * (c + 1)], identf[:]),
                    after=[t_q, tok_identf, L.get("war_b_psC%d" % (c % 2))])
                t_ev = p.do("vector", lambda E, c=c, pt=pt: E.tensor_scalar_add(
                    ctxqT[:, c, :], pt[:], 0.0), after=[t_tr, L.get("b_cqTw")])
                L["war_b_psC%d" % (c % 2)] = t_ev
                tk_cT.append(t_ev)
            L["b_cqw2"] = t_tr
            t_ep = []
            for half in range(2):
                pg = psS[half]
                mm = None
                for c in range(8):
                    mm = p.do("tensor", lambda E, c=c, half=half, pg=pg: E.matmul(
                        pg[:, 0:512], ctxqT[:, c, :],
                        terns["o"][:, c, 512 * half:512 * (half + 1)],
                        start=(c == 0), stop=(c == 7)),
                        after=tk_cT + [L.get("war_b_psS%d" % half)])
                t_e = p.do("scalar", lambda E, s_i=s_i, half=half, pg=pg, osc=osc: E.activation(
                    osb[:, s_i, 512 * half:512 * (half + 1)], pg[:, 0:512],
                    AF.Identity, scale=osc),
                    after=[mm, L.get("b_outw")], hard=[t_os])
                L["war_b_psS%d" % half] = t_e
                L["b_cqTw"] = mm
                t_ep.append(t_e)
            rs3, nb3, tst3 = ln_stats(p, osb[:, s_i, :], sq_scr[:], sta,
                                      12 * s_i + 7, 1e-5, tok_in=t_ep, expln=True)
            t_ln = p.do("vector", lambda E, s_i=s_i, rs3=rs3, nb3=nb3, osb=osb: E.tensor_scalar(
                osb[:, s_i, :], osb[:, s_i, :], rs3, nb3, ALU.mult, ALU.add),
                hard=[tst3])
            t_fin = p.do("vector", lambda E, s_i=s_i, osb=osb, bi=bi: E.tensor_add(
                osb[:, s_i, :], osb[:, s_i, :], x_sb[bi][:, s_i, :]),
                after=[x2_t[k]])
            t_out.append(t_fin)
        L["b_x2f%d" % bi] = t_out[-1]
        t_str = p.dma("gpsimd", xo_t[k], osb[:], stream=bi,
                      after=t_out + [L.get("b_outw")])
        L["b_outw"] = t_str
        tok_store.append(t_str)

    for k in range(NSS):
        emit_sweep2(k)
    return tok_store



def _prep_inputs(inputs):
    f32 = np.float32

    def t(a):
        return np.ascontiguousarray(np.asarray(a, f32).T)

    def cols(a, nch):
        return np.ascontiguousarray(np.asarray(a, f32).reshape(nch, 128).T)

    g = {}
    g["x"] = np.ascontiguousarray(np.asarray(inputs["L0_attn_tokens"], f32).reshape(B, S, D))
    g["a_w_in_t"] = t(inputs["L0_attn_W_in"])
    g["a_w_out_t"] = t(inputs["L0_attn_W_out"])
    g["a_w_skip_t"] = t(inputs["L0_attn_W_skip"])
    g["a_norm_w"] = cols(inputs["L0_attn_norm_w"], 8)
    g["a_b_in"] = cols(inputs["L0_attn_b_in"], 2)
    g["a_b_out"] = np.ascontiguousarray(np.asarray(inputs["L0_attn_b_out"], f32)).reshape(1, 1024)
    g["a_b_skip"] = np.ascontiguousarray(np.asarray(inputs["L0_attn_b_skip"], f32)).reshape(1, 1024)
    g["a_log_decay"] = cols(inputs["L0_attn_log_decay"], 2)
    g["c_w1t"] = t(inputs["L0_ffn_w1"])
    g["c_w2t"] = t(inputs["L0_ffn_w2"])
    cw = np.asarray(inputs["L0_ffn_conv_w"], f32)[:, 0, :]
    g["c_cwd"] = np.ascontiguousarray(cw.reshape(32, 128, 3).transpose(1, 2, 0))
    g["c_cbd"] = cols(inputs["L0_ffn_conv_b"], 32)
    g["b_wqt"] = t(inputs["L1_attn_wq"])
    g["b_wkt"] = t(inputs["L1_attn_wk"])
    g["b_wvt"] = t(inputs["L1_attn_wv"])
    g["b_wot"] = t(inputs["L1_attn_wo"])
    g["f_u1d"] = np.ascontiguousarray(np.asarray(inputs["L1_ffn_U1"], f32))
    g["f_v1d"] = np.ascontiguousarray(np.asarray(inputs["L1_ffn_V1"], f32))
    g["f_u2d"] = np.ascontiguousarray(np.asarray(inputs["L1_ffn_U2"], f32))
    g["f_v2d"] = np.ascontiguousarray(np.asarray(inputs["L1_ffn_V2"], f32))
    g["f_b1d"] = cols(inputs["L1_ffn_b1"], 32)
    g["f_b2d"] = np.ascontiguousarray(np.asarray(inputs["L1_ffn_b2"], f32)).reshape(1, 1024)
    return g


def _l0_attn_dram(nc):
    d = {}
    d["w_in_t"] = nc.dram_tensor("a_w_in_t", [1024, 256], F32, kind="ExternalInput").ap()
    d["w_out_t"] = nc.dram_tensor("a_w_out_t", [256, 1024], F32, kind="ExternalInput").ap()
    d["w_skip_t"] = nc.dram_tensor("a_w_skip_t", [1024, 1024], F32, kind="ExternalInput").ap()
    d["norm_w"] = nc.dram_tensor("a_norm_w", [128, 8], F32, kind="ExternalInput").ap()
    d["b_in"] = nc.dram_tensor("a_b_in", [128, 2], F32, kind="ExternalInput").ap()
    d["b_out"] = nc.dram_tensor("a_b_out", [1, 1024], F32, kind="ExternalInput").ap()
    d["b_skip"] = nc.dram_tensor("a_b_skip", [1, 1024], F32, kind="ExternalInput").ap()
    d["log_decay"] = nc.dram_tensor("a_log_decay", [128, 2], F32, kind="ExternalInput").ap()
    return d


def _l0_ffn_dram(nc):
    d = {}
    d["w1_t"] = nc.dram_tensor("c_w1t", [1024, 4096], F32, kind="ExternalInput").ap()
    d["w2_t"] = nc.dram_tensor("c_w2t", [4096, 1024], F32, kind="ExternalInput").ap()
    d["conv_w"] = nc.dram_tensor("c_cwd", [128, 3, 32], F32, kind="ExternalInput").ap()
    d["conv_b"] = nc.dram_tensor("c_cbd", [128, 32], F32, kind="ExternalInput").ap()
    return d


def _l1_attn_dram(nc):
    d = {}
    for nm in ("wq", "wk", "wv", "wo"):
        d[nm + "_t"] = nc.dram_tensor("b_" + nm + "t", [1024, 1024], F32,
                                      kind="ExternalInput").ap()
    return d


def _l1_ffn_dram(nc):
    d = {}
    d["u1"] = nc.dram_tensor("f_u1d", [1024, 256], F32, kind="ExternalInput").ap()
    d["v1"] = nc.dram_tensor("f_v1d", [256, 4096], F32, kind="ExternalInput").ap()
    d["u2"] = nc.dram_tensor("f_u2d", [4096, 256], F32, kind="ExternalInput").ap()
    d["v2"] = nc.dram_tensor("f_v2d", [256, 1024], F32, kind="ExternalInput").ap()
    d["b1"] = nc.dram_tensor("f_b1d", [128, 32], F32, kind="ExternalInput").ap()
    d["b2"] = nc.dram_tensor("f_b2d", [1, 1024], F32, kind="ExternalInput").ap()
    return d


def _block_prog(which, debug=False):
    nc = bass.Bass("TRN2", target_bir_lowering=False)
    x_in = nc.dram_tensor("x", [TOK, 1024], F32, kind="ExternalInput").ap()
    x_out = nc.dram_tensor("y", [TOK, 1024], F32, kind="ExternalOutput").ap()
    p = Prog(nc)
    with contextlib.ExitStack() as stk:
        com = Common(p, stk)
        if which == "l0a":
            toks = build_l0_attn(p, stk, com, _l0_attn_dram(nc), x_in, x_out)
        elif which == "l0f":
            h2_dram = nc.dram_tensor("c_h2", [32, 128, TOK], BF16, kind="Internal").ap()
            toks = build_l0_ffn(p, stk, com, _l0_ffn_dram(nc), x_in, x_out, h2_dram)
        elif which == "l1a":
            ctx_dram = nc.dram_tensor("b_ctx", [TOK, 1024], F32, kind="Internal").ap()
            toks = build_l1_attn(p, stk, com, _l1_attn_dram(nc), x_in, x_out, ctx_dram)
        elif which == "l1f":
            toks = build_l1_ffn(p, stk, com, _l1_ffn_dram(nc), x_in, x_out)
        p.wait_all("sync", toks + [("dw", p.cnt["dw"])])
        p.build()
    return nc


def build_fused_prog():
    nc = bass.Bass("TRN2", target_bir_lowering=False)
    x_in = nc.dram_tensor("x", [TOK, 1024], F32, kind="ExternalInput").ap()
    y = nc.dram_tensor("y", [TOK, 1024], F32, kind="ExternalOutput").ap()
    x1 = nc.dram_tensor("x1", [TOK, 1024], F32, kind="Internal").ap()
    x2 = nc.dram_tensor("x2", [TOK, 1024], F32, kind="Internal").ap()
    x3 = nc.dram_tensor("x3", [TOK, 1024], F32, kind="Internal").ap()
    h2_dram = nc.dram_tensor("c_h2", [32, 128, TOK], BF16, kind="Internal").ap()
    ctx_dram = nc.dram_tensor("b_ctx", [TOK, 1024], F32, kind="Internal").ap()
    da, dc = _l0_attn_dram(nc), _l0_ffn_dram(nc)
    db, df = _l1_attn_dram(nc), _l1_ffn_dram(nc)
    p = Prog(nc)
    with contextlib.ExitStack() as stk:
        com = Common(p, stk)
        s0 = stk.enter_context(contextlib.ExitStack())
        build_l0_attn(p, s0, com, da, x_in, x1)
        s0.close()
        p.barrier()
        s1 = stk.enter_context(contextlib.ExitStack())
        build_l0_ffn(p, s1, com, dc, x1, x2, h2_dram)
        s1.close()
        p.barrier()
        s2 = stk.enter_context(contextlib.ExitStack())
        build_l1_attn(p, s2, com, db, x2, x3, ctx_dram)
        s2.close()
        p.barrier()
        s3 = stk.enter_context(contextlib.ExitStack())
        toks = build_l1_ffn(p, s3, com, df, x3, y)
        s3.close()
        p.wait_all("sync", toks + [("dw", p.cnt["dw"])])
        p.build()
    return nc


_KEYS = {
    "l0a": ["a_w_in_t", "a_w_out_t", "a_w_skip_t", "a_norm_w", "a_b_in",
            "a_b_out", "a_b_skip", "a_log_decay"],
    "l0f": ["c_w1t", "c_w2t", "c_cwd", "c_cbd"],
    "l1a": ["b_wqt", "b_wkt", "b_wvt", "b_wot"],
    "l1f": ["f_u1d", "f_v1d", "f_u2d", "f_v2d", "f_b1d", "f_b2d"],
}


def kernel(**inputs):
    g = _prep_inputs(inputs)
    xs = [np.ascontiguousarray(g["x"][BPC * ci:BPC * (ci + 1)].reshape(TOK, D))
          for ci in range(NCORE)]
    nc = build_fused_prog()
    in_maps = []
    for ci in range(NCORE):
        m = {k: g[k] for ks in _KEYS.values() for k in ks}
        m["x"] = xs[ci]
        in_maps.append(m)
    res = run_bass_kernel_spmd(nc, in_maps, core_ids=list(range(NCORE)))
    return np.concatenate([r["y"].reshape(BPC, S, D) for r in res.results],
                          axis=0).reshape(B, S, D)
